# version 01
# baseline (speedup 1.0000x reference)
import numpy as np
import concourse.bass as bass
import concourse.mybir as mybir
from concourse.bass_utils import run_bass_kernel_spmd
from contextlib import ExitStack

F32 = mybir.dt.float32
BF16 = mybir.dt.bfloat16
ALU = mybir.AluOpType
ACT = mybir.ActivationFunctionType
AX = mybir.AxisListType

T = 2048
M = 1024
NT = T // 128
NM = M // 128
NH = 16
DH = 64
CH = 256
NCH = T // CH
GEPS = 1e-6
LNEPS = 1e-5
LOG_SCALE = float(np.log(DH ** -0.5))


class Sems:
    def __init__(self, nc, stack, names):
        self.h = {n: stack.enter_context(nc.semaphore(n)) for n in names}
        self.c = {n: 0 for n in names}

    def inc(self, instr, name, k=1):
        self.c[name] += k
        instr.then_inc(self.h[name], k)
        return instr

    def wait(self, eng, name, val=None):
        eng.wait_ge(self.h[name], self.c[name] if val is None else val)


SKIP_GLA = False


def build(debug=False):
    nc = bass.Bass("TRN2", target_bir_lowering=False)
    stack = ExitStack()
    dram = {}

    def din(name, shape, dt=F32):
        dram[name] = nc.dram_tensor(name, shape, dt, kind="ExternalInput").ap()

    def dout(name, shape, dt=F32):
        dram[name] = nc.dram_tensor(name, shape, dt, kind="ExternalOutput").ap()

    din("x", [T, M])
    for w in ("Wq", "Wk", "Wv", "Wg", "Wo"):
        din(w, [M, M])
    din("Wgt", [M, NH])
    din("gate_w", [M, 8])
    din("w1", [8, M, 512])
    din("w2", [8, 512, M])
    din("w3", [8, M, 512])
    din("W_inT", [M, 4096])
    din("conv_w", [2048, 4])
    din("conv_b", [2048, 1])
    din("W_xT", [2048, 33])
    din("W_dt", [2048, 1])
    din("b_dt", [2048, 1])
    din("A_log", [2048, 16])
    din("D_param", [2048, 1])
    din("W_outT", [2048, M])
    din("gam_l", [M, 1])
    din("wq_l", [M, 512])
    din("wkv_l", [M, 1024])
    din("wo_l", [512, M])
    din("q_rt", [M, 1])
    din("q_rt_row", [1, M])
    din("gam_h", [M, 1])
    din("wq_h", [M, 512])
    din("wkv_h", [M, 1024])
    din("wo_h", [512, M])
    din("onescol_f", [128, 1])
    din("ident", [128, 128])
    din("ones", [1, 128])
    din("trilT", [CH, CH])
    din("selh", [NH, NH * 128])
    din("onescol", [128, 1])
    dout("out", [T, M])
    if debug:
        dout("dbg_xn", [T, M])
        for nm, shp in (("dbg_q", [128, 2048]), ("dbg_k", [128, 2048]), ("dbg_v", [128, 2048]),
                        ("dbg_bc", [NH, T]), ("dbg_bcq", [NH, CH]), ("dbg_bkneg", [NH, CH]),
                        ("dbg_mtab", [128, 160]), ("dbg_silug", [128, 2048]),
                        ("dbg_dmask", [128, 512]), ("dbg_msT", [128, 512]),
                        ("dbg_otok", [128, 2048]), ("dbg_ktm", [128, 2048]), ("dbg_vtm", [128, 2048]), ("dbg_xnfm", [128, NM * T]), ("dbg_stt", [128, 8 * NT]),
                        ("dbg_og", [128, 2048]), ("dbg_ofm", [128, 2048]), ("dbg_xres0", [128, 2048]),
                        ("dbg_m_xd", [128, 16 * 516]), ("dbg_m_u", [128, 16 * 512]),
                        ("dbg_m_ssmt", [33, 512]), ("dbg_m_bca", [128, 32 * 512]),
                        ("dbg_m_lns", [128, 512]), ("dbg_m_h", [128, 512]),
                        ("dbg_m_dA", [128, 512]), ("dbg_m_bP", [128, 512]), ("dbg_m_wdd", [128, 512]),
                        ("dbg_m_zs", [128, 16 * 512]), ("dbg_mask", [128, 16]), ("dbg_score", [128, 16]),
                        ("dbg_s2d", [128, 16]), ("dbg_tiny", [1, 8]), ("dbg_tokrep", [128, M]), ("dbg_xnt", [128, M])):
            dout(nm, shp)

    S = Sems(nc, stack, ["te", "act", "dve", "dma", "gdma", "gp"])

    def sb(name, shape, dt):
        return stack.enter_context(nc.sbuf_tensor(name, shape, dt))

    xres = sb("xres", [128, NT * M], F32)
    xn_fm = sb("xn_fm", [128, NM * T], BF16)
    identf = sb("identf", [128, 128], F32)
    identb = sb("identb", [128, 128], BF16)
    onesf = sb("onesf", [1, 128], F32)
    onesb = sb("onesb", [1, 128], BF16)
    onescol_b = sb("onescol_b", [128, 1], BF16)
    consts = sb("consts", [128, 2], F32)
    trilT_sb = sb("trilT_sb", [128, 2 * CH], BF16)
    s2d = sb("s2d_rt", [128, 16], F32)

    with nc.Block() as blk:
        def body(q):
            S.inc(q.dma_start(out=xres.rearrange("p (t m) -> p t m", t=NT), in_=dram["x"].rearrange("(t p) m -> p t m", p=128)), "dma", 16)
            S.inc(q.dma_start(out=identf[:, :], in_=dram["ident"][:, :]), "dma", 16)
            S.inc(q.dma_start(out=onesf[:, :], in_=dram["ones"][:, :]), "dma", 16)
        blk.sync(body)

        def body(v):
            S.inc(v.memset(consts[:, 0:1], GEPS), "dve")
            v.drain()
        blk.vector(body)

        def body(g):
            S.inc(g.dma_start(out=identb[:, :], in_=dram["ident"][:, :]), "gdma", 16)
            S.inc(g.dma_start(out=onesb[:, :], in_=dram["ones"][:, :]), "gdma", 16)
            S.inc(g.dma_start(out=onescol_b[:, :], in_=dram["onescol"][:, :]), "gdma", 16)
            S.inc(g.dma_start(out=trilT_sb[:, 0:CH], in_=dram["trilT"][0:128, :]), "gdma", 16)
            S.inc(g.dma_start(out=trilT_sb[:, CH:2 * CH], in_=dram["trilT"][128:256, :]), "gdma", 16)
        blk.gpsimd(body)

    def layernorm(tag, dbg=None):
        with ExitStack() as L:
            xn_tm = L.enter_context(nc.sbuf_tensor(f"xntm_{tag}", [128, NT * M], BF16))
            scr = L.enter_context(nc.sbuf_tensor(f"lnscr_{tag}", [128, M], F32))
            stt = L.enter_context(nc.sbuf_tensor(f"lnst_{tag}", [128, 8 * NT], F32))
            with nc.Block() as blk:
                def body(a):
                    S.wait(a, "dma")
                    for tt in range(NT):
                        c = 8 * tt
                        S.inc(a.activation(scr[:, :], xres[:, tt * M:(tt + 1) * M], ACT.Square,
                                           accum_out=stt[:, c + 0:c + 1]), "act")
                blk.scalar(body)

                def body(v):
                    S.wait(v, "act")
                    for tt in range(NT):
                        c = 8 * tt
                        S.inc(v.reduce_sum(stt[:, c + 6:c + 7], xres[:, tt * M:(tt + 1) * M], axis=AX.X), "dve")
                    v.drain()
                    for tt in range(NT):
                        c = 8 * tt
                        S.inc(v.tensor_scalar_mul(stt[:, c + 1:c + 2], stt[:, c + 6:c + 7], -1.0 / M), "dve")
                    v.drain()
                    for tt in range(NT):
                        c = 8 * tt
                        S.inc(v.tensor_mul(stt[:, c + 2:c + 3], stt[:, c + 1:c + 2], stt[:, c + 1:c + 2]), "dve")
                    v.drain()
                    for tt in range(NT):
                        c = 8 * tt
                        S.inc(v.scalar_tensor_tensor(stt[:, c + 3:c + 4], stt[:, c + 0:c + 1], 1.0 / M,
                                                     stt[:, c + 2:c + 3], ALU.mult, ALU.subtract), "dve")
                    v.drain()
                    for tt in range(NT):
                        c = 8 * tt
                        S.inc(v.tensor_scalar_add(stt[:, c + 3:c + 4], stt[:, c + 3:c + 4], LNEPS), "dve")
                    v.drain()
                    for tt in range(NT):
                        c = 8 * tt
                        S.inc(v.reciprocal(stt[:, c + 4:c + 5], stt[:, c + 3:c + 4]), "dve")
                    v.drain()
                blk.vector(body)

                def body(a):
                    S.wait(a, "dve")
                    for tt in range(NT):
                        c = 8 * tt
                        S.inc(a.activation(stt[:, c + 5:c + 6], stt[:, c + 4:c + 5], ACT.Sqrt), "act")
                blk.scalar(body)

                def body(v):
                    S.wait(v, "act")
                    for tt in range(NT):
                        c = 8 * tt
                        S.inc(v.tensor_scalar(xn_tm[:, tt * M:(tt + 1) * M], xres[:, tt * M:(tt + 1) * M],
                                              stt[:, c + 1:c + 2], stt[:, c + 5:c + 6],
                                              ALU.add, ALU.mult), "dve")
                blk.vector(body)

                def body(q):
                    S.wait(q, "dve")
                    for tt in range(NT):
                        for mb in range(NM):
                            S.inc(q.dma_start_transpose(
                                xn_fm[:, mb * T + tt * 128:mb * T + (tt + 1) * 128],
                                xn_tm[:, tt * M + mb * 128:tt * M + (mb + 1) * 128]), "dma", 16)
                blk.sync(body)

                if dbg is not None:
                    def body(g):
                        S.wait(g, "dve")
                        S.inc(g.dma_start(out=dbg.rearrange("(t p) m -> p t m", p=128),
                                          in_=xn_tm.rearrange("p (t m) -> p t m", t=NT)), "gdma", 16)
                        if "dbg_stt" in dram:
                            S.inc(g.dma_start(out=dram["dbg_stt"][:, :], in_=stt[:, :]), "gdma", 16)
                        S.wait(g, "gdma")
                    blk.gpsimd(body)

    layernorm("l0", dram.get("dbg_xn"))
    if debug:
        with nc.Block() as blk:
            def body(g):
                S.wait(g, "dma")
                S.inc(g.dma_start(out=dram["dbg_xnfm"][:, :], in_=xn_fm[:, :]), "gdma", 16)
                S.wait(g, "gdma")
            blk.gpsimd(body)

    def gla():
        L = ExitStack()
        def lsb(name, shape, dt):
            return L.enter_context(nc.sbuf_tensor(name, shape, dt))

        wbuf = [lsb(f"glaw{i}", [128, NM * M], BF16) for i in range(2)]
        wgt_sb = lsb("wgt_sb", [128, NM * NH], BF16)
        bc = lsb("bc", [NH, T], F32)
        zz16 = lsb("zz16", [NH, CH], F32)
        ebrow = lsb("ebrow", [1, 128], F32)
        eb64 = lsb("eb64", [128, 128], F32)
        selh = lsb("selh_sb", [NH, NH * 128], F32)
        q_sb = lsb("q_sb", [128, NM * CH], BF16)
        k_sb = lsb("k_sb", [128, NM * CH], BF16)
        v_scr = lsb("v_scr", [128, NM * CH], BF16)
        sq_scr = lsb("sq_scr", [128, NM * CH], BF16)
        k_tm = lsb("k_tm", [128, 2 * M], BF16)
        v_tm = lsb("v_tm", [128, 2 * M], BF16)
        silu_g = lsb("silu_g", [128, 2 * M], BF16)
        rows = lsb("rows", [1, 3 * CH], F32)
        bcq = lsb("bcq", [NH, CH], F32)
        bkneg = lsb("bkneg", [NH, CH], F32)
        mk = lsb("mk", [NH, CH], F32)
        mtab_tm = lsb("mtab_tm", [128, 2 * 80], F32)
        S_f = lsb("S_f", [128, (NH // 2) * 64], F32)
        S_b = lsb("S_b", [128, (NH // 2) * 64], BF16)
        dmask = lsb("dmask", [128, 2 * CH], BF16)
        msT = lsb("msT", [128, 2 * CH], BF16)
        o_tok = lsb("o_tok", [128, 2 * M], F32)
        osq = lsb("osq", [128, M], F32)
        rmsst = lsb("rmsst", [128, 2 * 64], F32)

        PS = ExitStack()
        def psb(name):
            return PS.enter_context(nc.psum_tensor(name, [128, 512], F32))
        ps = [psb(f"g_p{i}") for i in range(8)]
        ps_a0, ps_a1, ps_a2, ps_a3, ps_b, ps_c, ps_d, ps_e = ps

        wname = ("Wq", "Wk", "Wv", "Wg", "Wo")
        wtarget = {}
        wfree = {}

        def load_w(g, idx):
            assert idx < 2 or idx - 2 in wfree, f"load_w({idx}) before consumer of {idx-2} emitted"
            if idx - 2 in wfree:
                S.wait(g, "te", wfree[idx - 2])
            S.inc(g.dma_start(out=wbuf[idx % 2].rearrange("p (c d) -> p c d", d=M),
                              in_=dram[wname[idx % 5]].rearrange("(c p) d -> p c d", p=128)), "gdma", 16)
            wtarget[idx] = S.c["gdma"]

        with ExitStack() as G:
            gtt = G.enter_context(nc.sbuf_tensor("gtt", [NH, 512], F32))
            with nc.Block() as blk:
                def body(g):
                    S.inc(g.dma_start(out=wgt_sb.rearrange("p (c h) -> p c h", h=NH),
                                      in_=dram["Wgt"].rearrange("(c p) h -> p c h", p=128)), "gdma", 16)
                    wtarget["gt"] = S.c["gdma"]
                    load_w(g, 0)
                    load_w(g, 1)
                blk.gpsimd(body)

                def body(q):
                    S.inc(q.dma_start(out=selh[:, :], in_=dram["selh"][:, :]), "dma", 16)
                blk.sync(body)

                def body(v):
                    S.inc(v.memset(zz16[:, :], 0.0), "dve")
                    S.inc(v.memset(S_f[:, :], 0.0), "dve")
                    S.inc(v.memset(S_b[:, :], 0.0), "dve")
                    v.drain()
                blk.vector(body)

                def body(t):
                    S.wait(t, "gdma", wtarget["gt"])
                    S.wait(t, "dma")
                    for ts4 in range(4):
                        for mc in range(NM):
                            ins = t.matmul(ps[ts4][0:NH, 0:512],
                                           wgt_sb[:, mc * NH:(mc + 1) * NH],
                                           xn_fm[:, mc * T + ts4 * 512:mc * T + (ts4 + 1) * 512],
                                           start=(mc == 0), stop=(mc == NM - 1))
                        S.inc(ins, "te")
                blk.tensor(body)

                for ts4 in range(4):
                    def body(a, ts4=ts4):
                        if ts4 == 0:
                            S.wait(a, "te")
                        S.wait(a, "dve")
                        S.inc(a.activation(gtt[:, :], ps[ts4][0:NH, 0:512], ACT.Sigmoid), "act")
                        a.drain()
                        S.inc(a.activation(gtt[:, :], gtt[:, :], ACT.Ln), "act")
                    blk.scalar(body)
                    def body(v, ts4=ts4):
                        S.wait(v, "act")
                        S.inc(v.tensor_scalar_mul(gtt[:, :], gtt[:, :], 1.0 / 16.0), "dve")
                        v.drain()
                        for cc in range(2):
                            c = ts4 * 2 + cc
                            S.inc(v.tensor_tensor_scan(bc[:, c * CH:(c + 1) * CH],
                                                       gtt[:, cc * CH:(cc + 1) * CH],
                                                       zz16[:, :], 0.0, ALU.add, ALU.add), "dve")
                        v.drain()
                    blk.vector(body)

                def body(q):
                    S.wait(q, "dve")
                    S.inc(q.dma_start(out=ebrow[0:1, :].rearrange("o (h c) -> o h c", h=NH),
                                      in_=bc.rearrange("h (c q) -> h c q", q=CH)[:, :, CH - 1:CH],
                                      allow_slow_non_contiguous=True), "dma", 16)
                blk.sync(body)

                def body(a):
                    S.wait(a, "dma")
                    S.inc(a.activation(ebrow[:, :], ebrow[:, :], ACT.Exp), "act")
                blk.scalar(body)

                def body(t):
                    S.wait(t, "act")
                    S.inc(t.matmul(ps_b[:, 0:128], onesf[0:1, 0:128], ebrow[0:1, :],
                                   start=True, stop=True), "te")
                blk.tensor(body)

                def body(v):
                    S.wait(v, "te")
                    S.inc(v.tensor_copy(eb64[:, :], ps_b[:, 0:128]), "dve")
                blk.vector(body)

        for c in range(NCH):
            t0 = c * CH
            with nc.Block() as blk:
                for wi, dst in ((0, q_sb), (1, k_sb), (2, v_scr)):
                    widx = c * 5 + wi
                    wb = wbuf[widx % 2]
                    def body(t, wb=wb, widx=widx):
                        S.wait(t, "gdma", wtarget[widx])
                        S.wait(t, "act")
                        S.wait(t, "dve")
                        for ddb in range(NM):
                            for mc in range(NM):
                                ins = t.matmul(ps[ddb][:, 0:CH],
                                               wb[:, mc * M + ddb * 128:mc * M + ddb * 128 + 128],
                                               xn_fm[:, mc * T + t0:mc * T + t0 + CH],
                                               start=(mc == 0), stop=(mc == NM - 1))
                            S.inc(ins, "te")
                        wfree[widx] = S.c["te"]
                    blk.tensor(body)

                    def body(a, dst=dst):
                        S.wait(a, "te")
                        S.wait(a, "dma")
                        for ddb in range(NM):
                            S.inc(a.activation(dst[:, ddb * CH:(ddb + 1) * CH],
                                               ps[ddb][:, 0:CH], ACT.Copy), "act")
                            S.inc(a.activation(sq_scr[:, ddb * CH:(ddb + 1) * CH],
                                               ps[ddb][:, 0:CH], ACT.Square), "act")
                    blk.scalar(body)

                    def body(g, widx=widx):
                        nxt = widx + 2
                        if nxt < 40 and nxt not in wtarget:
                            load_w(g, nxt)
                    blk.gpsimd(body)

                    def body(t):
                        S.wait(t, "act")
                        S.wait(t, "dve")
                        for seg in range(NM):
                            ins = t.matmul(ps_b[0:1, 0:CH], onescol_b[:, 0:1],
                                           sq_scr[:, seg * CH:(seg + 1) * CH],
                                           start=(seg == 0), stop=(seg == NM - 1))
                        S.inc(ins, "te")
                    blk.tensor(body)

                    if wi == 0:
                        def body(a):
                            S.wait(a, "te")
                            S.inc(a.activation(rows[0:1, 0:CH], ps_b[0:1, 0:CH],
                                               ACT.Ln, bias=consts[0:1, 0:1], scale=1.0 / M), "act")
                        blk.scalar(body)
                        def body(v):
                            S.wait(v, "act")
                            S.inc(v.tensor_scalar(rows[0:1, 0:CH], rows[0:1, 0:CH],
                                                  -0.5, LOG_SCALE, ALU.mult, ALU.add), "dve")
                            v.drain()
                        blk.vector(body)
                    elif wi == 1:
                        def body(a):
                            S.wait(a, "te")
                            S.inc(a.activation(rows[0:1, CH:2 * CH], ps_b[0:1, 0:CH],
                                               ACT.Ln, bias=consts[0:1, 0:1], scale=1.0 / M), "act")
                        blk.scalar(body)
                        def body(v):
                            S.wait(v, "act")
                            S.inc(v.tensor_scalar_mul(rows[0:1, CH:2 * CH], rows[0:1, CH:2 * CH], -0.5), "dve")
                            v.drain()
                        blk.vector(body)
                    else:
                        def body(v):
                            S.wait(v, "te")
                            S.inc(v.tensor_scalar(rows[0:1, 2 * CH:3 * CH], ps_b[0:1, 0:CH],
                                                  1.0 / M, GEPS, ALU.mult, ALU.add), "dve")
                            v.drain()
                        blk.vector(body)

                gidx = c * 5 + 3
                wbg = wbuf[gidx % 2]
                for tl in range(2):
                    for half in range(2):
                        pst = ps[tl * 2 + half]
                        def body(t, pst=pst, tl=tl, half=half):
                            S.wait(t, "gdma", wtarget[gidx])
                            S.wait(t, "act")
                            S.wait(t, "dve")
                            for mc in range(NM):
                                ins = t.matmul(pst[:, 0:512],
                                               xn_fm[:, mc * T + t0 + tl * 128:mc * T + t0 + (tl + 1) * 128],
                                               wbg[:, mc * M + half * 512:mc * M + (half + 1) * 512],
                                               start=(mc == 0), stop=(mc == NM - 1))
                            S.inc(ins, "te")
                            if half == 1 and tl == 1:
                                wfree[gidx] = S.c["te"]
                        blk.tensor(body)
                        def body(a, pst=pst, tl=tl, half=half):
                            S.wait(a, "te")
                            S.inc(a.activation(silu_g[:, tl * M + half * 512:tl * M + (half + 1) * 512],
                                               pst[:, 0:512], ACT.Silu), "act")
                        blk.scalar(body)

                def body(g, gidx=gidx):
                    nxt = gidx + 2
                    if nxt < 40 and nxt not in wtarget:
                        load_w(g, nxt)
                blk.gpsimd(body)

                def body(t):
                    S.wait(t, "dve")
                    S.wait(t, "act")
                    S.inc(t.matmul(ps_b[0:NH, 0:CH], onesf[0:1, 0:NH], rows[0:1, 0:CH],
                                   start=True, stop=True), "te")
                    S.inc(t.matmul(ps_c[0:NH, 0:CH], onesf[0:1, 0:NH], rows[0:1, CH:2 * CH],
                                   start=True, stop=True), "te")
                blk.tensor(body)

                def body(v):
                    S.wait(v, "te")
                    S.inc(v.tensor_add(bcq[:, :], bc[:, t0:t0 + CH], ps_b[0:NH, 0:CH]), "dve")
                    S.inc(v.scalar_tensor_tensor(bkneg[:, :], bc[:, t0:t0 + CH], -1.0,
                                                 ps_c[0:NH, 0:CH], ALU.mult, ALU.add), "dve")
                    v.drain()
                blk.vector(body)

                def body(a):
                    S.wait(a, "dve")
                    S.inc(a.activation(mk[:, :], bkneg[:, :], ACT.Exp,
                                       bias=bc[:, t0 + CH - 1:t0 + CH]), "act")
                blk.scalar(body)

                for tl in range(2):
                    def body(t, tl=tl):
                        S.wait(t, "act")
                        S.wait(t, "dve")
                        sl = slice(tl * 128, (tl + 1) * 128)
                        S.inc(t.transpose(ps_b[:, 0:16], bcq[0:16, sl], identf[0:16, 0:16]), "te")
                        S.inc(t.transpose(ps_c[:, 0:16], mk[0:16, sl], identf[0:16, 0:16]), "te")
                        S.inc(t.transpose(ps_d[:, 0:1], rows[0:1, 2 * CH + tl * 128:2 * CH + (tl + 1) * 128],
                                          identf[0:1, 0:1]), "te")
                        S.inc(t.transpose(ps_e[:, 0:16], bkneg[0:16, sl], identf[0:16, 0:16]), "te")
                    blk.tensor(body)
                    def body(v, tl=tl):
                        S.wait(v, "te")
                        b0 = tl * 80
                        S.inc(v.tensor_copy(mtab_tm[:, b0 + 0:b0 + 16], ps_b[:, 0:16]), "dve")
                        S.inc(v.tensor_copy(mtab_tm[:, b0 + 16:b0 + 32], ps_c[:, 0:16]), "dve")
                        S.inc(v.reciprocal(mtab_tm[:, b0 + 32:b0 + 33], ps_d[:, 0:1]), "dve")
                        S.inc(v.tensor_copy(mtab_tm[:, b0 + 33:b0 + 49], ps_e[:, 0:16]), "dve")
                        v.drain()
                    blk.vector(body)
                    def body(a, tl=tl):
                        S.wait(a, "dve")
                        b0 = tl * 80
                        S.inc(a.activation(mtab_tm[:, b0 + 0:b0 + 16], mtab_tm[:, b0 + 0:b0 + 16], ACT.Exp), "act")
                        S.inc(a.activation(mtab_tm[:, b0 + 32:b0 + 33], mtab_tm[:, b0 + 32:b0 + 33], ACT.Sqrt), "act")
                    blk.scalar(body)

                def body(q):
                    S.wait(q, "act")
                    S.wait(q, "te")
                    for tl in range(2):
                        for seg in range(NM):
                            S.inc(q.dma_start_transpose(
                                k_tm[:, tl * M + seg * 128:tl * M + (seg + 1) * 128],
                                k_sb[:, seg * CH + tl * 128:seg * CH + (tl + 1) * 128]), "dma", 16)
                            S.inc(q.dma_start_transpose(
                                v_tm[:, tl * M + seg * 128:tl * M + (seg + 1) * 128],
                                v_scr[:, seg * CH + tl * 128:seg * CH + (tl + 1) * 128]), "dma", 16)
                blk.sync(body)

                def body(v):
                    S.wait(v, "dma")
                    S.wait(v, "act")
                    v.drain()
                    for tl in range(2):
                        S.inc(v.tensor_scalar_mul(v_tm[:, tl * M:(tl + 1) * M],
                                                  v_tm[:, tl * M:(tl + 1) * M],
                                                  mtab_tm[:, tl * 80 + 32:tl * 80 + 33]), "dve")
                        for h in range(NH):
                            S.inc(v.tensor_scalar_mul(k_tm[:, tl * M + h * 64:tl * M + (h + 1) * 64],
                                                      k_tm[:, tl * M + h * 64:tl * M + (h + 1) * 64],
                                                      mtab_tm[:, tl * 80 + 16 + h:tl * 80 + 17 + h]), "dve")
                blk.vector(body)

                if c == 0 and "dbg_q" in dram:
                    def body(g):
                        S.wait(g, "dve")
                        S.wait(g, "act")
                        S.inc(g.dma_start(out=dram["dbg_q"][:, :], in_=q_sb[:, :]), "gdma", 16)
                        S.inc(g.dma_start(out=dram["dbg_k"][:, :], in_=k_sb[:, :]), "gdma", 16)
                        S.inc(g.dma_start(out=dram["dbg_v"][:, :], in_=v_scr[:, :]), "gdma", 16)
                        S.inc(g.dma_start(out=dram["dbg_bc"][:, :], in_=bc[:, :]), "gdma", 16)
                        S.inc(g.dma_start(out=dram["dbg_bcq"][:, :], in_=bcq[:, :]), "gdma", 16)
                        S.inc(g.dma_start(out=dram["dbg_bkneg"][:, :], in_=bkneg[:, :]), "gdma", 16)
                        S.inc(g.dma_start(out=dram["dbg_mtab"][:, :], in_=mtab_tm[:, :]), "gdma", 16)
                        S.inc(g.dma_start(out=dram["dbg_silug"][:, :], in_=silu_g[:, :]), "gdma", 16)
                        S.inc(g.dma_start(out=dram["dbg_ktm"][:, :], in_=k_tm[:, :]), "gdma", 16)
                        S.inc(g.dma_start(out=dram["dbg_vtm"][:, :], in_=v_tm[:, :]), "gdma", 16)
                        S.wait(g, "gdma")
                    blk.gpsimd(body)

                for h in range(NH):
                    po = (h % 2) * 64
                    sg = h // 2
                    def body(t, h=h):
                        S.wait(t, "act")
                        S.wait(t, "dve")
                        S.inc(t.matmul(ps_c[:, 0:CH], selh[:, h * 128:(h + 1) * 128],
                                       bcq[:, :], start=True, stop=True), "te")
                    blk.tensor(body)

                    def body(a, h=h):
                        S.wait(a, "te")
                        for jh in range(2):
                            S.inc(a.activation(dmask[:, jh * CH:(jh + 1) * CH], ps_c[:, 0:CH],
                                               ACT.Exp,
                                               bias=mtab_tm[:, jh * 80 + 33 + h:jh * 80 + 34 + h]), "act")
                    blk.scalar(body)

                    def body(t, h=h, po=po, sg=sg):
                        for jh in range(2):
                            S.inc(t.matmul((ps_d, ps_e)[jh][:, 0:CH],
                                           k_sb[po:po + 64, sg * CH + jh * 128:sg * CH + (jh + 1) * 128],
                                           q_sb[po:po + 64, sg * CH:(sg + 1) * CH],
                                           start=True, stop=True), "te")
                    blk.tensor(body)

                    def body(v, h=h):
                        S.wait(v, "act")
                        S.wait(v, "te")
                        for jh in range(2):
                            S.inc(v.tensor_mul(dmask[:, jh * CH:(jh + 1) * CH],
                                               dmask[:, jh * CH:(jh + 1) * CH],
                                               trilT_sb[:, jh * CH:(jh + 1) * CH]), "dve")
                        v.drain()
                        for jh in range(2):
                            S.inc(v.tensor_mul(msT[:, jh * CH:(jh + 1) * CH],
                                               (ps_d, ps_e)[jh][:, 0:CH],
                                               dmask[:, jh * CH:(jh + 1) * CH]), "dve")
                        v.drain()
                    blk.vector(body)

                    if c == 0 and h == 0 and "dbg_dmask" in dram:
                        def body(g):
                            S.wait(g, "dve")
                            S.inc(g.dma_start(out=dram["dbg_dmask"][:, :], in_=dmask[:, :]), "gdma", 16)
                            S.inc(g.dma_start(out=dram["dbg_msT"][:, :], in_=msT[:, :]), "gdma", 16)
                            S.wait(g, "gdma")
                        blk.gpsimd(body)

                    def body(t, h=h, po=po, sg=sg):
                        S.wait(t, "dve")
                        S.wait(t, "act")
                        for tl in range(2):
                            S.inc(t.matmul(ps_a2[po:po + 64, 0:64],
                                           k_tm[:, tl * M + h * 64:tl * M + (h + 1) * 64],
                                           v_tm[:, tl * M + h * 64:tl * M + (h + 1) * 64],
                                           start=(tl == 0), stop=(tl == 1)), "te")
                        for ih in range(2):
                            for jh in range(2):
                                S.inc(t.matmul((ps_a0, ps_a1)[ih][:, 0:64],
                                               msT[:, jh * CH + ih * 128:jh * CH + (ih + 1) * 128],
                                               v_tm[:, jh * M + h * 64:jh * M + (h + 1) * 64],
                                               start=(jh == 0), stop=(jh == 1)), "te")
                            S.inc(t.matmul((ps_d, ps_e)[ih][:, 0:64],
                                           q_sb[po:po + 64, sg * CH + ih * 128:sg * CH + (ih + 1) * 128],
                                           S_b[po:po + 64, sg * 64:(sg + 1) * 64],
                                           start=True, stop=True), "te")
                    blk.tensor(body)

                    def body(a, h=h):
                        S.wait(a, "te")
                        for ih in range(2):
                            S.inc(a.activation(o_tok[:, ih * M + h * 64:ih * M + (h + 1) * 64],
                                               (ps_a0, ps_a1)[ih][:, 0:64], ACT.Copy), "act")
                    blk.scalar(body)

                    def body(v, h=h, c=c, po=po, sg=sg):
                        S.wait(v, "act")
                        for ih in range(2):
                            S.inc(v.scalar_tensor_tensor(
                                o_tok[:, ih * M + h * 64:ih * M + (h + 1) * 64],
                                (ps_d, ps_e)[ih][:, 0:64],
                                mtab_tm[:, ih * 80 + h:ih * 80 + h + 1],
                                o_tok[:, ih * M + h * 64:ih * M + (h + 1) * 64],
                                ALU.mult, ALU.add), "dve")
                        S.inc(v.scalar_tensor_tensor(S_f[po:po + 64, sg * 64:(sg + 1) * 64],
                                                     S_f[po:po + 64, sg * 64:(sg + 1) * 64],
                                                     eb64[po:po + 64, h * NCH + c:h * NCH + c + 1],
                                                     ps_a2[po:po + 64, 0:64], ALU.mult, ALU.add), "dve")
                        v.drain()
                        S.inc(v.tensor_copy(S_b[po:po + 64, sg * 64:(sg + 1) * 64],
                                            S_f[po:po + 64, sg * 64:(sg + 1) * 64]), "dve")
                        v.drain()
                    blk.vector(body)

                if c == 0 and "dbg_otok" in dram:
                    def body(g):
                        S.wait(g, "dve")
                        S.inc(g.dma_start(out=dram["dbg_otok"][:, :], in_=o_tok[:, :]), "gdma", 16)
                        S.wait(g, "gdma")
                    blk.gpsimd(body)

                oidx = c * 5 + 4
                wbo = wbuf[oidx % 2]
                og = v_scr
                o_fm = q_sb
                for ih in range(2):
                    tg = 2 * c + ih
                    r = ih * 64
                    def body(a, ih=ih):
                        S.wait(a, "dve")
                        S.inc(a.activation(osq[:, :], o_tok[:, ih * M:(ih + 1) * M], ACT.Square), "act")
                    blk.scalar(body)
                    def body(v, r=r):
                        S.wait(v, "act")
                        S.inc(v.reduce_sum(rmsst[:, r:r + 16],
                                           osq.rearrange("p (h d) -> p h d", d=64), axis=AX.X), "dve")
                        v.drain()
                        S.inc(v.tensor_scalar(rmsst[:, r + 16:r + 32], rmsst[:, r:r + 16],
                                              1.0 / 64, GEPS, ALU.mult, ALU.add), "dve")
                        v.drain()
                        S.inc(v.reciprocal(rmsst[:, r + 32:r + 48], rmsst[:, r + 16:r + 32]), "dve")
                        v.drain()
                    blk.vector(body)
                    def body(a, r=r):
                        S.wait(a, "dve")
                        S.inc(a.activation(rmsst[:, r + 48:r + 64], rmsst[:, r + 32:r + 48], ACT.Sqrt), "act")
                    blk.scalar(body)
                    def body(v, ih=ih, r=r):
                        S.wait(v, "act")
                        for h in range(NH):
                            S.inc(v.scalar_tensor_tensor(
                                og[:, ih * M + h * 64:ih * M + (h + 1) * 64],
                                o_tok[:, ih * M + h * 64:ih * M + (h + 1) * 64],
                                rmsst[:, r + 48 + h:r + 49 + h],
                                silu_g[:, ih * M + h * 64:ih * M + (h + 1) * 64],
                                ALU.mult, ALU.mult), "dve")
                        v.drain()
                    blk.vector(body)
                    def body(q, ih=ih):
                        S.wait(q, "dve")
                        S.wait(q, "te")
                        for seg in range(NM):
                            S.inc(q.dma_start_transpose(
                                o_fm[:, seg * CH + ih * 128:seg * CH + (ih + 1) * 128],
                                og[:, ih * M + seg * 128:ih * M + (seg + 1) * 128]), "dma", 16)
                    blk.sync(body)
                    def body(t, ih=ih, wbo=wbo, oidx=oidx):
                        S.wait(t, "gdma", wtarget[oidx])
                        S.wait(t, "dma")
                        S.wait(t, "act")
                        S.wait(t, "dve")
                        for mh in range(2):
                            for seg in range(NM):
                                ins = t.matmul((ps_b, ps_c)[mh][:, 0:512],
                                               o_fm[:, seg * CH + ih * 128:seg * CH + (ih + 1) * 128],
                                               wbo[:, seg * M + mh * 512:seg * M + (mh + 1) * 512],
                                               start=(seg == 0), stop=(seg == NM - 1))
                            S.inc(ins, "te")
                        if ih == 1:
                            wfree[oidx] = S.c["te"]
                    blk.tensor(body)
                    def body(v, tg=tg):
                        S.wait(v, "te")
                        for mh in range(2):
                            S.inc(v.tensor_add(xres[:, tg * M + mh * 512:tg * M + (mh + 1) * 512],
                                               (ps_b, ps_c)[mh][:, 0:512],
                                               xres[:, tg * M + mh * 512:tg * M + (mh + 1) * 512]), "dve")
                        v.drain()
                    blk.vector(body)

                if c == 0 and "dbg_og" in dram:
                    def body(g):
                        S.wait(g, "dve")
                        S.wait(g, "dma")
                        S.inc(g.dma_start(out=dram["dbg_og"][:, :], in_=og[:, :]), "gdma", 16)
                        S.inc(g.dma_start(out=dram["dbg_ofm"][:, :], in_=o_fm[:, :]), "gdma", 16)
                        S.inc(g.dma_start(out=dram["dbg_xres0"][:, :], in_=xres[:, 0:2 * M]), "gdma", 16)
                        S.wait(g, "gdma")
                    blk.gpsimd(body)

                def body(g, oidx=oidx):
                    for nxt in (oidx + 1, oidx + 2):
                        if nxt < 40 and nxt not in wtarget:
                            load_w(g, nxt)
                blk.gpsimd(body)

        L.close()
        PS.close()

    if not SKIP_GLA:
        gla()

    def moe():
        NE, HF = 8, 512
        L = ExitStack()
        def lsb(name, shape, dt):
            return L.enter_context(nc.sbuf_tensor(name, shape, dt))
        gw_sb = lsb("gw_sb", [128, NM * NE], BF16)
        glog = lsb("glog", [NE, T], F32)
        gsc = lsb("gsc", [128, NT * NE], F32)
        scl = lsb("scl", [128, NT * NE], F32)
        mx8 = lsb("mx8", [128, 8], F32)
        mi8 = lsb("mi8", [128, 8], mybir.dt.uint32)
        gst = lsb("gst", [128, NT * 4], F32)
        w1_sb = lsb("w1_sb", [128, NM * HF], BF16)
        w3_sb = lsb("w3_sb", [128, NM * HF], BF16)
        w2_sb = lsb("w2_sb", [128, 4 * M], BF16)
        h1_sb = lsb("h1_sb", [128, HF], BF16)
        h_sb = lsb("h_sb", [128, HF], BF16)
        h_fm = lsb("h_fm", [128, 4 * 128], BF16)
        y_sb = lsb("y_sb", [128, NT * M], F32)

        PS = ExitStack()
        def psb(name, shape):
            return PS.enter_context(nc.psum_tensor(name, shape, F32))
        ps_l = psb("m_pl", [NE, T])
        ps0 = psb("m_p0", [128, 512])
        ps1 = psb("m_p1", [128, 512])
        ps2 = psb("m_p2", [128, 512])
        ps3 = psb("m_p3", [128, 512])

        wtgt = {}
        def load_we(g, e):
            S.inc(g.dma_start(out=w1_sb.rearrange("p (c h) -> p c h", h=HF),
                              in_=dram["w1"][e].rearrange("(c p) h -> p c h", p=128)), "gdma", 16)
            S.inc(g.dma_start(out=w3_sb.rearrange("p (c h) -> p c h", h=HF),
                              in_=dram["w3"][e].rearrange("(c p) h -> p c h", p=128)), "gdma", 16)
            S.inc(g.dma_start(out=w2_sb.rearrange("p (c m) -> p c m", m=M),
                              in_=dram["w2"][e].rearrange("(c p) m -> p c m", p=128)), "gdma", 16)
            wtgt[e] = S.c["gdma"]

        with nc.Block() as blk:
            def body(g):
                S.inc(g.dma_start(out=gw_sb.rearrange("p (c e) -> p c e", e=NE),
                                  in_=dram["gate_w"].rearrange("(c p) e -> p c e", p=128)), "gdma", 16)
                wtgt["g"] = S.c["gdma"]
            blk.gpsimd(body)

            def body(t):
                S.wait(t, "gdma", wtgt["g"])
                S.wait(t, "dve")
                for ts4 in range(4):
                    for mc in range(NM):
                        ins = t.matmul(ps_l[0:NE, ts4 * 512:(ts4 + 1) * 512],
                                       gw_sb[:, mc * NE:(mc + 1) * NE],
                                       xn_fm[:, mc * T + ts4 * 512:mc * T + (ts4 + 1) * 512],
                                       start=(mc == 0), stop=(mc == NM - 1))
                    S.inc(ins, "te")
            blk.tensor(body)

            def body(a):
                S.wait(a, "te")
                S.inc(a.activation(glog[:, :], ps_l[0:NE, 0:T], ACT.Copy), "act")
            blk.scalar(body)

            for grp in range(4):
                def body(t, grp=grp):
                    S.wait(t, "act")
                    S.wait(t, "dve")
                    for q4 in range(4):
                        tt = grp * 4 + q4
                        S.inc(t.transpose((ps0, ps1, ps2, ps3)[q4][:, 0:NE],
                                          glog[0:NE, tt * 128:(tt + 1) * 128],
                                          identf[0:NE, 0:NE]), "te")
                blk.tensor(body)
                def body(v, grp=grp):
                    S.wait(v, "te")
                    for q4 in range(4):
                        tt = grp * 4 + q4
                        S.inc(v.tensor_copy(gsc[:, tt * NE:(tt + 1) * NE],
                                            (ps0, ps1, ps2, ps3)[q4][:, 0:NE]), "dve")
                blk.vector(body)

            def body(v):
                for tt in range(NT):
                    S.inc(v.reduce_max(gst[:, tt * 4:tt * 4 + 1], gsc[:, tt * NE:(tt + 1) * NE],
                                       axis=AX.X, negate=True), "dve")
                v.drain()
            blk.vector(body)
            def body(a):
                S.wait(a, "dve")
                for tt in range(NT):
                    S.inc(a.activation(scl[:, tt * NE:(tt + 1) * NE], gsc[:, tt * NE:(tt + 1) * NE],
                                       ACT.Exp, bias=gst[:, tt * 4:tt * 4 + 1],
                                       accum_out=gst[:, tt * 4 + 1:tt * 4 + 2]), "act")
            blk.scalar(body)
            def body(v):
                S.wait(v, "act")
                for tt in range(NT):
                    S.inc(v.reciprocal(gst[:, tt * 4 + 2:tt * 4 + 3], gst[:, tt * 4 + 1:tt * 4 + 2]), "dve")
                v.drain()
                for tt in range(NT):
                    S.inc(v.tensor_scalar_mul(scl[:, tt * NE:(tt + 1) * NE], scl[:, tt * NE:(tt + 1) * NE],
                                              gst[:, tt * 4 + 2:tt * 4 + 3]), "dve")
                v.drain()
                for tt in range(NT):
                    v.max_with_indices(mx8[:, :], mi8[:, :], scl[:, tt * NE:(tt + 1) * NE])
                    v.drain()
                    S.inc(v.scalar_tensor_tensor(scl[:, tt * NE:(tt + 1) * NE],
                                                 scl[:, tt * NE:(tt + 1) * NE],
                                                 mx8[:, 1:2],
                                                 scl[:, tt * NE:(tt + 1) * NE],
                                                 ALU.is_ge, ALU.mult), "dve")
                    v.drain()
            blk.vector(body)

        for e in range(NE):
            with nc.Block() as blk:
                def body(g, e=e):
                    S.wait(g, "te")
                    load_we(g, e)
                blk.gpsimd(body)

                for tt in range(NT):
                    def body(t, e=e, tt=tt):
                        S.wait(t, "gdma", wtgt[e])
                        S.wait(t, "act")
                        S.wait(t, "dve")
                        for mc in range(NM):
                            ins = t.matmul(ps0[:, 0:HF],
                                           xn_fm[:, mc * T + tt * 128:mc * T + (tt + 1) * 128],
                                           w1_sb[:, mc * HF:(mc + 1) * HF],
                                           start=(mc == 0), stop=(mc == NM - 1))
                        S.inc(ins, "te")
                        for mc in range(NM):
                            ins = t.matmul(ps1[:, 0:HF],
                                           xn_fm[:, mc * T + tt * 128:mc * T + (tt + 1) * 128],
                                           w3_sb[:, mc * HF:(mc + 1) * HF],
                                           start=(mc == 0), stop=(mc == NM - 1))
                        S.inc(ins, "te")
                    blk.tensor(body)

                    def body(a, e=e, tt=tt):
                        S.wait(a, "te")
                        S.inc(a.activation(h1_sb[:, :], ps0[:, 0:HF], ACT.Silu,
                                           scale=scl[:, tt * NE + e:tt * NE + e + 1]), "act")
                    blk.scalar(body)

                    def body(v, e=e, tt=tt):
                        S.wait(v, "act")
                        S.inc(v.scalar_tensor_tensor(h_sb[:, :], ps1[:, 0:HF],
                                                     scl[:, tt * NE + e:tt * NE + e + 1],
                                                     h1_sb[:, :], ALU.mult, ALU.mult), "dve")
                        v.drain()
                    blk.vector(body)

                    def body(q):
                        S.wait(q, "dve")
                        S.wait(q, "te")
                        for hb in range(4):
                            S.inc(q.dma_start_transpose(h_fm[:, hb * 128:(hb + 1) * 128],
                                                        h_sb[:, hb * 128:(hb + 1) * 128]), "dma", 16)
                    blk.sync(body)

                    def body(t, tt=tt):
                        S.wait(t, "dma")
                        for mh in range(2):
                            for hb in range(4):
                                ins = t.matmul((ps2, ps3)[mh][:, 0:512],
                                               h_fm[:, hb * 128:(hb + 1) * 128],
                                               w2_sb[:, hb * M + mh * 512:hb * M + (mh + 1) * 512],
                                               start=(hb == 0), stop=(hb == 3))
                            S.inc(ins, "te")
                    blk.tensor(body)

                    if e == 0:
                        def body(a, tt=tt):
                            S.wait(a, "te")
                            for mh in range(2):
                                S.inc(a.activation(y_sb[:, tt * M + mh * 512:tt * M + (mh + 1) * 512],
                                                   (ps2, ps3)[mh][:, 0:512], ACT.Copy), "act")
                        blk.scalar(body)
                    else:
                        def body(v, tt=tt):
                            S.wait(v, "te")
                            for mh in range(2):
                                S.inc(v.tensor_add(y_sb[:, tt * M + mh * 512:tt * M + (mh + 1) * 512],
                                                   (ps2, ps3)[mh][:, 0:512],
                                                   y_sb[:, tt * M + mh * 512:tt * M + (mh + 1) * 512]), "dve")
                            v.drain()
                        blk.vector(body)

        with nc.Block() as blk:
            def body(v):
                S.wait(v, "act")
                for tt in range(NT):
                    for mh in range(2):
                        S.inc(v.tensor_add(xres[:, tt * M + mh * 512:tt * M + (mh + 1) * 512],
                                           y_sb[:, tt * M + mh * 512:tt * M + (mh + 1) * 512],
                                           xres[:, tt * M + mh * 512:tt * M + (mh + 1) * 512]), "dve")
                v.drain()
            blk.vector(body)

        L.close()
        PS.close()

    if not globals().get('SKIP_MOE', False):
        moe()

    layernorm("l1")

    def routing_s():
        with ExitStack() as RS:
            tokrep = RS.enter_context(nc.sbuf_tensor("rt_tokrep", [128, M], F32))
            xnscr = RS.enter_context(nc.sbuf_tensor("rt_xnscr", [128, M], F32))
            st2 = RS.enter_context(nc.sbuf_tensor("rt_st2", [128, 8], F32))
            xnkeep = RS.enter_context(nc.sbuf_tensor("rt_keep", [128, M + 8], F32))
            with nc.Block() as blk:
                with ExitStack() as RPS:
                    ps0 = RPS.enter_context(nc.psum_tensor("rt_ps0", [128, 512], F32))
                    ps1 = RPS.enter_context(nc.psum_tensor("rt_ps1", [128, 512], F32))
                    def body(q):
                        S.inc(q.dma_start(out=xnscr[0:1, 0:M], in_=dram["q_rt_row"][:, :]), "dma", 16)
                    blk.sync(body)
                    def body(t):
                        S.wait(t, "dma")
                        S.wait(t, "dve")
                        S.wait(t, "act")
                        for mh in range(2):
                            S.inc(t.matmul((ps0, ps1)[mh][:, 0:512], onesf[0:1, :],
                                           xnscr[0:1, mh * 512:(mh + 1) * 512], start=True, stop=True), "te")
                    blk.tensor(body)
                    def body(a):
                        S.wait(a, "te")
                        for mh in range(2):
                            S.inc(a.activation(tokrep[:, mh * 512:(mh + 1) * 512],
                                               (ps0, ps1)[mh][:, 0:512], ACT.Copy), "act")
                    blk.scalar(body)
                    for tt in range(NT):
                        def body(a, tt=tt):
                            S.wait(a, "dve")
                            if tt == 0:
                                S.wait(a, "te")
                            S.inc(a.activation(xnscr[:, :], xres[:, tt * M:(tt + 1) * M], ACT.Square,
                                               accum_out=st2[:, 0:1]), "act")
                        blk.scalar(body)
                        def body(v, tt=tt):
                            S.wait(v, "act")
                            S.inc(v.reduce_sum(st2[:, 1:2], xres[:, tt * M:(tt + 1) * M], axis=AX.X), "dve")
                            v.drain()
                            S.inc(v.tensor_scalar_mul(st2[:, 2:3], st2[:, 1:2], -1.0 / M), "dve")
                            v.drain()
                            S.inc(v.tensor_mul(st2[:, 3:4], st2[:, 2:3], st2[:, 2:3]), "dve")
                            v.drain()
                            S.inc(v.scalar_tensor_tensor(st2[:, 4:5], st2[:, 0:1], 1.0 / M,
                                                         st2[:, 3:4], ALU.mult, ALU.subtract), "dve")
                            v.drain()
                            S.inc(v.tensor_scalar_add(st2[:, 4:5], st2[:, 4:5], LNEPS), "dve")
                            v.drain()
                            S.inc(v.reciprocal(st2[:, 5:6], st2[:, 4:5]), "dve")
                            v.drain()
                        blk.vector(body)
                        def body(a, tt=tt):
                            S.wait(a, "dve")
                            S.inc(a.activation(st2[:, 6:7], st2[:, 5:6], ACT.Sqrt), "act")
                        blk.scalar(body)
                        def body(v, tt=tt):
                            S.wait(v, "act")
                            S.inc(v.tensor_scalar(xnscr[:, :], xres[:, tt * M:(tt + 1) * M],
                                                  st2[:, 2:3], st2[:, 6:7], ALU.add, ALU.mult), "dve")
                            v.drain()
                        blk.vector(body)
                        if tt == 0 and "dbg_tokrep" in dram:
                            def body(v):
                                S.inc(v.tensor_copy(xnkeep[:, 0:M], xnscr[:, :]), "dve")
                                S.inc(v.tensor_copy(xnkeep[:, M:M + 8], st2[:, :]), "dve")
                                v.drain()
                            blk.vector(body)
                            def body(g):
                                S.wait(g, "dve")
                                S.inc(g.dma_start(out=dram["dbg_tokrep"][:, :], in_=tokrep[:, :]), "gdma", 16)
                                S.inc(g.dma_start(out=dram["dbg_xnt"][:, :], in_=xnkeep[:, 0:M]), "gdma", 16)
                                S.inc(g.dma_start(out=dram["dbg_tiny"][:, :], in_=xnkeep[0:1, M:M + 8]), "gdma", 16)
                            blk.gpsimd(body)
                        def body(v, tt=tt):
                            S.inc(v.tensor_mul(xnscr[:, :], xnscr[:, :], tokrep[:, :]), "dve")
                            v.drain()
                            S.inc(v.reduce_sum(s2d[:, tt:tt + 1], xnscr[:, :], axis=AX.X), "dve")
                            v.drain()
                        blk.vector(body)

    routing_s()

    def mamba():
        DIN, DST, NDT = 2048, 16, 16
        TC, NC = 512, 4
        L = ExitStack()
        def lsb(name, shape, dt):
            return L.enter_context(nc.sbuf_tensor(name, shape, dt))
        wq_sb = lsb("mw_q", [128, 4 * 1024], BF16)
        wx_sb = lsb("mw_x", [128, NDT * 33], BF16)
        wdt_n = lsb("m_wdtn", [128, DST], F32)
        bdt_n = lsb("m_bdtn", [128, DST], F32)
        wdt_r = lsb("m_wdtr", [128, DST], F32)
        bdt_r = lsb("m_bdtr", [128, DST], F32)
        cb_sb = lsb("m_cb", [128, DST], F32)
        dp_sb = lsb("m_dp", [128, DST], F32)
        cw_sb = lsb("m_cw", [128, NDT * 4], F32)
        alog = lsb("m_alog", [128, NDT * DST], F32)
        brow = lsb("m_brow", [1, 2 * 512], BF16)
        states = lsb("m_states", [128, NDT * DST], F32)
        xd = lsb("m_xd", [128, NDT * 516], BF16)
        zs = lsb("m_zs", [128, NDT * TC], BF16)
        uu = lsb("m_u", [128, NDT * TC], BF16)
        bca = lsb("m_bca", [128, 32 * TC], BF16)
        ssm_t = lsb("m_ssmt", [33, TC], F32)
        ssm_tb = lsb("m_ssmtb", [33, TC], BF16)
        ssm_e = lsb("m_ssme", [128, 33 * 4], F32)
        lns = lsb("m_lns", [128, TC], F32)
        sig = lns
        acc = lns
        wdd = lsb("m_w", [128, TC], BF16)
        dA = lsb("m_dA", [128, TC], F32)
        bP = lsb("m_bP", [128, TC], F32)
        hh = lsb("m_h", [128, TC], F32)
        gg = lsb("m_g", [128, TC], BF16)

        PS = ExitStack()
        def psb(name):
            return PS.enter_context(nc.psum_tensor(name, [128, 512], F32))
        ps = [psb(f"mm_p{i}") for i in range(8)]

        wtgt = {}
        xd3 = xd.rearrange("p (c w) -> p c w", w=516)

        with nc.Block() as blk:
            def body(g):
                S.inc(g.dma_start(out=wx_sb.rearrange("p (c k) -> p c k", k=33),
                                  in_=dram["W_xT"].rearrange("(c p) k -> p c k", p=128)), "gdma", 16)
                S.inc(g.dma_start(out=alog.rearrange("p (c s) -> p c s", s=DST),
                                  in_=dram["A_log"].rearrange("(c p) s -> p c s", p=128)), "gdma", 16)
                S.inc(g.dma_start(out=wdt_r.rearrange("p (c o) -> p c o", o=1),
                                  in_=dram["W_dt"].rearrange("(c p) o -> p c o", p=128),
                                  allow_slow_non_contiguous=True), "gdma", 16)
                S.inc(g.dma_start(out=bdt_r.rearrange("p (c o) -> p c o", o=1),
                                  in_=dram["b_dt"].rearrange("(c p) o -> p c o", p=128),
                                  allow_slow_non_contiguous=True), "gdma", 16)
                S.inc(g.dma_start(out=cb_sb.rearrange("p (c o) -> p c o", o=1),
                                  in_=dram["conv_b"].rearrange("(c p) o -> p c o", p=128),
                                  allow_slow_non_contiguous=True), "gdma", 16)
                S.inc(g.dma_start(out=dp_sb.rearrange("p (c o) -> p c o", o=1),
                                  in_=dram["D_param"].rearrange("(c p) o -> p c o", p=128),
                                  allow_slow_non_contiguous=True), "gdma", 16)
                S.inc(g.dma_start(out=cw_sb.rearrange("p (c j) -> p c j", j=4),
                                  in_=dram["conv_w"].rearrange("(c p) j -> p c j", p=128)), "gdma", 16)
            blk.gpsimd(body)

            def body(v):
                S.wait(v, "gdma")
                S.inc(v.memset(states[:, :], 0.0), "dve")
                S.inc(v.memset(xd3[:, :, 0:3], 0.0), "dve")
                S.inc(v.tensor_scalar_mul(wdt_n[:, :], wdt_r[:, :], -1.0), "dve")
                S.inc(v.tensor_scalar_mul(bdt_n[:, :], bdt_r[:, :], -1.0), "dve")
                v.drain()
            blk.vector(body)

            def body(a):
                S.wait(a, "gdma")
                S.inc(a.activation(alog[:, :], alog[:, :], ACT.Exp), "act")
            blk.scalar(body)

        for c in range(NC):
            t0 = c * TC
            with nc.Block() as blk:
                for q in range(8):
                    def body(g, q=q, c=c):
                        S.wait(g, "te")
                        S.inc(g.dma_start(out=wq_sb.rearrange("p (m d) -> p m d", d=512),
                                          in_=dram["W_inT"][:, q * 512:(q + 1) * 512]
                                          .rearrange("(m p) d -> p m d", p=128)), "gdma", 16)
                        wtgt[("in", c, q)] = S.c["gdma"]
                    blk.gpsimd(body)

                    def body(t, q=q, c=c):
                        S.wait(t, "gdma", wtgt[("in", c, q)])
                        S.wait(t, "act")
                        S.wait(t, "dve")
                        for ddb in range(4):
                            for mc in range(NM):
                                ins = t.matmul(ps[(q % 2) * 4 + ddb][:, 0:512],
                                               wq_sb[:, mc * 512 + ddb * 128:mc * 512 + ddb * 128 + 128],
                                               xn_fm[:, mc * T + t0:mc * T + t0 + TC],
                                               start=(mc == 0), stop=(mc == NM - 1))
                            S.inc(ins, "te")
                    blk.tensor(body)

                    def body(a, q=q):
                        S.wait(a, "te")
                        for ddb in range(4):
                            gi = q * 4 + ddb
                            if gi < 16:
                                S.inc(a.activation(xd[:, gi * 516 + 3:gi * 516 + 515],
                                                   ps[(q % 2) * 4 + ddb][:, 0:512], ACT.Copy), "act")
                            else:
                                zq = gi - 16
                                S.inc(a.activation(zs[:, zq * TC:(zq + 1) * TC],
                                                   ps[(q % 2) * 4 + ddb][:, 0:512], ACT.Silu), "act")
                    blk.scalar(body)

                for dt_i in range(NDT):
                    def body(v, dt_i=dt_i):
                        S.wait(v, "act")
                        b0 = dt_i * 516
                        S.inc(v.tensor_scalar_mul(acc[:, :], xd[:, b0:b0 + 512],
                                                  cw_sb[:, dt_i * 4:dt_i * 4 + 1]), "dve")
                        v.drain()
                        for j in (1, 2, 3):
                            S.inc(v.scalar_tensor_tensor(acc[:, :], xd[:, b0 + j:b0 + j + 512],
                                                         cw_sb[:, dt_i * 4 + j:dt_i * 4 + j + 1],
                                                         acc[:, :], ALU.mult, ALU.add), "dve")
                            v.drain()
                    blk.vector(body)
                    def body(a, dt_i=dt_i):
                        S.wait(a, "dve")
                        S.inc(a.activation(uu[:, dt_i * TC:(dt_i + 1) * TC], acc[:, :], ACT.Silu,
                                           bias=cb_sb[:, dt_i:dt_i + 1]), "act")
                    blk.scalar(body)

                def body(v):
                    S.wait(v, "act")
                    S.inc(v.tensor_copy(xd3[:, :, 0:3], xd3[:, :, 512:515]), "dve")
                    v.drain()
                blk.vector(body)

                def body(t):
                    S.wait(t, "act")
                    for ts4 in range(4):
                        for dt_i in range(NDT):
                            ins = t.matmul(ps[ts4][:, 0:33],
                                           uu[:, dt_i * TC + ts4 * 128:dt_i * TC + (ts4 + 1) * 128],
                                           wx_sb[:, dt_i * 33:(dt_i + 1) * 33],
                                           start=(dt_i == 0), stop=(dt_i == NDT - 1))
                        S.inc(ins, "te")
                blk.tensor(body)
                def body(a):
                    S.wait(a, "te")
                    for ts4 in range(4):
                        S.inc(a.activation(ssm_e[:, ts4 * 33:(ts4 + 1) * 33], ps[ts4][:, 0:33],
                                           ACT.Copy), "act")
                blk.scalar(body)
                for ts4 in range(4):
                    def body(t, ts4=ts4):
                        S.wait(t, "act")
                        S.wait(t, "dve")
                        S.inc(t.transpose(ps[4 + ts4][0:33, 0:128],
                                          ssm_e[:, ts4 * 33:(ts4 + 1) * 33], identf[:, :]), "te")
                    blk.tensor(body)
                    def body(v, ts4=ts4):
                        S.wait(v, "te")
                        S.inc(v.tensor_copy(ssm_t[:, ts4 * 128:(ts4 + 1) * 128],
                                            ps[4 + ts4][0:33, 0:128]), "dve")
                    blk.vector(body)
                def body(v):
                    v.drain()
                    S.inc(v.tensor_copy(ssm_tb[:, :], ssm_t[:, :]), "dve")
                    v.drain()
                blk.vector(body)

                def body(t):
                    S.wait(t, "dve")
                    S.inc(t.matmul(ps[1][:, 0:512], onesf[0:1, :], ssm_t[0:1, :],
                                   start=True, stop=True), "te")
                blk.tensor(body)
                for s in range(DST):
                    def body(q, s=s):
                        if s == 0:
                            S.wait(q, "dve")
                        else:
                            S.wait(q, "te")
                        S.inc(q.dma_start(out=brow[0:1, 0:512],
                                          in_=ssm_tb[1 + s:2 + s, :]), "dma", 16)
                        S.inc(q.dma_start(out=brow[0:1, 512:1024],
                                          in_=ssm_tb[17 + s:18 + s, :]), "dma", 16)
                    blk.sync(body)
                    def body(t, s=s):
                        S.wait(t, "dma")
                        if s > 0:
                            S.wait(t, "act")
                        S.inc(t.matmul(ps[2][:, 0:512], onesb[0:1, :],
                                       brow[0:1, 0:512], start=True, stop=True), "te")
                        S.inc(t.matmul(ps[3][:, 0:512], onesb[0:1, :],
                                       brow[0:1, 512:1024], start=True, stop=True), "te")
                    blk.tensor(body)
                    def body(a, s=s):
                        S.wait(a, "te")
                        S.inc(a.activation(bca[:, s * TC:(s + 1) * TC], ps[2][:, 0:512], ACT.Copy), "act")
                        S.inc(a.activation(bca[:, (16 + s) * TC:(17 + s) * TC], ps[3][:, 0:512], ACT.Copy), "act")
                    blk.scalar(body)

                if c == 0 and "dbg_m_xd" in dram:
                    def body(g):
                        S.wait(g, "act")
                        S.wait(g, "dve")
                        S.inc(g.dma_start(out=dram["dbg_m_xd"][:, :], in_=xd[:, :]), "gdma", 16)
                        S.inc(g.dma_start(out=dram["dbg_m_u"][:, :], in_=uu[:, :]), "gdma", 16)
                        S.inc(g.dma_start(out=dram["dbg_m_ssmt"][:, :], in_=ssm_t[:, :]), "gdma", 16)
                        S.inc(g.dma_start(out=dram["dbg_m_bca"][:, :], in_=bca[:, :]), "gdma", 16)
                        S.inc(g.dma_start(out=dram["dbg_m_zs"][:, :], in_=zs[:, :]), "gdma", 16)
                        S.wait(g, "gdma")
                    blk.gpsimd(body)

                for dt_i in range(NDT):
                    def body(a, dt_i=dt_i):
                        S.wait(a, "dve")
                        S.inc(a.activation(sig[:, :], ps[1][:, 0:512], ACT.Sigmoid,
                                           scale=wdt_n[:, dt_i:dt_i + 1], bias=bdt_n[:, dt_i:dt_i + 1]), "act")
                        a.drain()
                        S.inc(a.activation(lns[:, :], sig[:, :], ACT.Ln), "act")
                        a.drain()
                    blk.scalar(body)
                    def body(v, dt_i=dt_i):
                        S.wait(v, "act")
                        S.inc(v.scalar_tensor_tensor(wdd[:, :], lns[:, :], -1.0,
                                                     uu[:, dt_i * TC:(dt_i + 1) * TC],
                                                     ALU.mult, ALU.mult), "dve")
                        v.drain()
                    blk.vector(body)
                    for s in range(DST):
                        def body(a, dt_i=dt_i, s=s):
                            S.wait(a, "dve")
                            S.inc(a.activation(dA[:, :], lns[:, :], ACT.Exp,
                                               scale=alog[:, dt_i * DST + s:dt_i * DST + s + 1]), "act")
                        blk.scalar(body)
                        def body(v, dt_i=dt_i, s=s, c=c):
                            S.wait(v, "act")
                            S.wait(v, "te")
                            S.inc(v.tensor_mul(bP[:, :], wdd[:, :], bca[:, s * TC:(s + 1) * TC]), "dve")
                            v.drain()
                            S.inc(v.tensor_tensor_scan(hh[:, :], dA[:, :], bP[:, :],
                                                       states[:, dt_i * DST + s:dt_i * DST + s + 1],
                                                       ALU.mult, ALU.add), "dve")
                            v.drain()
                            S.inc(v.tensor_copy(states[:, dt_i * DST + s:dt_i * DST + s + 1],
                                                hh[:, TC - 1:TC]), "dve")
                            S.inc(v.tensor_mul(gg[:, :], hh[:, :], bca[:, (16 + s) * TC:(17 + s) * TC]), "dve")
                            v.drain()
                        blk.vector(body)
                        if c == 0 and dt_i == 0 and s == 0 and "dbg_m_lns" in dram:
                            def body(g):
                                S.wait(g, "dve")
                                S.inc(g.dma_start(out=dram["dbg_m_lns"][:, :], in_=lns[:, :]), "gdma", 16)
                                S.inc(g.dma_start(out=dram["dbg_m_h"][:, :], in_=hh[:, :]), "gdma", 16)
                                S.inc(g.dma_start(out=dram["dbg_m_dA"][:, :], in_=dA[:, :]), "gdma", 16)
                                S.inc(g.dma_start(out=dram["dbg_m_bP"][:, :], in_=bP[:, :]), "gdma", 16)
                                S.inc(g.dma_start(out=dram["dbg_m_wdd"][:, :], in_=wdd[:, :]), "gdma", 16)
                                S.wait(g, "gdma")
                            blk.gpsimd(body)
                        if not globals().get("SKIP_SCAN_S", False):
                            def body(t, dt_i=dt_i, s=s):
                                S.wait(t, "dve")
                                S.inc(t.matmul(ps[0][:, 0:512], identb[:, :], gg[:, :],
                                               start=(s == 0), stop=False), "te")
                            blk.tensor(body)
                    def body(v, dt_i=dt_i):
                        S.wait(v, "te")
                        S.inc(v.tensor_scalar_mul(gg[:, :], uu[:, dt_i * TC:(dt_i + 1) * TC],
                                                  dp_sb[:, dt_i:dt_i + 1]), "dve")
                        v.drain()
                    blk.vector(body)
                    def body(t, dt_i=dt_i):
                        S.wait(t, "dve")
                        sk = globals().get("SKIP_SCAN_S", False)
                        S.inc(t.matmul(ps[0][:, 0:512], identb[:, :], gg[:, :],
                                       start=sk, stop=True), "te")
                    blk.tensor(body)
                    def body(v, dt_i=dt_i):
                        S.wait(v, "te")
                        S.inc(v.tensor_mul(xd[:, dt_i * 516 + 3:dt_i * 516 + 515], ps[0][:, 0:512],
                                           zs[:, dt_i * TC:(dt_i + 1) * TC]), "dve")
                        v.drain()
                    blk.vector(body)

                for quar in range(4):
                    def body(g, quar=quar, c=c):
                        S.wait(g, "te")
                        S.inc(g.dma_start(out=wq_sb.rearrange("p (m d) -> p m d", d=1024),
                                          in_=dram["W_outT"][quar * 512:(quar + 1) * 512, :]
                                          .rearrange("(m p) d -> p m d", p=128)), "gdma", 16)
                        wtgt[("out", c, quar)] = S.c["gdma"]
                    blk.gpsimd(body)
                    def body(t, quar=quar, c=c):
                        S.wait(t, "gdma", wtgt[("out", c, quar)])
                        S.wait(t, "dve")
                        for ts4 in range(4):
                            for mh in range(2):
                                for db in range(4):
                                    dt_i = quar * 4 + db
                                    ins = t.matmul(ps[ts4 * 2 + mh][:, 0:512],
                                                   xd[:, dt_i * 516 + 3 + ts4 * 128:dt_i * 516 + 3 + (ts4 + 1) * 128],
                                                   wq_sb[:, db * 1024 + mh * 512:db * 1024 + (mh + 1) * 512],
                                                   start=(quar == 0 and db == 0),
                                                   stop=(quar == 3 and db == 3))
                                S.inc(ins, "te")
                    blk.tensor(body)
                def body(v, c=c):
                    S.wait(v, "te")
                    for ts4 in range(4):
                        tg = c * 4 + ts4
                        for mh in range(2):
                            S.inc(v.tensor_add(xres[:, tg * M + mh * 512:tg * M + (mh + 1) * 512],
                                               ps[ts4 * 2 + mh][:, 0:512],
                                               xres[:, tg * M + mh * 512:tg * M + (mh + 1) * 512]), "dve")
                    v.drain()
                blk.vector(body)

        L.close()
        PS.close()

    if not globals().get('SKIP_MAMBA', False):
        mamba()

    def l1ffn():
        LH = 8
        LOGK = float(np.log(1152.0))
        L = ExitStack()
        def lsb(name, shape, dt):
            return L.enter_context(nc.sbuf_tensor(name, shape, dt))
        onescol_f = lsb("f_ones", [128, 1], F32)
        onesall = lsb("f_onesall", [128, 128], F32)
        rowbank = lsb("f_rowb", [128, T], F32)
        sqscr = lsb("f_sq", [128, T], BF16)
        rfacc = lsb("f_rfac", [128, NT], F32)
        rfac8 = lsb("f_rfac8", [128, NT], F32)
        gcol = lsb("f_gcol", [128, NM], F32)
        bb = lsb("f_bb", [128, 16], F32)
        XX = lsb("f_XX", [128, 16], F32)
        score = lsb("f_score", [128, 16], F32)
        mask2d = lsb("f_mask", [128, 16], F32)
        eqm = lsb("f_eqm", [128, 16], BF16)
        etile = lsb("f_et", [128, 16], F32)
        zzrow = lsb("f_zzr", [1, 16], F32)
        tiny = lsb("f_tiny", [1, 8], F32)
        st2 = lsb("f_st2", [128, 8], F32)
        tcol = lsb("f_tcol", [128, 8], F32)
        qn2 = lsb("f_qn2", [128, 2 * NT], F32)
        q_tok = lsb("f_qtok", [128, 512], BF16)
        k_tok = lsb("f_ktok", [128, 512], BF16)
        v_tok = lsb("f_vtok", [128, 512], BF16)
        p_sb = lsb("f_psb", [128, 128], F32)
        pT_sb = lsb("f_pTsb", [128, 128], BF16)
        qfw = lsb("f_qfw", [128, 512], BF16)
        kfw = lsb("f_kfw", [128, 512], BF16)
        ogl = lsb("f_ogl", [128, 512], BF16)
        ofw = lsb("f_ofw", [128, 512], BF16)

        PS = ExitStack()
        def psb(name, shape):
            return PS.enter_context(nc.psum_tensor(name, shape, F32))
        ps_a = psb("f_pa", [128, 512])
        ps_b = psb("f_pb", [128, 512])
        ps_c = psb("f_pc", [128, 512])
        ps_d = psb("f_pd", [128, 512])
        ps_e = psb("f_pe", [128, 2048])
        ps_q = (ps_a, ps_b, ps_c, ps_d)
        wtgt = {}

        with nc.Block() as blk:
            def body(q):
                S.inc(q.dma_start(out=onescol_f[:, :], in_=dram["onescol_f"][:, :]), "dma", 16)
            blk.sync(body)
            def body(g):
                S.inc(g.dma_start(out=gcol.rearrange("p (c o) -> p c o", o=1),
                                  in_=dram["q_rt"].rearrange("(c p) o -> p c o", p=128),
                                  allow_slow_non_contiguous=True), "gdma", 16)
                wtgt["qrt"] = S.c["gdma"]
            blk.gpsimd(body)
            def body(v):
                S.inc(v.memset(zzrow[:, :], 0.0), "dve")
                S.inc(v.memset(onesall[:, :], 1.0), "dve")
                v.drain()
            blk.vector(body)
            def body(v):
                S.wait(v, "te")
                S.wait(v, "act")
                S.inc(v.reduce_max(st2[:, 0:1], s2d[:, :], axis=AX.X), "dve")
                v.drain()
            blk.vector(body)
            def body(t):
                S.wait(t, "dve")
                S.inc(t.transpose(ps_a[0:1, 0:128], st2[:, 0:1], identf[:, :]), "te")
            blk.tensor(body)
            def body(v):
                S.wait(v, "te")
                S.inc(v.reduce_max(tiny[0:1, 0:1], ps_a[0:1, 0:128], axis=AX.X, negate=True), "dve")
                v.drain()
            blk.vector(body)
            for mb in range(NM):
                def body(a, mb=mb):
                    S.wait(a, "te")
                    S.inc(a.activation(sqscr[:, :], xn_fm[:, mb * T:(mb + 1) * T], ACT.Square), "act")
                blk.scalar(body)
                def body(t, mb=mb):
                    S.wait(t, "act")
                    for q4 in range(4):
                        S.inc(t.matmul(ps_e[0:1, q4 * 512:(q4 + 1) * 512], onescol_b[:, 0:1],
                                       sqscr[:, q4 * 512:(q4 + 1) * 512],
                                       start=(mb == 0), stop=(mb == NM - 1)), "te")
                blk.tensor(body)
            def body(a):
                S.wait(a, "te")
                S.inc(a.activation(rowbank[0:1, 0:T], ps_e[0:1, 0:T], ACT.Sqrt), "act")
            blk.scalar(body)
            def body(v):
                S.wait(v, "act")
                S.inc(v.tensor_scalar_max(rowbank[0:1, 0:T], rowbank[0:1, 0:T], 1e-12), "dve")
                v.drain()
                S.inc(v.reciprocal(rowbank[0:1, 0:T], rowbank[0:1, 0:T]), "dve")
                v.drain()
                S.inc(v.tensor_scalar_mul(rowbank[0:1, 0:T], rowbank[0:1, 0:T], 32.0), "dve")
                v.drain()
            blk.vector(body)
            for grp in range(4):
                def body(t, grp=grp):
                    S.wait(t, "dve")
                    for q4 in range(4):
                        tt = grp * 4 + q4
                        S.inc(t.transpose(ps_q[q4][:, 0:1],
                                          rowbank[0:1, tt * 128:(tt + 1) * 128],
                                          identf[0:1, 0:1]), "te")
                blk.tensor(body)
                def body(v, grp=grp):
                    S.wait(v, "te")
                    for q4 in range(4):
                        tt = grp * 4 + q4
                        S.inc(v.tensor_copy(rfacc[:, tt:tt + 1], ps_q[q4][:, 0:1]), "dve")
                    v.drain()
                blk.vector(body)
            def body(v):
                S.inc(v.tensor_scalar_mul(rfac8[:, :], rfacc[:, :], 0.125), "dve")
                S.inc(v.tensor_scalar_mul(bb[:, :], s2d[:, :], -1.0), "dve")
                v.drain()
            blk.vector(body)
            def body(t):
                S.wait(t, "dve")
                S.inc(t.matmul(ps_a[:, 0:1], onesf[0:1, :], tiny[0:1, 0:1], start=True, stop=True), "te")
            blk.tensor(body)
            def body(v):
                S.wait(v, "te")
                S.inc(v.tensor_copy(tcol[:, 0:1], ps_a[:, 0:1]), "dve")
                v.drain()
            blk.vector(body)

        with nc.Block() as blk:
            for it in range(50):
                def body(v, it=it):
                    if it > 0:
                        S.wait(v, "te")
                    S.inc(v.tensor_add(XX[:, :], s2d[:, :], bb[:, :]), "dve")
                    v.drain()
                    S.inc(v.reduce_max(st2[:, 2:3], XX[:, :], axis=AX.X), "dve")
                    v.drain()
                blk.vector(body)
                def body(t):
                    S.wait(t, "dve")
                    S.inc(t.transpose(ps_c[0:1, 0:128], st2[:, 2:3], identf[:, :]), "te")
                blk.tensor(body)
                def body(v):
                    S.wait(v, "te")
                    S.inc(v.reduce_max(tiny[0:1, 2:3], ps_c[0:1, 0:128], axis=AX.X, negate=True), "dve")
                    v.drain()
                blk.vector(body)
                def body(t):
                    S.wait(t, "dve")
                    S.inc(t.matmul(ps_d[:, 0:1], onesf[0:1, :], tiny[0:1, 2:3], start=True, stop=True), "te")
                blk.tensor(body)
                def body(v):
                    S.wait(v, "te")
                    S.inc(v.tensor_copy(tcol[:, 0:1], ps_d[:, 0:1]), "dve")
                    v.drain()
                blk.vector(body)
                def body(a):
                    S.wait(a, "dve")
                    S.inc(a.activation(etile[:, :], XX[:, :], ACT.Exp, bias=tcol[:, 0:1],
                                       accum_out=st2[:, 0:1]), "act")
                blk.scalar(body)
                def body(t):
                    S.wait(t, "act")
                    S.inc(t.matmul(ps_b[0:1, 0:1], onescol_f[:, 0:1], st2[:, 0:1],
                                   start=True, stop=True), "te")
                blk.tensor(body)
                def body(a):
                    S.wait(a, "te")
                    S.inc(a.activation(tiny[0:1, 1:2], ps_b[0:1, 0:1], ACT.Ln), "act")
                blk.scalar(body)
                def body(v):
                    S.wait(v, "act")
                    S.inc(v.tensor_sub(tiny[0:1, 3:4], tiny[0:1, 2:3], tiny[0:1, 1:2]), "dve")
                    v.drain()
                    S.inc(v.tensor_scalar_add(tiny[0:1, 3:4], tiny[0:1, 3:4], LOGK), "dve")
                    v.drain()
                blk.vector(body)
                def body(t):
                    S.wait(t, "dve")
                    S.inc(t.matmul(ps_a[:, 0:1], onesf[0:1, :], tiny[0:1, 3:4], start=True, stop=True), "te")
                blk.tensor(body)
                def body(v):
                    S.wait(v, "te")
                    S.inc(v.tensor_copy(tcol[:, 1:2], ps_a[:, 0:1]), "dve")
                    v.drain()
                    S.inc(v.tensor_scalar_add(XX[:, :], s2d[:, :], tcol[:, 1:2]), "dve")
                    v.drain()
                    S.inc(v.tensor_scalar(bb[:, :], XX[:, :], -1.0, 0.0, ALU.mult, ALU.min), "dve")
                    v.drain()
                blk.vector(body)
            def body(v):
                S.inc(v.tensor_scalar_min(score[:, :], XX[:, :], 0.0), "dve")
                v.drain()
            blk.vector(body)
            def body(a):
                S.wait(a, "dve")
                S.inc(a.activation(score[:, :], score[:, :], ACT.Exp), "act")
            blk.scalar(body)

        with nc.Block() as blk:
            def body(v):
                S.wait(v, "act")
                S.inc(v.memset(tiny[0:1, 4:5], 0.0), "dve")
                S.inc(v.memset(tiny[0:1, 5:6], 1.5), "dve")
                v.drain()
            blk.vector(body)
            for it in range(40):
                def body(v):
                    S.wait(v, "te")
                    S.inc(v.tensor_add(tiny[0:1, 6:7], tiny[0:1, 4:5], tiny[0:1, 5:6]), "dve")
                    v.drain()
                    S.inc(v.tensor_scalar_mul(tiny[0:1, 6:7], tiny[0:1, 6:7], 0.5), "dve")
                    v.drain()
                blk.vector(body)
                def body(t):
                    S.wait(t, "dve")
                    S.inc(t.matmul(ps_a[:, 0:1], onesf[0:1, :], tiny[0:1, 6:7], start=True, stop=True), "te")
                blk.tensor(body)
                def body(v):
                    S.wait(v, "te")
                    S.inc(v.tensor_copy(tcol[:, 2:3], ps_a[:, 0:1]), "dve")
                    v.drain()
                    S.inc(v.tensor_scalar(etile[:, :], score[:, :], tcol[:, 2:3], None,
                                          ALU.is_gt), "dve")
                    v.drain()
                    S.inc(v.reduce_sum(st2[:, 1:2], etile[:, :], axis=AX.X), "dve")
                    v.drain()
                blk.vector(body)
                def body(t):
                    S.wait(t, "dve")
                    S.inc(t.matmul(ps_b[0:1, 0:1], onescol_f[:, 0:1], st2[:, 1:2],
                                   start=True, stop=True), "te")
                blk.tensor(body)
                def body(v):
                    S.wait(v, "te")
                    S.inc(v.tensor_scalar(tiny[0:1, 7:8], ps_b[0:1, 0:1], 1024.0, None, ALU.is_ge), "dve")
                    v.drain()
                    S.inc(v.tensor_sub(st2[0:1, 2:3], tiny[0:1, 6:7], tiny[0:1, 4:5]), "dve")
                    S.inc(v.tensor_sub(st2[0:1, 3:4], tiny[0:1, 6:7], tiny[0:1, 5:6]), "dve")
                    v.drain()
                    S.inc(v.scalar_tensor_tensor(tiny[0:1, 4:5], st2[0:1, 2:3], tiny[0:1, 7:8],
                                                 tiny[0:1, 4:5], ALU.mult, ALU.add), "dve")
                    v.drain()
                    S.inc(v.tensor_scalar(st2[0:1, 4:5], tiny[0:1, 7:8], -1.0, 1.0, ALU.mult, ALU.add), "dve")
                    v.drain()
                    S.inc(v.scalar_tensor_tensor(tiny[0:1, 5:6], st2[0:1, 3:4], st2[0:1, 4:5],
                                                 tiny[0:1, 5:6], ALU.mult, ALU.add), "dve")
                    v.drain()
                blk.vector(body)
            def body(t):
                S.wait(t, "dve")
                S.inc(t.matmul(ps_a[:, 0:1], onesf[0:1, :], tiny[0:1, 5:6], start=True, stop=True), "te")
                S.inc(t.matmul(ps_b[:, 0:1], onesf[0:1, :], tiny[0:1, 4:5], start=True, stop=True), "te")
            blk.tensor(body)
            def body(v):
                S.wait(v, "te")
                S.inc(v.tensor_copy(tcol[:, 3:4], ps_a[:, 0:1]), "dve")
                S.inc(v.tensor_copy(tcol[:, 4:5], ps_b[:, 0:1]), "dve")
                v.drain()
                S.inc(v.tensor_scalar(mask2d[:, :], score[:, :], tcol[:, 3:4], None, ALU.is_gt), "dve")
                v.drain()
                S.inc(v.reduce_sum(st2[:, 5:6], mask2d[:, :], axis=AX.X), "dve")
                v.drain()
                S.inc(v.tensor_scalar(etile[:, :], score[:, :], tcol[:, 4:5], None, ALU.is_gt), "dve")
                S.inc(v.tensor_scalar(XX[:, :], score[:, :], tcol[:, 3:4], None, ALU.is_le), "dve")
                v.drain()
                S.inc(v.tensor_mul(eqm[:, :], etile[:, :], XX[:, :]), "dve")
                v.drain()
            blk.vector(body)
            def body(t):
                S.wait(t, "dve")
                S.inc(t.matmul(ps_c[0:1, 0:1], onescol_f[:, 0:1], st2[:, 5:6], start=True, stop=True), "te")
                S.inc(t.matmul(ps_d[:, 0:16], trilT_sb[:, 0:128], eqm[:, :], start=True, stop=False), "te")
                S.inc(t.matmul(ps_e[0:1, 0:16], onescol_b[:, 0:1], eqm[:, :], start=True, stop=True), "te")
            blk.tensor(body)
            def body(v):
                S.wait(v, "te")
                S.inc(v.tensor_scalar(tiny[0:1, 1:2], ps_c[0:1, 0:1], -1.0, 1024.0, ALU.mult, ALU.add), "dve")
                v.drain()
                S.inc(v.tensor_tensor_scan(etile[0:1, 0:16], ps_e[0:1, 0:16], zzrow[0:1, 0:16],
                                           0.0, ALU.add, ALU.add), "dve")
                v.drain()
                S.inc(v.tensor_sub(etile[0:1, 0:16], etile[0:1, 0:16], ps_e[0:1, 0:16]), "dve")
                v.drain()
            blk.vector(body)
            def body(t):
                S.wait(t, "dve")
                S.inc(t.matmul(ps_a[:, 0:1], onesf[0:1, :], tiny[0:1, 1:2], start=True, stop=True), "te")
                S.inc(t.matmul(ps_d[:, 0:16], onesf[0:1, :], etile[0:1, 0:16], start=False, stop=True), "te")
            blk.tensor(body)
            def body(v):
                S.wait(v, "te")
                S.inc(v.tensor_copy(tcol[:, 4:5], ps_a[:, 0:1]), "dve")
                v.drain()
                S.inc(v.tensor_scalar(XX[:, :], ps_d[:, 0:16], tcol[:, 4:5], None, ALU.is_le), "dve")
                v.drain()
                S.inc(v.tensor_mul(XX[:, :], XX[:, :], eqm[:, :]), "dve")
                v.drain()
                S.inc(v.tensor_add(mask2d[:, :], mask2d[:, :], XX[:, :]), "dve")
                v.drain()
            blk.vector(body)
            if "dbg_mask" in dram:
                def body(g):
                    S.wait(g, "dve")
                    S.inc(g.dma_start(out=dram["dbg_mask"][:, :], in_=mask2d[:, :]), "gdma", 16)
                    S.inc(g.dma_start(out=dram["dbg_score"][:, :], in_=score[:, :]), "gdma", 16)
                    S.inc(g.dma_start(out=dram["dbg_s2d"][:, :], in_=s2d[:, :]), "gdma", 16)
                    S.inc(g.dma_start(out=dram["dbg_tiny"][:, :], in_=tiny[:, :]), "gdma", 16)
                    S.wait(g, "gdma")
                blk.gpsimd(body)

        with ExitStack() as WL:
            wql = WL.enter_context(nc.sbuf_tensor("f_wql", [128, NM * 512], BF16))
            wkvl = WL.enter_context(nc.sbuf_tensor("f_wkvl", [128, NM * 1024], BF16))
            wol = WL.enter_context(nc.sbuf_tensor("f_wol", [128, 4 * M], BF16))
            with nc.Block() as blk:
                def body(g):
                    S.inc(g.dma_start(out=wql.rearrange("p (c h) -> p c h", h=512),
                                      in_=dram["wq_l"].rearrange("(c p) h -> p c h", p=128)), "gdma", 16)
                    S.inc(g.dma_start(out=wkvl.rearrange("p (c h) -> p c h", h=1024),
                                      in_=dram["wkv_l"].rearrange("(c p) h -> p c h", p=128)), "gdma", 16)
                    S.inc(g.dma_start(out=wol.rearrange("p (c h) -> p c h", h=1024),
                                      in_=dram["wo_l"].rearrange("(c p) h -> p c h", p=128)), "gdma", 16)
                    S.inc(g.dma_start(out=gcol.rearrange("p (c o) -> p c o", o=1),
                                      in_=dram["gam_l"].rearrange("(c p) o -> p c o", p=128),
                                      allow_slow_non_contiguous=True), "gdma", 16)
                    wtgt["wl"] = S.c["gdma"]
                blk.gpsimd(body)
                def body(v):
                    S.wait(v, "gdma", wtgt["wl"])
                    for mc in range(NM):
                        S.inc(v.tensor_scalar_mul(wql[:, mc * 512:(mc + 1) * 512],
                                                  wql[:, mc * 512:(mc + 1) * 512], gcol[:, mc:mc + 1]), "dve")
                        S.inc(v.tensor_scalar_mul(wkvl[:, mc * M:(mc + 1) * M],
                                                  wkvl[:, mc * M:(mc + 1) * M], gcol[:, mc:mc + 1]), "dve")
                    v.drain()
                blk.vector(body)
                for tt in range(NT):
                    def body(t, tt=tt):
                        S.wait(t, "dve")
                        S.wait(t, "act")
                        for mc in range(NM):
                            ins = t.matmul(ps_a[:, 0:512],
                                           xn_fm[:, mc * T + tt * 128:mc * T + (tt + 1) * 128],
                                           wql[:, mc * 512:(mc + 1) * 512],
                                           start=(mc == 0), stop=(mc == NM - 1))
                        S.inc(ins, "te")
                        for half in range(2):
                            for mc in range(NM):
                                ins = t.matmul((ps_b, ps_c)[half][:, 0:512],
                                               xn_fm[:, mc * T + tt * 128:mc * T + (tt + 1) * 128],
                                               wkvl[:, mc * M + half * 512:mc * M + (half + 1) * 512],
                                               start=(mc == 0), stop=(mc == NM - 1))
                            S.inc(ins, "te")
                    blk.tensor(body)
                    def body(a, tt=tt):
                        S.wait(a, "te")
                        S.wait(a, "dma")
                        S.inc(a.activation(q_tok[:, :], ps_a[:, 0:512], ACT.Copy,
                                           scale=rfac8[:, tt:tt + 1]), "act")
                        S.inc(a.activation(k_tok[:, :], ps_b[:, 0:512], ACT.Copy,
                                           scale=rfacc[:, tt:tt + 1]), "act")
                        S.inc(a.activation(v_tok[:, :], ps_c[:, 0:512], ACT.Copy,
                                           scale=rfacc[:, tt:tt + 1]), "act")
                    blk.scalar(body)
                    def body(q, tt=tt):
                        S.wait(q, "act")
                        S.wait(q, "te")
                        for db in range(4):
                            S.inc(q.dma_start_transpose(qfw[:, db * 128:(db + 1) * 128],
                                                        q_tok[:, db * 128:(db + 1) * 128]), "dma", 16)
                            S.inc(q.dma_start_transpose(kfw[:, db * 128:(db + 1) * 128],
                                                        k_tok[:, db * 128:(db + 1) * 128]), "dma", 16)
                    blk.sync(body)
                    for h in range(LH):
                        po = (h % 2) * 64
                        sg = h // 2
                        def body(t, h=h, po=po, sg=sg):
                            if h == 0:
                                S.wait(t, "dma")
                            S.wait(t, "dve")
                            S.inc(t.matmul(ps_d[:, 0:128],
                                           qfw[po:po + 64, sg * 128:(sg + 1) * 128],
                                           kfw[po:po + 64, sg * 128:(sg + 1) * 128],
                                           start=True, stop=True), "te")
                        blk.tensor(body)
                        def body(v, h=h):
                            S.wait(v, "te")
                            S.inc(v.reduce_max(st2[:, 0:1], ps_d[:, 0:128], axis=AX.X, negate=True), "dve")
                            v.drain()
                        blk.vector(body)
                        def body(a, h=h):
                            S.wait(a, "dve")
                            S.inc(a.activation(p_sb[:, :], ps_d[:, 0:128], ACT.Exp,
                                               bias=st2[:, 0:1], accum_out=st2[:, 1:2]), "act")
                        blk.scalar(body)
                        def body(v, h=h):
                            S.wait(v, "act")
                            S.inc(v.reciprocal(st2[:, 2:3], st2[:, 1:2]), "dve")
                            v.drain()
                        blk.vector(body)
                        def body(t, h=h):
                            S.wait(t, "dve")
                            S.inc(t.transpose(ps_a[:, 0:128], p_sb[:, :], identf[:, :]), "te")
                        blk.tensor(body)
                        def body(a, h=h):
                            S.wait(a, "te")
                            S.inc(a.activation(pT_sb[:, :], ps_a[:, 0:128], ACT.Copy), "act")
                        blk.scalar(body)
                        def body(t, h=h):
                            S.wait(t, "act")
                            S.inc(t.matmul(ps_b[:, 0:64], pT_sb[:, :],
                                           v_tok[:, h * 64:(h + 1) * 64], start=True, stop=True), "te")
                        blk.tensor(body)
                        def body(v, h=h):
                            S.wait(v, "te")
                            S.inc(v.tensor_scalar_mul(ogl[:, h * 64:(h + 1) * 64],
                                                      ps_b[:, 0:64], st2[:, 2:3]), "dve")
                            v.drain()
                        blk.vector(body)
                    def body(q, tt=tt):
                        S.wait(q, "dve")
                        for db in range(4):
                            S.inc(q.dma_start_transpose(ofw[:, db * 128:(db + 1) * 128],
                                                        ogl[:, db * 128:(db + 1) * 128]), "dma", 16)
                    blk.sync(body)
                    def body(t, tt=tt):
                        S.wait(t, "dma")
                        for mh in range(2):
                            for sg4 in range(4):
                                ins = t.matmul((ps_c, ps_d)[mh][:, 0:512],
                                               ofw[:, sg4 * 128:(sg4 + 1) * 128],
                                               wol[:, sg4 * M + mh * 512:sg4 * M + (mh + 1) * 512],
                                               start=(sg4 == 0), stop=(sg4 == 3))
                            S.inc(ins, "te")
                    blk.tensor(body)
                    def body(v, tt=tt):
                        S.wait(v, "te")
                        for mh in range(2):
                            S.inc(v.tensor_add(xres[:, tt * M + mh * 512:tt * M + (mh + 1) * 512],
                                               (ps_c, ps_d)[mh][:, 0:512],
                                               xres[:, tt * M + mh * 512:tt * M + (mh + 1) * 512]), "dve")
                        v.drain()
                    blk.vector(body)

        qfh = lsb("f_qfh", [128, 4 * T], BF16)
        kfh = lsb("f_kfh", [128, 4 * T], BF16)
        v_h = lsb("f_vh", [128, NT * 512], BF16)

        with ExitStack() as WH:
            wqh = WH.enter_context(nc.sbuf_tensor("f_wqh", [128, NM * 512], BF16))
            wkvh = WH.enter_context(nc.sbuf_tensor("f_wkvh", [128, NM * 1024], BF16))
            with nc.Block() as blk:
                def body(g):
                    S.inc(g.dma_start(out=wqh.rearrange("p (c h) -> p c h", h=512),
                                      in_=dram["wq_h"].rearrange("(c p) h -> p c h", p=128)), "gdma", 16)
                    S.inc(g.dma_start(out=wkvh.rearrange("p (c h) -> p c h", h=1024),
                                      in_=dram["wkv_h"].rearrange("(c p) h -> p c h", p=128)), "gdma", 16)
                    S.inc(g.dma_start(out=gcol.rearrange("p (c o) -> p c o", o=1),
                                      in_=dram["gam_h"].rearrange("(c p) o -> p c o", p=128),
                                      allow_slow_non_contiguous=True), "gdma", 16)
                    wtgt["wh"] = S.c["gdma"]
                blk.gpsimd(body)
                def body(v):
                    S.wait(v, "gdma", wtgt["wh"])
                    S.wait(v, "act")
                    S.wait(v, "te")
                    for mc in range(NM):
                        S.inc(v.tensor_scalar_mul(wqh[:, mc * 512:(mc + 1) * 512],
                                                  wqh[:, mc * 512:(mc + 1) * 512], gcol[:, mc:mc + 1]), "dve")
                        S.inc(v.tensor_scalar_mul(wkvh[:, mc * M:(mc + 1) * M],
                                                  wkvh[:, mc * M:(mc + 1) * M], gcol[:, mc:mc + 1]), "dve")
                    v.drain()
                blk.vector(body)
                for tt in range(NT):
                    def body(t, tt=tt):
                        S.wait(t, "dve")
                        S.wait(t, "act")
                        if tt > 0:
                            S.wait(t, "dma")
                        for mc in range(NM):
                            ins = t.matmul(ps_a[:, 0:512],
                                           xn_fm[:, mc * T + tt * 128:mc * T + (tt + 1) * 128],
                                           wqh[:, mc * 512:(mc + 1) * 512],
                                           start=(mc == 0), stop=(mc == NM - 1))
                        S.inc(ins, "te")
                        for half in range(2):
                            for mc in range(NM):
                                ins = t.matmul((ps_b, ps_c)[half][:, 0:512],
                                               xn_fm[:, mc * T + tt * 128:mc * T + (tt + 1) * 128],
                                               wkvh[:, mc * M + half * 512:mc * M + (half + 1) * 512],
                                               start=(mc == 0), stop=(mc == NM - 1))
                            S.inc(ins, "te")
                    blk.tensor(body)
                    def body(a, tt=tt):
                        S.wait(a, "te")
                        S.inc(a.activation(q_tok[:, :], ps_a[:, 0:512], ACT.Copy,
                                           scale=rfac8[:, tt:tt + 1]), "act")
                        S.inc(a.activation(k_tok[:, :], ps_b[:, 0:512], ACT.Copy,
                                           scale=rfacc[:, tt:tt + 1]), "act")
                        S.inc(a.activation(v_h[:, tt * 512:(tt + 1) * 512], ps_c[:, 0:512], ACT.Copy,
                                           scale=rfacc[:, tt:tt + 1]), "act")
                        a.drain()
                        S.inc(a.activation(sqscr[:, 0:512], q_tok[:, :], ACT.Square,
                                           accum_out=qn2[:, tt:tt + 1]), "act")
                        S.inc(a.activation(sqscr[:, 512:1024], k_tok[:, :], ACT.Square,
                                           accum_out=qn2[:, NT + tt:NT + tt + 1]), "act")
                    blk.scalar(body)
                    def body(q, tt=tt):
                        S.wait(q, "act")
                        for db in range(4):
                            S.inc(q.dma_start_transpose(
                                qfh[:, db * T + tt * 128:db * T + (tt + 1) * 128],
                                q_tok[:, db * 128:(db + 1) * 128]), "dma", 16)
                            S.inc(q.dma_start_transpose(
                                kfh[:, db * T + tt * 128:db * T + (tt + 1) * 128],
                                k_tok[:, db * 128:(db + 1) * 128]), "dma", 16)
                    blk.sync(body)

        pT_h = lsb("f_pTh", [128, T], BF16)
        o_sbh = lsb("f_osbh", [128, T], BF16)
        o_fmh = lsb("f_ofmh", [128, 4 * T], BF16)
        if True:
            with nc.Block() as blk:
                def body(v):
                    S.wait(v, "act")
                    S.inc(v.reduce_max(st2[:, 0:1], qn2[:, 0:NT], axis=AX.X), "dve")
                    S.inc(v.reduce_max(st2[:, 1:2], qn2[:, NT:2 * NT], axis=AX.X), "dve")
                    v.drain()
                blk.vector(body)
                def body(t):
                    S.wait(t, "dve")
                    S.inc(t.transpose(ps_a[0:1, 0:128], st2[:, 0:1], identf[:, :]), "te")
                    S.inc(t.transpose(ps_b[0:1, 0:128], st2[:, 1:2], identf[:, :]), "te")
                blk.tensor(body)
                def body(v):
                    S.wait(v, "te")
                    S.inc(v.reduce_max(tiny[0:1, 2:3], ps_a[0:1, 0:128], axis=AX.X), "dve")
                    S.inc(v.reduce_max(tiny[0:1, 3:4], ps_b[0:1, 0:128], axis=AX.X), "dve")
                    v.drain()
                blk.vector(body)
                def body(a):
                    S.wait(a, "dve")
                    S.inc(a.activation(tiny[0:1, 2:3], tiny[0:1, 2:3], ACT.Sqrt), "act")
                    a.drain()
                    S.inc(a.activation(tiny[0:1, 3:4], tiny[0:1, 3:4], ACT.Sqrt), "act")
                blk.scalar(body)
                def body(v):
                    S.wait(v, "act")
                    S.inc(v.tensor_mul(tiny[0:1, 2:3], tiny[0:1, 2:3], tiny[0:1, 3:4]), "dve")
                    v.drain()
                    S.inc(v.tensor_scalar_mul(tiny[0:1, 2:3], tiny[0:1, 2:3], -0.125), "dve")
                    v.drain()
                blk.vector(body)
                def body(t):
                    S.wait(t, "dve")
                    S.inc(t.matmul(ps_a[:, 0:1], onesf[0:1, :], tiny[0:1, 2:3], start=True, stop=True), "te")
                blk.tensor(body)
                def body(v):
                    S.wait(v, "te")
                    S.inc(v.tensor_copy(tcol[:, 3:4], ps_a[:, 0:1]), "dve")
                    v.drain()
                blk.vector(body)

            with nc.Block() as blk:
                for h in range(LH):
                    po = (h % 2) * 64
                    sg = h // 2
                    srow = 64 - po
                    for jt in range(NT):
                        def body(t, h=h, po=po, sg=sg, jt=jt):
                            S.wait(t, "act")
                            S.wait(t, "dve")
                            if h == 0 and jt == 0:
                                S.wait(t, "dma")
                            for ib in range(4):
                                S.inc(t.matmul(ps_e[:, ib * 512:(ib + 1) * 512],
                                               kfh[po:po + 64, sg * T + jt * 128:sg * T + (jt + 1) * 128],
                                               qfh[po:po + 64, sg * T + ib * 512:sg * T + (ib + 1) * 512],
                                               start=True, stop=True), "te")
                        blk.tensor(body)
                        def body(a, h=h, jt=jt):
                            S.wait(a, "te")
                            S.inc(a.activation(pT_h[:, :], ps_e[:, 0:2048], ACT.Exp,
                                               bias=tcol[:, 3:4]), "act")
                        blk.scalar(body)
                        def body(t, h=h, po=po, jt=jt, srow=srow):
                            S.wait(t, "act")
                            for ib in range(4):
                                S.inc(t.matmul(ps_q[ib][po:po + 64, 0:512],
                                               v_h[:, jt * 512 + h * 64:jt * 512 + (h + 1) * 64],
                                               pT_h[:, ib * 512:(ib + 1) * 512],
                                               start=(jt == 0), stop=(jt == NT - 1)), "te")
                                S.inc(t.matmul(ps_q[ib][srow:srow + 1, 0:512],
                                               onescol_b[:, 0:1],
                                               pT_h[:, ib * 512:(ib + 1) * 512],
                                               start=(jt == 0), stop=(jt == NT - 1)), "te")
                        blk.tensor(body)
                    def body(v, h=h, srow=srow):
                        S.wait(v, "te")
                        for ib in range(4):
                            S.inc(v.reciprocal(rowbank[srow:srow + 1, ib * 512:(ib + 1) * 512],
                                               ps_q[ib][srow:srow + 1, 0:512]), "dve")
                        v.drain()
                    blk.vector(body)
                    def body(t, h=h, po=po, srow=srow):
                        S.wait(t, "dve")
                        for ib in range(4):
                            S.inc(t.matmul(ps_e[po:po + 64, ib * 512:(ib + 1) * 512],
                                           onesall[srow:srow + 1, 0:64],
                                           rowbank[srow:srow + 1, ib * 512:(ib + 1) * 512],
                                           start=True, stop=True), "te")
                    blk.tensor(body)
                    def body(a, h=h, po=po):
                        S.wait(a, "te")
                        for ib in range(4):
                            S.inc(a.activation(o_sbh[po:po + 64, ib * 512:(ib + 1) * 512],
                                               ps_q[ib][po:po + 64, 0:512], ACT.Copy), "act")
                    blk.scalar(body)
                    def body(v, h=h, po=po, sg=sg):
                        S.wait(v, "act")
                        S.inc(v.tensor_mul(o_fmh[po:po + 64, sg * T:(sg + 1) * T],
                                           o_sbh[po:po + 64, :],
                                           ps_e[po:po + 64, 0:2048]), "dve")
                        v.drain()
                    blk.vector(body)

            with ExitStack() as WO:
                woh = WO.enter_context(nc.sbuf_tensor("f_woh", [128, 4 * M], BF16))
                with nc.Block() as blk:
                    def body(g):
                        S.inc(g.dma_start(out=woh.rearrange("p (c h) -> p c h", h=1024),
                                          in_=dram["wo_h"].rearrange("(c p) h -> p c h", p=128)), "gdma", 16)
                        wtgt["woh"] = S.c["gdma"]
                    blk.gpsimd(body)
                    for tt in range(NT):
                        def body(t, tt=tt):
                            S.wait(t, "gdma", wtgt["woh"])
                            if tt == 0:
                                S.wait(t, "dve")
                            else:
                                S.wait(t, "dve")
                            for mh in range(2):
                                for sg4 in range(4):
                                    ins = t.matmul((ps_a, ps_b)[mh][:, 0:512],
                                                   o_fmh[:, sg4 * T + tt * 128:sg4 * T + (tt + 1) * 128],
                                                   woh[:, sg4 * M + mh * 512:sg4 * M + (mh + 1) * 512],
                                                   start=(sg4 == 0), stop=(sg4 == 3))
                                S.inc(ins, "te")
                        blk.tensor(body)
                        def body(v, tt=tt):
                            S.wait(v, "te")
                            for mh in range(2):
                                S.inc(v.scalar_tensor_tensor(
                                    xres[:, tt * M + mh * 512:tt * M + (mh + 1) * 512],
                                    (ps_a, ps_b)[mh][:, 0:512],
                                    mask2d[:, tt:tt + 1],
                                    xres[:, tt * M + mh * 512:tt * M + (mh + 1) * 512],
                                    ALU.mult, ALU.add), "dve")
                            v.drain()
                        blk.vector(body)

        L.close()
        PS.close()

    if not globals().get('SKIP_L1FFN', False):
        l1ffn()

    with nc.Block() as blk:
        def body(q):
            S.wait(q, "dve")
            S.inc(q.dma_start(out=dram["out"].rearrange("(t p) m -> p t m", p=128), in_=xres.rearrange("p (t m) -> p t m", t=NT)), "dma", 16)
            S.wait(q, "dma")
        blk.sync(body)

    stack.close()
    return nc, dram


def _reorder_kvh(wkv):
    wt = wkv.T
    idx_k = np.concatenate([np.arange(h * 128, h * 128 + 64) for h in range(8)])
    idx_v = np.concatenate([np.arange(h * 128 + 64, h * 128 + 128) for h in range(8)])
    return np.ascontiguousarray(wt[:, np.concatenate([idx_k, idx_v])])


def host_inputs(inputs, core):
    x = np.ascontiguousarray(inputs["L0_attn_x"][core])
    d = {
        "x": x,
        "Wq": inputs["L0_attn_Wq"], "Wk": inputs["L0_attn_Wk"], "Wv": inputs["L0_attn_Wv"],
        "Wg": inputs["L0_attn_Wg"], "Wo": inputs["L0_attn_Wo"], "Wgt": inputs["L0_attn_Wgt"],
        "gate_w": inputs["L0_ffn_gate_w"], "w1": inputs["L0_ffn_w1"],
        "w2": inputs["L0_ffn_w2"], "w3": inputs["L0_ffn_w3"],
        "W_inT": np.ascontiguousarray(np.asarray(inputs["L1_attn_W_in"]).T),
        "conv_w": np.asarray(inputs["L1_attn_conv_w"]).reshape(2048, 4),
        "conv_b": np.asarray(inputs["L1_attn_conv_b"]).reshape(2048, 1),
        "W_xT": np.ascontiguousarray(np.asarray(inputs["L1_attn_W_x"]).T),
        "W_dt": np.asarray(inputs["L1_attn_W_dt"]).reshape(2048, 1),
        "b_dt": np.asarray(inputs["L1_attn_b_dt"]).reshape(2048, 1),
        "A_log": inputs["L1_attn_A_log"],
        "D_param": np.asarray(inputs["L1_attn_D_param"]).reshape(2048, 1),
        "W_outT": np.ascontiguousarray(np.asarray(inputs["L1_attn_W_out"]).T),
        "gam_l": np.asarray(inputs["L1_ffn_gamma_light"]).reshape(M, 1),
        "wq_l": inputs["L1_ffn_wq_light"],
        "wkv_l": inputs["L1_ffn_wkv_light"],
        "wo_l": inputs["L1_ffn_wo_light"],
        "q_rt": np.asarray(inputs["L1_ffn_q_routing_token"]).reshape(M, 1),
        "q_rt_row": np.asarray(inputs["L1_ffn_q_routing_token"]).reshape(1, M),
        "gam_h": np.asarray(inputs["L1_ffn_gamma_heavy"]).reshape(M, 1),
        "wq_h": inputs["L1_ffn_wq_heavy"],
        "wkv_h": _reorder_kvh(np.asarray(inputs["L1_ffn_wkv_heavy"])),
        "wo_h": inputs["L1_ffn_wo_heavy"],
        "onescol_f": np.ones((128, 1), dtype=np.float32),
        "ident": np.eye(128, dtype=np.float32),
        "selh": np.kron(np.eye(16, dtype=np.float32), np.ones((1, 128), np.float32)).reshape(16, 16 * 128),
        "ones": np.ones((1, 128), dtype=np.float32),
        "onescol": np.ones((128, 1), dtype=np.float32),
        "trilT": np.triu(np.ones((CH, CH), dtype=np.float32)),
    }
    return {k: np.ascontiguousarray(np.asarray(v, dtype=np.float32)) for k, v in d.items()}


def kernel(**inputs):
    nc, _ = build(debug=False)
    in_maps = [host_inputs(inputs, core) for core in range(8)]
    res = run_bass_kernel_spmd(nc, in_maps, core_ids=list(range(8)))
    out = np.stack([r["out"] for r in res.results], axis=0)
    return out.astype(np.float32)


# version 02
# speedup vs baseline: 1.2248x; 1.2248x over previous
import numpy as np
import concourse.bass as bass
import concourse.mybir as mybir
from concourse.bass_utils import run_bass_kernel_spmd
from contextlib import ExitStack

F32 = mybir.dt.float32
BF16 = mybir.dt.bfloat16
ALU = mybir.AluOpType
ACT = mybir.ActivationFunctionType
AX = mybir.AxisListType

T = 2048
M = 1024
NT = T // 128
NM = M // 128
NH = 16
DH = 64
CH = 256
NCH = T // CH
GEPS = 1e-6
LNEPS = 1e-5
LOG_SCALE = float(np.log(DH ** -0.5))


class Sems:
    def __init__(self, nc, stack, names):
        self.h = {n: stack.enter_context(nc.semaphore(n)) for n in names}
        self.c = {n: 0 for n in names}

    def inc(self, instr, name, k=1):
        self.c[name] += k
        instr.then_inc(self.h[name], k)
        return instr

    def wait(self, eng, name, val=None):
        eng.wait_ge(self.h[name], self.c[name] if val is None else val)


SKIP_GLA = False


def build(debug=False):
    nc = bass.Bass("TRN2", target_bir_lowering=False)
    stack = ExitStack()
    dram = {}

    def din(name, shape, dt=F32):
        dram[name] = nc.dram_tensor(name, shape, dt, kind="ExternalInput").ap()

    def dout(name, shape, dt=F32):
        dram[name] = nc.dram_tensor(name, shape, dt, kind="ExternalOutput").ap()

    din("x", [T, M])
    for w in ("Wq", "Wk", "Wv", "Wg", "Wo"):
        din(w, [M, M])
    din("Wgt", [M, NH])
    din("gate_w", [M, 8])
    din("w1", [8, M, 512])
    din("w2", [8, 512, M])
    din("w3", [8, M, 512])
    din("W_inT", [M, 4096])
    din("conv_w", [2048, 4])
    din("conv_b", [2048, 1])
    din("W_xT", [2048, 33])
    din("W_dt", [2048, 1])
    din("b_dt", [2048, 1])
    din("A_log", [2048, 16])
    din("D_param", [2048, 1])
    din("W_outT", [2048, M])
    din("gam_l", [M, 1])
    din("wq_l", [M, 512])
    din("wkv_l", [M, 1024])
    din("wo_l", [512, M])
    din("q_rt", [M, 1])
    din("q_rt_row", [1, M])
    din("gam_h", [M, 1])
    din("wq_h", [M, 512])
    din("wkv_h", [M, 1024])
    din("wo_h", [512, M])
    din("onescol_f", [128, 1])
    din("ident", [128, 128])
    din("ones", [1, 128])
    din("trilT", [CH, CH])
    din("selh", [NH, NH * 128])
    din("onescol", [128, 1])
    dout("out", [T, M])
    if debug:
        dout("dbg_xn", [T, M])
        for nm, shp in (("dbg_q", [128, 2048]), ("dbg_k", [128, 2048]), ("dbg_v", [128, 2048]),
                        ("dbg_bc", [NH, T]), ("dbg_bcq", [NH, CH]), ("dbg_bkneg", [NH, CH]),
                        ("dbg_mtab", [128, 160]), ("dbg_silug", [128, 2048]),
                        ("dbg_dmask", [128, 512]), ("dbg_msT", [128, 512]),
                        ("dbg_otok", [128, 2048]), ("dbg_ktm", [128, 2048]), ("dbg_vtm", [128, 2048]), ("dbg_xnfm", [128, NM * T]), ("dbg_stt", [128, 8 * NT]),
                        ("dbg_og", [128, 2048]), ("dbg_ofm", [128, 2048]), ("dbg_xres0", [128, 2048]),
                        ("dbg_m_xd", [128, 16 * 516]), ("dbg_m_u", [128, 16 * 512]),
                        ("dbg_m_ssmt", [33, 512]), ("dbg_m_bca", [128, 32 * 512]),
                        ("dbg_m_lns", [128, 512]), ("dbg_m_h", [128, 512]),
                        ("dbg_m_dA", [128, 512]), ("dbg_m_bP", [128, 512]), ("dbg_m_wdd", [128, 512]),
                        ("dbg_m_zs", [128, 16 * 512]), ("dbg_mask", [128, 16]), ("dbg_score", [128, 16]),
                        ("dbg_s2d", [128, 16]), ("dbg_tiny", [1, 8]), ("dbg_tokrep", [128, M]), ("dbg_xnt", [128, M])):
            dout(nm, shp)

    S = Sems(nc, stack, ["te", "act", "dve", "dma", "gdma", "gp"])

    def sb(name, shape, dt):
        return stack.enter_context(nc.sbuf_tensor(name, shape, dt))

    xres = sb("xres", [128, NT * M], F32)
    xn_fm = sb("xn_fm", [128, NM * T], BF16)
    identf = sb("identf", [128, 128], F32)
    identb = sb("identb", [128, 128], BF16)
    onesf = sb("onesf", [1, 128], F32)
    onesb = sb("onesb", [1, 128], BF16)
    onescol_b = sb("onescol_b", [128, 1], BF16)
    consts = sb("consts", [128, 4], F32)
    trilT_sb = sb("trilT_sb", [128, 2 * CH], BF16)
    s2d = sb("s2d_rt", [128, 16], F32)

    with nc.Block() as blk:
        def body(q):
            S.inc(q.dma_start(out=xres.rearrange("p (t m) -> p t m", t=NT), in_=dram["x"].rearrange("(t p) m -> p t m", p=128)), "dma", 16)
            S.inc(q.dma_start(out=identf[:, :], in_=dram["ident"][:, :]), "dma", 16)
            S.inc(q.dma_start(out=onesf[:, :], in_=dram["ones"][:, :]), "dma", 16)
        blk.sync(body)

        def body(v):
            S.inc(v.memset(consts[:, 0:1], GEPS), "dve")
            S.inc(v.memset(consts[:, 1:2], float(np.log(32.0))), "dve")
            S.inc(v.memset(consts[:, 2:3], 1.0), "dve")
            S.inc(v.memset(consts[:, 3:4], float(np.log(0.125))), "dve")
            v.drain()
        blk.vector(body)

        def body(g):
            S.inc(g.dma_start(out=identb[:, :], in_=dram["ident"][:, :]), "gdma", 16)
            S.inc(g.dma_start(out=onesb[:, :], in_=dram["ones"][:, :]), "gdma", 16)
            S.inc(g.dma_start(out=onescol_b[:, :], in_=dram["onescol"][:, :]), "gdma", 16)
            S.inc(g.dma_start(out=trilT_sb[:, 0:CH], in_=dram["trilT"][0:128, :]), "gdma", 16)
            S.inc(g.dma_start(out=trilT_sb[:, CH:2 * CH], in_=dram["trilT"][128:256, :]), "gdma", 16)
        blk.gpsimd(body)

    def layernorm(tag, dbg=None):
        with ExitStack() as L:
            xn_tm = L.enter_context(nc.sbuf_tensor(f"xntm_{tag}", [128, NT * M], BF16))
            scr = L.enter_context(nc.sbuf_tensor(f"lnscr_{tag}", [128, M], F32))
            stt = L.enter_context(nc.sbuf_tensor(f"lnst_{tag}", [128, 8 * NT], F32))
            with nc.Block() as blk:
                def body(a):
                    S.wait(a, "dma")
                    for tt in range(NT):
                        c = 8 * tt
                        S.inc(a.activation(scr[:, :], xres[:, tt * M:(tt + 1) * M], ACT.Square,
                                           accum_out=stt[:, c + 0:c + 1]), "act")
                blk.scalar(body)

                def body(v):
                    S.wait(v, "act")
                    for tt in range(NT):
                        c = 8 * tt
                        S.inc(v.reduce_sum(stt[:, c + 6:c + 7], xres[:, tt * M:(tt + 1) * M], axis=AX.X), "dve")
                    v.drain()
                    for tt in range(NT):
                        c = 8 * tt
                        S.inc(v.tensor_scalar_mul(stt[:, c + 1:c + 2], stt[:, c + 6:c + 7], -1.0 / M), "dve")
                    v.drain()
                    for tt in range(NT):
                        c = 8 * tt
                        S.inc(v.tensor_mul(stt[:, c + 2:c + 3], stt[:, c + 1:c + 2], stt[:, c + 1:c + 2]), "dve")
                    v.drain()
                    for tt in range(NT):
                        c = 8 * tt
                        S.inc(v.scalar_tensor_tensor(stt[:, c + 3:c + 4], stt[:, c + 0:c + 1], 1.0 / M,
                                                     stt[:, c + 2:c + 3], ALU.mult, ALU.subtract), "dve")
                    v.drain()
                    for tt in range(NT):
                        c = 8 * tt
                        S.inc(v.tensor_scalar_add(stt[:, c + 3:c + 4], stt[:, c + 3:c + 4], LNEPS), "dve")
                    v.drain()
                blk.vector(body)

                def body(a):
                    S.wait(a, "dve")
                    for tt in range(NT):
                        c = 8 * tt
                        S.inc(a.activation(stt[:, c + 4:c + 5], stt[:, c + 3:c + 4], ACT.Ln), "act")
                    a.drain()
                    for tt in range(NT):
                        c = 8 * tt
                        S.inc(a.activation(stt[:, c + 5:c + 6], stt[:, c + 4:c + 5], ACT.Exp, scale=-0.5), "act")
                blk.scalar(body)

                def body(v):
                    S.wait(v, "act")
                    for tt in range(NT):
                        c = 8 * tt
                        S.inc(v.tensor_scalar(xn_tm[:, tt * M:(tt + 1) * M], xres[:, tt * M:(tt + 1) * M],
                                              stt[:, c + 1:c + 2], stt[:, c + 5:c + 6],
                                              ALU.add, ALU.mult), "dve")
                blk.vector(body)

                def body(q):
                    S.wait(q, "dve")
                    for tt in range(NT):
                        for mb in range(NM):
                            S.inc(q.dma_start_transpose(
                                xn_fm[:, mb * T + tt * 128:mb * T + (tt + 1) * 128],
                                xn_tm[:, tt * M + mb * 128:tt * M + (mb + 1) * 128]), "dma", 16)
                blk.sync(body)

                if dbg is not None:
                    def body(g):
                        S.wait(g, "dve")
                        S.inc(g.dma_start(out=dbg.rearrange("(t p) m -> p t m", p=128),
                                          in_=xn_tm.rearrange("p (t m) -> p t m", t=NT)), "gdma", 16)
                        if "dbg_stt" in dram:
                            S.inc(g.dma_start(out=dram["dbg_stt"][:, :], in_=stt[:, :]), "gdma", 16)
                        S.wait(g, "gdma")
                    blk.gpsimd(body)

    layernorm("l0", dram.get("dbg_xn"))
    if debug:
        with nc.Block() as blk:
            def body(g):
                S.wait(g, "dma")
                S.inc(g.dma_start(out=dram["dbg_xnfm"][:, :], in_=xn_fm[:, :]), "gdma", 16)
                S.wait(g, "gdma")
            blk.gpsimd(body)

    def gla():
        L = ExitStack()
        def lsb(name, shape, dt):
            return L.enter_context(nc.sbuf_tensor(name, shape, dt))

        wbuf = [lsb(f"glaw{i}", [128, NM * M], BF16) for i in range(2)]
        wgt_sb = lsb("wgt_sb", [128, NM * NH], BF16)
        bc = lsb("bc", [NH, T], F32)
        zz16 = lsb("zz16", [NH, CH], F32)
        ebrow = lsb("ebrow", [1, 128], F32)
        eb64 = lsb("eb64", [128, 128], F32)
        selh = lsb("selh_sb", [NH, NH * 128], F32)
        q_sb = lsb("q_sb", [128, NM * CH], BF16)
        k_sb = lsb("k_sb", [128, NM * CH], BF16)
        v_scr = lsb("v_scr", [128, NM * CH], BF16)
        sq_scr = lsb("sq_scr", [128, NM * CH], BF16)
        k_tm = lsb("k_tm", [128, 2 * M], BF16)
        v_tm = lsb("v_tm", [128, 2 * M], BF16)
        silu_g = lsb("silu_g", [128, 2 * M], BF16)
        rows = lsb("rows", [1, 3 * CH], F32)
        bcq = lsb("bcq", [NH, CH], F32)
        bkneg = lsb("bkneg", [NH, CH], F32)
        mk = lsb("mk", [NH, CH], F32)
        mtab_tm = lsb("mtab_tm", [128, 2 * 80], F32)
        S_f = lsb("S_f", [128, (NH // 2) * 64], F32)
        S_b = lsb("S_b", [128, (NH // 2) * 64], BF16)
        dmask = lsb("dmask", [128, 2 * CH], BF16)
        msT = lsb("msT", [128, 2 * CH], BF16)
        o_tok = lsb("o_tok", [128, 2 * M], F32)
        osq = lsb("osq", [128, M], F32)
        rmsst = lsb("rmsst", [128, 2 * 64], F32)

        PS = ExitStack()
        def psb(name):
            return PS.enter_context(nc.psum_tensor(name, [128, 512], F32))
        ps = [psb(f"g_p{i}") for i in range(8)]
        ps_a0, ps_a1, ps_a2, ps_a3, ps_b, ps_c, ps_d, ps_e = ps

        wname = ("Wq", "Wk", "Wv", "Wg", "Wo")
        wtarget = {}
        wfree = {}

        def load_w(g, idx):
            assert idx < 2 or idx - 2 in wfree, f"load_w({idx}) before consumer of {idx-2} emitted"
            if idx - 2 in wfree:
                S.wait(g, "te", wfree[idx - 2])
            S.inc(g.dma_start(out=wbuf[idx % 2].rearrange("p (c d) -> p c d", d=M),
                              in_=dram[wname[idx % 5]].rearrange("(c p) d -> p c d", p=128)), "gdma", 16)
            wtarget[idx] = S.c["gdma"]

        with ExitStack() as G:
            gtt = G.enter_context(nc.sbuf_tensor("gtt", [NH, 512], F32))
            with nc.Block() as blk:
                def body(g):
                    S.inc(g.dma_start(out=wgt_sb.rearrange("p (c h) -> p c h", h=NH),
                                      in_=dram["Wgt"].rearrange("(c p) h -> p c h", p=128)), "gdma", 16)
                    wtarget["gt"] = S.c["gdma"]
                    load_w(g, 0)
                    load_w(g, 1)
                blk.gpsimd(body)

                def body(q):
                    S.inc(q.dma_start(out=selh[:, :], in_=dram["selh"][:, :]), "dma", 16)
                blk.sync(body)

                def body(v):
                    S.inc(v.memset(zz16[:, :], 0.0), "dve")
                    S.inc(v.memset(S_f[:, :], 0.0), "dve")
                    S.inc(v.memset(S_b[:, :], 0.0), "dve")
                    v.drain()
                blk.vector(body)

                def body(t):
                    S.wait(t, "gdma", wtarget["gt"])
                    S.wait(t, "dma")
                    for ts4 in range(4):
                        for mc in range(NM):
                            ins = t.matmul(ps[ts4][0:NH, 0:512],
                                           wgt_sb[:, mc * NH:(mc + 1) * NH],
                                           xn_fm[:, mc * T + ts4 * 512:mc * T + (ts4 + 1) * 512],
                                           start=(mc == 0), stop=(mc == NM - 1))
                        S.inc(ins, "te")
                blk.tensor(body)

                for ts4 in range(4):
                    def body(a, ts4=ts4):
                        if ts4 == 0:
                            S.wait(a, "te")
                        S.wait(a, "dve")
                        S.inc(a.activation(gtt[:, :], ps[ts4][0:NH, 0:512], ACT.Sigmoid), "act")
                        a.drain()
                        S.inc(a.activation(gtt[:, :], gtt[:, :], ACT.Ln), "act")
                    blk.scalar(body)
                    def body(v, ts4=ts4):
                        S.wait(v, "act")
                        S.inc(v.tensor_scalar_mul(gtt[:, :], gtt[:, :], 1.0 / 16.0), "dve")
                        v.drain()
                        for cc in range(2):
                            c = ts4 * 2 + cc
                            S.inc(v.tensor_tensor_scan(bc[:, c * CH:(c + 1) * CH],
                                                       gtt[:, cc * CH:(cc + 1) * CH],
                                                       zz16[:, :], 0.0, ALU.add, ALU.add), "dve")
                        v.drain()
                    blk.vector(body)

                def body(q):
                    S.wait(q, "dve")
                    S.inc(q.dma_start(out=ebrow[0:1, :].rearrange("o (h c) -> o h c", h=NH),
                                      in_=bc.rearrange("h (c q) -> h c q", q=CH)[:, :, CH - 1:CH],
                                      allow_slow_non_contiguous=True), "dma", 16)
                blk.sync(body)

                def body(a):
                    S.wait(a, "dma")
                    S.inc(a.activation(ebrow[:, :], ebrow[:, :], ACT.Exp), "act")
                blk.scalar(body)

                def body(t):
                    S.wait(t, "act")
                    S.inc(t.matmul(ps_b[:, 0:128], onesf[0:1, 0:128], ebrow[0:1, :],
                                   start=True, stop=True), "te")
                blk.tensor(body)

                def body(v):
                    S.wait(v, "te")
                    S.inc(v.tensor_copy(eb64[:, :], ps_b[:, 0:128]), "dve")
                blk.vector(body)

        for c in range(NCH):
            t0 = c * CH
            with nc.Block() as blk:
                for wi, dst in ((0, q_sb), (1, k_sb), (2, v_scr)):
                    widx = c * 5 + wi
                    wb = wbuf[widx % 2]
                    def body(t, wb=wb, widx=widx):
                        S.wait(t, "gdma", wtarget[widx])
                        S.wait(t, "act")
                        S.wait(t, "dve")
                        for ddb in range(NM):
                            for mc in range(NM):
                                ins = t.matmul(ps[ddb][:, 0:CH],
                                               wb[:, mc * M + ddb * 128:mc * M + ddb * 128 + 128],
                                               xn_fm[:, mc * T + t0:mc * T + t0 + CH],
                                               start=(mc == 0), stop=(mc == NM - 1))
                            S.inc(ins, "te")
                        wfree[widx] = S.c["te"]
                    blk.tensor(body)

                    def body(a, dst=dst):
                        S.wait(a, "te")
                        S.wait(a, "dma")
                        for ddb in range(NM):
                            S.inc(a.activation(dst[:, ddb * CH:(ddb + 1) * CH],
                                               ps[ddb][:, 0:CH], ACT.Copy), "act")
                            S.inc(a.activation(sq_scr[:, ddb * CH:(ddb + 1) * CH],
                                               ps[ddb][:, 0:CH], ACT.Square), "act")
                    blk.scalar(body)

                    def body(g, widx=widx):
                        nxt = widx + 2
                        if nxt < 40 and nxt not in wtarget:
                            load_w(g, nxt)
                    blk.gpsimd(body)

                    def body(t):
                        S.wait(t, "act")
                        S.wait(t, "dve")
                        for seg in range(NM):
                            ins = t.matmul(ps_b[0:1, 0:CH], onescol_b[:, 0:1],
                                           sq_scr[:, seg * CH:(seg + 1) * CH],
                                           start=(seg == 0), stop=(seg == NM - 1))
                        S.inc(ins, "te")
                    blk.tensor(body)

                    if wi == 0:
                        def body(a):
                            S.wait(a, "te")
                            S.inc(a.activation(rows[0:1, 0:CH], ps_b[0:1, 0:CH],
                                               ACT.Ln, bias=consts[0:1, 0:1], scale=1.0 / M), "act")
                        blk.scalar(body)
                        def body(v):
                            S.wait(v, "act")
                            S.inc(v.tensor_scalar(rows[0:1, 0:CH], rows[0:1, 0:CH],
                                                  -0.5, LOG_SCALE, ALU.mult, ALU.add), "dve")
                            v.drain()
                        blk.vector(body)
                    elif wi == 1:
                        def body(a):
                            S.wait(a, "te")
                            S.inc(a.activation(rows[0:1, CH:2 * CH], ps_b[0:1, 0:CH],
                                               ACT.Ln, bias=consts[0:1, 0:1], scale=1.0 / M), "act")
                        blk.scalar(body)
                        def body(v):
                            S.wait(v, "act")
                            S.inc(v.tensor_scalar_mul(rows[0:1, CH:2 * CH], rows[0:1, CH:2 * CH], -0.5), "dve")
                            v.drain()
                        blk.vector(body)
                    else:
                        def body(v):
                            S.wait(v, "te")
                            S.inc(v.tensor_scalar(rows[0:1, 2 * CH:3 * CH], ps_b[0:1, 0:CH],
                                                  1.0 / M, GEPS, ALU.mult, ALU.add), "dve")
                            v.drain()
                        blk.vector(body)

                gidx = c * 5 + 3
                wbg = wbuf[gidx % 2]
                for tl in range(2):
                    for half in range(2):
                        pst = ps[tl * 2 + half]
                        def body(t, pst=pst, tl=tl, half=half):
                            S.wait(t, "gdma", wtarget[gidx])
                            S.wait(t, "act")
                            S.wait(t, "dve")
                            for mc in range(NM):
                                ins = t.matmul(pst[:, 0:512],
                                               xn_fm[:, mc * T + t0 + tl * 128:mc * T + t0 + (tl + 1) * 128],
                                               wbg[:, mc * M + half * 512:mc * M + (half + 1) * 512],
                                               start=(mc == 0), stop=(mc == NM - 1))
                            S.inc(ins, "te")
                            if half == 1 and tl == 1:
                                wfree[gidx] = S.c["te"]
                        blk.tensor(body)
                        def body(a, pst=pst, tl=tl, half=half):
                            S.wait(a, "te")
                            S.inc(a.activation(silu_g[:, tl * M + half * 512:tl * M + (half + 1) * 512],
                                               pst[:, 0:512], ACT.Silu), "act")
                        blk.scalar(body)

                def body(g, gidx=gidx):
                    nxt = gidx + 2
                    if nxt < 40 and nxt not in wtarget:
                        load_w(g, nxt)
                blk.gpsimd(body)

                def body(t):
                    S.wait(t, "dve")
                    S.wait(t, "act")
                    S.inc(t.matmul(ps_b[0:NH, 0:CH], onesf[0:1, 0:NH], rows[0:1, 0:CH],
                                   start=True, stop=True), "te")
                    S.inc(t.matmul(ps_c[0:NH, 0:CH], onesf[0:1, 0:NH], rows[0:1, CH:2 * CH],
                                   start=True, stop=True), "te")
                blk.tensor(body)

                def body(v):
                    S.wait(v, "te")
                    S.inc(v.tensor_add(bcq[:, :], bc[:, t0:t0 + CH], ps_b[0:NH, 0:CH]), "dve")
                    S.inc(v.scalar_tensor_tensor(bkneg[:, :], bc[:, t0:t0 + CH], -1.0,
                                                 ps_c[0:NH, 0:CH], ALU.mult, ALU.add), "dve")
                    v.drain()
                blk.vector(body)

                def body(a):
                    S.wait(a, "dve")
                    S.inc(a.activation(mk[:, :], bkneg[:, :], ACT.Exp,
                                       bias=bc[:, t0 + CH - 1:t0 + CH]), "act")
                blk.scalar(body)

                for tl in range(2):
                    def body(t, tl=tl):
                        S.wait(t, "act")
                        S.wait(t, "dve")
                        sl = slice(tl * 128, (tl + 1) * 128)
                        S.inc(t.transpose(ps_b[:, 0:16], bcq[0:16, sl], identf[0:16, 0:16]), "te")
                        S.inc(t.transpose(ps_c[:, 0:16], mk[0:16, sl], identf[0:16, 0:16]), "te")
                        S.inc(t.transpose(ps_d[:, 0:1], rows[0:1, 2 * CH + tl * 128:2 * CH + (tl + 1) * 128],
                                          identf[0:1, 0:1]), "te")
                        S.inc(t.transpose(ps_e[:, 0:16], bkneg[0:16, sl], identf[0:16, 0:16]), "te")
                    blk.tensor(body)
                    def body(v, tl=tl):
                        S.wait(v, "te")
                        b0 = tl * 80
                        S.inc(v.tensor_copy(mtab_tm[:, b0 + 0:b0 + 16], ps_b[:, 0:16]), "dve")
                        S.inc(v.tensor_copy(mtab_tm[:, b0 + 16:b0 + 32], ps_c[:, 0:16]), "dve")
                        S.inc(v.tensor_copy(mtab_tm[:, b0 + 32:b0 + 33], ps_d[:, 0:1]), "dve")
                        S.inc(v.tensor_copy(mtab_tm[:, b0 + 33:b0 + 49], ps_e[:, 0:16]), "dve")
                        v.drain()
                    blk.vector(body)
                    def body(a, tl=tl):
                        S.wait(a, "dve")
                        b0 = tl * 80
                        S.inc(a.activation(mtab_tm[:, b0 + 0:b0 + 16], mtab_tm[:, b0 + 0:b0 + 16], ACT.Exp), "act")
                        S.inc(a.activation(mtab_tm[:, b0 + 32:b0 + 33], mtab_tm[:, b0 + 32:b0 + 33], ACT.Ln), "act")
                        a.drain()
                        S.inc(a.activation(mtab_tm[:, b0 + 32:b0 + 33], mtab_tm[:, b0 + 32:b0 + 33], ACT.Exp, scale=-0.5), "act")
                    blk.scalar(body)

                def body(q):
                    S.wait(q, "act")
                    S.wait(q, "te")
                    for tl in range(2):
                        for seg in range(NM):
                            S.inc(q.dma_start_transpose(
                                k_tm[:, tl * M + seg * 128:tl * M + (seg + 1) * 128],
                                k_sb[:, seg * CH + tl * 128:seg * CH + (tl + 1) * 128]), "dma", 16)
                            S.inc(q.dma_start_transpose(
                                v_tm[:, tl * M + seg * 128:tl * M + (seg + 1) * 128],
                                v_scr[:, seg * CH + tl * 128:seg * CH + (tl + 1) * 128]), "dma", 16)
                blk.sync(body)

                def body(v):
                    S.wait(v, "dma")
                    S.wait(v, "act")
                    v.drain()
                    for tl in range(2):
                        S.inc(v.tensor_scalar_mul(v_tm[:, tl * M:(tl + 1) * M],
                                                  v_tm[:, tl * M:(tl + 1) * M],
                                                  mtab_tm[:, tl * 80 + 32:tl * 80 + 33]), "dve")
                        for h in range(NH):
                            S.inc(v.tensor_scalar_mul(k_tm[:, tl * M + h * 64:tl * M + (h + 1) * 64],
                                                      k_tm[:, tl * M + h * 64:tl * M + (h + 1) * 64],
                                                      mtab_tm[:, tl * 80 + 16 + h:tl * 80 + 17 + h]), "dve")
                blk.vector(body)

                if c == 0 and "dbg_q" in dram:
                    def body(g):
                        S.wait(g, "dve")
                        S.wait(g, "act")
                        S.inc(g.dma_start(out=dram["dbg_q"][:, :], in_=q_sb[:, :]), "gdma", 16)
                        S.inc(g.dma_start(out=dram["dbg_k"][:, :], in_=k_sb[:, :]), "gdma", 16)
                        S.inc(g.dma_start(out=dram["dbg_v"][:, :], in_=v_scr[:, :]), "gdma", 16)
                        S.inc(g.dma_start(out=dram["dbg_bc"][:, :], in_=bc[:, :]), "gdma", 16)
                        S.inc(g.dma_start(out=dram["dbg_bcq"][:, :], in_=bcq[:, :]), "gdma", 16)
                        S.inc(g.dma_start(out=dram["dbg_bkneg"][:, :], in_=bkneg[:, :]), "gdma", 16)
                        S.inc(g.dma_start(out=dram["dbg_mtab"][:, :], in_=mtab_tm[:, :]), "gdma", 16)
                        S.inc(g.dma_start(out=dram["dbg_silug"][:, :], in_=silu_g[:, :]), "gdma", 16)
                        S.inc(g.dma_start(out=dram["dbg_ktm"][:, :], in_=k_tm[:, :]), "gdma", 16)
                        S.inc(g.dma_start(out=dram["dbg_vtm"][:, :], in_=v_tm[:, :]), "gdma", 16)
                        S.wait(g, "gdma")
                    blk.gpsimd(body)

                for h in range(NH):
                    po = (h % 2) * 64
                    sg = h // 2
                    def body(t, h=h):
                        S.wait(t, "act")
                        S.wait(t, "dve")
                        S.inc(t.matmul(ps_c[:, 0:CH], selh[:, h * 128:(h + 1) * 128],
                                       bcq[:, :], start=True, stop=True), "te")
                    blk.tensor(body)

                    def body(a, h=h):
                        S.wait(a, "te")
                        for jh in range(2):
                            S.inc(a.activation(dmask[:, jh * CH:(jh + 1) * CH], ps_c[:, 0:CH],
                                               ACT.Exp,
                                               bias=mtab_tm[:, jh * 80 + 33 + h:jh * 80 + 34 + h]), "act")
                    blk.scalar(body)

                    def body(t, h=h, po=po, sg=sg):
                        for jh in range(2):
                            S.inc(t.matmul((ps_d, ps_e)[jh][:, 0:CH],
                                           k_sb[po:po + 64, sg * CH + jh * 128:sg * CH + (jh + 1) * 128],
                                           q_sb[po:po + 64, sg * CH:(sg + 1) * CH],
                                           start=True, stop=True), "te")
                    blk.tensor(body)

                    def body(v, h=h):
                        S.wait(v, "act")
                        S.wait(v, "te")
                        for jh in range(2):
                            S.inc(v.tensor_mul(dmask[:, jh * CH:(jh + 1) * CH],
                                               dmask[:, jh * CH:(jh + 1) * CH],
                                               trilT_sb[:, jh * CH:(jh + 1) * CH]), "dve")
                        v.drain()
                        for jh in range(2):
                            S.inc(v.tensor_mul(msT[:, jh * CH:(jh + 1) * CH],
                                               (ps_d, ps_e)[jh][:, 0:CH],
                                               dmask[:, jh * CH:(jh + 1) * CH]), "dve")
                        v.drain()
                    blk.vector(body)

                    if c == 0 and h == 0 and "dbg_dmask" in dram:
                        def body(g):
                            S.wait(g, "dve")
                            S.inc(g.dma_start(out=dram["dbg_dmask"][:, :], in_=dmask[:, :]), "gdma", 16)
                            S.inc(g.dma_start(out=dram["dbg_msT"][:, :], in_=msT[:, :]), "gdma", 16)
                            S.wait(g, "gdma")
                        blk.gpsimd(body)

                    def body(t, h=h, po=po, sg=sg):
                        S.wait(t, "dve")
                        S.wait(t, "act")
                        for tl in range(2):
                            S.inc(t.matmul(ps_a2[po:po + 64, 0:64],
                                           k_tm[:, tl * M + h * 64:tl * M + (h + 1) * 64],
                                           v_tm[:, tl * M + h * 64:tl * M + (h + 1) * 64],
                                           start=(tl == 0), stop=(tl == 1)), "te")
                        for ih in range(2):
                            for jh in range(2):
                                S.inc(t.matmul((ps_a0, ps_a1)[ih][:, 0:64],
                                               msT[:, jh * CH + ih * 128:jh * CH + (ih + 1) * 128],
                                               v_tm[:, jh * M + h * 64:jh * M + (h + 1) * 64],
                                               start=(jh == 0), stop=(jh == 1)), "te")
                            S.inc(t.matmul((ps_d, ps_e)[ih][:, 0:64],
                                           q_sb[po:po + 64, sg * CH + ih * 128:sg * CH + (ih + 1) * 128],
                                           S_b[po:po + 64, sg * 64:(sg + 1) * 64],
                                           start=True, stop=True), "te")
                    blk.tensor(body)

                    def body(a, h=h):
                        S.wait(a, "te")
                        for ih in range(2):
                            S.inc(a.activation(o_tok[:, ih * M + h * 64:ih * M + (h + 1) * 64],
                                               (ps_a0, ps_a1)[ih][:, 0:64], ACT.Copy), "act")
                    blk.scalar(body)

                    def body(v, h=h, c=c, po=po, sg=sg):
                        S.wait(v, "act")
                        for ih in range(2):
                            S.inc(v.scalar_tensor_tensor(
                                o_tok[:, ih * M + h * 64:ih * M + (h + 1) * 64],
                                (ps_d, ps_e)[ih][:, 0:64],
                                mtab_tm[:, ih * 80 + h:ih * 80 + h + 1],
                                o_tok[:, ih * M + h * 64:ih * M + (h + 1) * 64],
                                ALU.mult, ALU.add), "dve")
                        S.inc(v.scalar_tensor_tensor(S_f[po:po + 64, sg * 64:(sg + 1) * 64],
                                                     S_f[po:po + 64, sg * 64:(sg + 1) * 64],
                                                     eb64[po:po + 64, h * NCH + c:h * NCH + c + 1],
                                                     ps_a2[po:po + 64, 0:64], ALU.mult, ALU.add), "dve")
                        v.drain()
                        S.inc(v.tensor_copy(S_b[po:po + 64, sg * 64:(sg + 1) * 64],
                                            S_f[po:po + 64, sg * 64:(sg + 1) * 64]), "dve")
                        v.drain()
                    blk.vector(body)

                if c == 0 and "dbg_otok" in dram:
                    def body(g):
                        S.wait(g, "dve")
                        S.inc(g.dma_start(out=dram["dbg_otok"][:, :], in_=o_tok[:, :]), "gdma", 16)
                        S.wait(g, "gdma")
                    blk.gpsimd(body)

                oidx = c * 5 + 4
                wbo = wbuf[oidx % 2]
                og = v_scr
                o_fm = q_sb
                for ih in range(2):
                    tg = 2 * c + ih
                    r = ih * 64
                    def body(a, ih=ih):
                        S.wait(a, "dve")
                        S.inc(a.activation(osq[:, :], o_tok[:, ih * M:(ih + 1) * M], ACT.Square), "act")
                    blk.scalar(body)
                    def body(v, r=r):
                        S.wait(v, "act")
                        S.inc(v.reduce_sum(rmsst[:, r:r + 16],
                                           osq.rearrange("p (h d) -> p h d", d=64), axis=AX.X), "dve")
                        v.drain()
                        S.inc(v.tensor_scalar(rmsst[:, r + 16:r + 32], rmsst[:, r:r + 16],
                                              1.0 / 64, GEPS, ALU.mult, ALU.add), "dve")
                        v.drain()
                    blk.vector(body)
                    def body(a, r=r):
                        S.wait(a, "dve")
                        S.inc(a.activation(rmsst[:, r + 32:r + 48], rmsst[:, r + 16:r + 32], ACT.Ln), "act")
                        a.drain()
                        S.inc(a.activation(rmsst[:, r + 48:r + 64], rmsst[:, r + 32:r + 48], ACT.Exp, scale=-0.5), "act")
                    blk.scalar(body)
                    def body(v, ih=ih, r=r):
                        S.wait(v, "act")
                        for h in range(NH):
                            S.inc(v.scalar_tensor_tensor(
                                og[:, ih * M + h * 64:ih * M + (h + 1) * 64],
                                o_tok[:, ih * M + h * 64:ih * M + (h + 1) * 64],
                                rmsst[:, r + 48 + h:r + 49 + h],
                                silu_g[:, ih * M + h * 64:ih * M + (h + 1) * 64],
                                ALU.mult, ALU.mult), "dve")
                        v.drain()
                    blk.vector(body)
                    def body(q, ih=ih):
                        S.wait(q, "dve")
                        S.wait(q, "te")
                        for seg in range(NM):
                            S.inc(q.dma_start_transpose(
                                o_fm[:, seg * CH + ih * 128:seg * CH + (ih + 1) * 128],
                                og[:, ih * M + seg * 128:ih * M + (seg + 1) * 128]), "dma", 16)
                    blk.sync(body)
                    def body(t, ih=ih, wbo=wbo, oidx=oidx):
                        S.wait(t, "gdma", wtarget[oidx])
                        S.wait(t, "dma")
                        S.wait(t, "act")
                        S.wait(t, "dve")
                        for mh in range(2):
                            for seg in range(NM):
                                ins = t.matmul((ps_b, ps_c)[mh][:, 0:512],
                                               o_fm[:, seg * CH + ih * 128:seg * CH + (ih + 1) * 128],
                                               wbo[:, seg * M + mh * 512:seg * M + (mh + 1) * 512],
                                               start=(seg == 0), stop=(seg == NM - 1))
                            S.inc(ins, "te")
                        if ih == 1:
                            wfree[oidx] = S.c["te"]
                    blk.tensor(body)
                    def body(v, tg=tg):
                        S.wait(v, "te")
                        for mh in range(2):
                            S.inc(v.tensor_add(xres[:, tg * M + mh * 512:tg * M + (mh + 1) * 512],
                                               (ps_b, ps_c)[mh][:, 0:512],
                                               xres[:, tg * M + mh * 512:tg * M + (mh + 1) * 512]), "dve")
                        v.drain()
                    blk.vector(body)

                if c == 0 and "dbg_og" in dram:
                    def body(g):
                        S.wait(g, "dve")
                        S.wait(g, "dma")
                        S.inc(g.dma_start(out=dram["dbg_og"][:, :], in_=og[:, :]), "gdma", 16)
                        S.inc(g.dma_start(out=dram["dbg_ofm"][:, :], in_=o_fm[:, :]), "gdma", 16)
                        S.inc(g.dma_start(out=dram["dbg_xres0"][:, :], in_=xres[:, 0:2 * M]), "gdma", 16)
                        S.wait(g, "gdma")
                    blk.gpsimd(body)

                def body(g, oidx=oidx):
                    for nxt in (oidx + 1, oidx + 2):
                        if nxt < 40 and nxt not in wtarget:
                            load_w(g, nxt)
                blk.gpsimd(body)

        L.close()
        PS.close()

    if not SKIP_GLA:
        gla()

    def moe():
        NE, HF = 8, 512
        L = ExitStack()
        def lsb(name, shape, dt):
            return L.enter_context(nc.sbuf_tensor(name, shape, dt))
        gw_sb = lsb("gw_sb", [128, NM * NE], BF16)
        glog = lsb("glog", [NE, T], F32)
        gsc = lsb("gsc", [128, NT * NE], F32)
        scl = lsb("scl", [128, NT * NE], F32)
        mx8 = lsb("mx8", [128, 8], F32)
        mi8 = lsb("mi8", [128, 8], mybir.dt.uint32)
        gst = lsb("gst", [128, NT * 4], F32)
        w1_sb = lsb("w1_sb", [128, NM * HF], BF16)
        w3_sb = lsb("w3_sb", [128, NM * HF], BF16)
        w2_sb = lsb("w2_sb", [128, 4 * M], BF16)
        h1_sb = lsb("h1_sb", [128, HF], BF16)
        h_sb = lsb("h_sb", [128, HF], BF16)
        h_fm = lsb("h_fm", [128, 4 * 128], BF16)
        y_sb = lsb("y_sb", [128, NT * M], F32)

        PS = ExitStack()
        def psb(name, shape):
            return PS.enter_context(nc.psum_tensor(name, shape, F32))
        ps_l = psb("m_pl", [NE, T])
        ps0 = psb("m_p0", [128, 512])
        ps1 = psb("m_p1", [128, 512])
        ps2 = psb("m_p2", [128, 512])
        ps3 = psb("m_p3", [128, 512])

        wtgt = {}
        def load_we(g, e):
            S.inc(g.dma_start(out=w1_sb.rearrange("p (c h) -> p c h", h=HF),
                              in_=dram["w1"][e].rearrange("(c p) h -> p c h", p=128)), "gdma", 16)
            S.inc(g.dma_start(out=w3_sb.rearrange("p (c h) -> p c h", h=HF),
                              in_=dram["w3"][e].rearrange("(c p) h -> p c h", p=128)), "gdma", 16)
            S.inc(g.dma_start(out=w2_sb.rearrange("p (c m) -> p c m", m=M),
                              in_=dram["w2"][e].rearrange("(c p) m -> p c m", p=128)), "gdma", 16)
            wtgt[e] = S.c["gdma"]

        with nc.Block() as blk:
            def body(g):
                S.inc(g.dma_start(out=gw_sb.rearrange("p (c e) -> p c e", e=NE),
                                  in_=dram["gate_w"].rearrange("(c p) e -> p c e", p=128)), "gdma", 16)
                wtgt["g"] = S.c["gdma"]
            blk.gpsimd(body)

            def body(t):
                S.wait(t, "gdma", wtgt["g"])
                S.wait(t, "dve")
                for ts4 in range(4):
                    for mc in range(NM):
                        ins = t.matmul(ps_l[0:NE, ts4 * 512:(ts4 + 1) * 512],
                                       gw_sb[:, mc * NE:(mc + 1) * NE],
                                       xn_fm[:, mc * T + ts4 * 512:mc * T + (ts4 + 1) * 512],
                                       start=(mc == 0), stop=(mc == NM - 1))
                    S.inc(ins, "te")
            blk.tensor(body)

            def body(a):
                S.wait(a, "te")
                S.inc(a.activation(glog[:, :], ps_l[0:NE, 0:T], ACT.Copy), "act")
            blk.scalar(body)

            for grp in range(4):
                def body(t, grp=grp):
                    S.wait(t, "act")
                    S.wait(t, "dve")
                    for q4 in range(4):
                        tt = grp * 4 + q4
                        S.inc(t.transpose((ps0, ps1, ps2, ps3)[q4][:, 0:NE],
                                          glog[0:NE, tt * 128:(tt + 1) * 128],
                                          identf[0:NE, 0:NE]), "te")
                blk.tensor(body)
                def body(v, grp=grp):
                    S.wait(v, "te")
                    for q4 in range(4):
                        tt = grp * 4 + q4
                        S.inc(v.tensor_copy(gsc[:, tt * NE:(tt + 1) * NE],
                                            (ps0, ps1, ps2, ps3)[q4][:, 0:NE]), "dve")
                blk.vector(body)

            def body(v):
                for tt in range(NT):
                    S.inc(v.reduce_max(gst[:, tt * 4:tt * 4 + 1], gsc[:, tt * NE:(tt + 1) * NE],
                                       axis=AX.X, negate=True), "dve")
                v.drain()
            blk.vector(body)
            def body(a):
                S.wait(a, "dve")
                for tt in range(NT):
                    S.inc(a.activation(scl[:, tt * NE:(tt + 1) * NE], gsc[:, tt * NE:(tt + 1) * NE],
                                       ACT.Exp, bias=gst[:, tt * 4:tt * 4 + 1],
                                       accum_out=gst[:, tt * 4 + 1:tt * 4 + 2]), "act")
            blk.scalar(body)
            def body(v):
                S.wait(v, "act")
                for tt in range(NT):
                    S.inc(v.reciprocal(gst[:, tt * 4 + 2:tt * 4 + 3], gst[:, tt * 4 + 1:tt * 4 + 2]), "dve")
                v.drain()
                for tt in range(NT):
                    S.inc(v.tensor_scalar_mul(scl[:, tt * NE:(tt + 1) * NE], scl[:, tt * NE:(tt + 1) * NE],
                                              gst[:, tt * 4 + 2:tt * 4 + 3]), "dve")
                v.drain()
                for tt in range(NT):
                    v.max_with_indices(mx8[:, :], mi8[:, :], scl[:, tt * NE:(tt + 1) * NE])
                    v.drain()
                    S.inc(v.scalar_tensor_tensor(scl[:, tt * NE:(tt + 1) * NE],
                                                 scl[:, tt * NE:(tt + 1) * NE],
                                                 mx8[:, 1:2],
                                                 scl[:, tt * NE:(tt + 1) * NE],
                                                 ALU.is_ge, ALU.mult), "dve")
                    v.drain()
            blk.vector(body)

        for e in range(NE):
            with nc.Block() as blk:
                def body(g, e=e):
                    S.wait(g, "te")
                    load_we(g, e)
                blk.gpsimd(body)

                for tt in range(NT):
                    def body(t, e=e, tt=tt):
                        S.wait(t, "gdma", wtgt[e])
                        S.wait(t, "act")
                        S.wait(t, "dve")
                        for mc in range(NM):
                            ins = t.matmul(ps0[:, 0:HF],
                                           xn_fm[:, mc * T + tt * 128:mc * T + (tt + 1) * 128],
                                           w1_sb[:, mc * HF:(mc + 1) * HF],
                                           start=(mc == 0), stop=(mc == NM - 1))
                        S.inc(ins, "te")
                        for mc in range(NM):
                            ins = t.matmul(ps1[:, 0:HF],
                                           xn_fm[:, mc * T + tt * 128:mc * T + (tt + 1) * 128],
                                           w3_sb[:, mc * HF:(mc + 1) * HF],
                                           start=(mc == 0), stop=(mc == NM - 1))
                        S.inc(ins, "te")
                    blk.tensor(body)

                    def body(a, e=e, tt=tt):
                        S.wait(a, "te")
                        S.inc(a.activation(h1_sb[:, :], ps0[:, 0:HF], ACT.Silu,
                                           scale=scl[:, tt * NE + e:tt * NE + e + 1]), "act")
                    blk.scalar(body)

                    def body(v, e=e, tt=tt):
                        S.wait(v, "act")
                        S.inc(v.scalar_tensor_tensor(h_sb[:, :], ps1[:, 0:HF],
                                                     scl[:, tt * NE + e:tt * NE + e + 1],
                                                     h1_sb[:, :], ALU.mult, ALU.mult), "dve")
                        v.drain()
                    blk.vector(body)

                    def body(q):
                        S.wait(q, "dve")
                        S.wait(q, "te")
                        for hb in range(4):
                            S.inc(q.dma_start_transpose(h_fm[:, hb * 128:(hb + 1) * 128],
                                                        h_sb[:, hb * 128:(hb + 1) * 128]), "dma", 16)
                    blk.sync(body)

                    def body(t, tt=tt):
                        S.wait(t, "dma")
                        for mh in range(2):
                            for hb in range(4):
                                ins = t.matmul((ps2, ps3)[mh][:, 0:512],
                                               h_fm[:, hb * 128:(hb + 1) * 128],
                                               w2_sb[:, hb * M + mh * 512:hb * M + (mh + 1) * 512],
                                               start=(hb == 0), stop=(hb == 3))
                            S.inc(ins, "te")
                    blk.tensor(body)

                    if e == 0:
                        def body(a, tt=tt):
                            S.wait(a, "te")
                            for mh in range(2):
                                S.inc(a.activation(y_sb[:, tt * M + mh * 512:tt * M + (mh + 1) * 512],
                                                   (ps2, ps3)[mh][:, 0:512], ACT.Copy), "act")
                        blk.scalar(body)
                    else:
                        def body(v, tt=tt):
                            S.wait(v, "te")
                            for mh in range(2):
                                S.inc(v.tensor_add(y_sb[:, tt * M + mh * 512:tt * M + (mh + 1) * 512],
                                                   (ps2, ps3)[mh][:, 0:512],
                                                   y_sb[:, tt * M + mh * 512:tt * M + (mh + 1) * 512]), "dve")
                            v.drain()
                        blk.vector(body)

        with nc.Block() as blk:
            def body(v):
                S.wait(v, "act")
                for tt in range(NT):
                    for mh in range(2):
                        S.inc(v.tensor_add(xres[:, tt * M + mh * 512:tt * M + (mh + 1) * 512],
                                           y_sb[:, tt * M + mh * 512:tt * M + (mh + 1) * 512],
                                           xres[:, tt * M + mh * 512:tt * M + (mh + 1) * 512]), "dve")
                v.drain()
            blk.vector(body)

        L.close()
        PS.close()

    if not globals().get('SKIP_MOE', False):
        moe()

    layernorm("l1")

    def routing_s():
        with ExitStack() as RS:
            tokrep = RS.enter_context(nc.sbuf_tensor("rt_tokrep", [128, M], F32))
            xnscr = RS.enter_context(nc.sbuf_tensor("rt_xnscr", [128, M], F32))
            st2 = RS.enter_context(nc.sbuf_tensor("rt_st2", [128, 8], F32))
            xnkeep = RS.enter_context(nc.sbuf_tensor("rt_keep", [128, M + 8], F32))
            with nc.Block() as blk:
                with ExitStack() as RPS:
                    ps0 = RPS.enter_context(nc.psum_tensor("rt_ps0", [128, 512], F32))
                    ps1 = RPS.enter_context(nc.psum_tensor("rt_ps1", [128, 512], F32))
                    def body(q):
                        S.inc(q.dma_start(out=xnscr[0:1, 0:M], in_=dram["q_rt_row"][:, :]), "dma", 16)
                    blk.sync(body)
                    def body(t):
                        S.wait(t, "dma")
                        S.wait(t, "dve")
                        S.wait(t, "act")
                        for mh in range(2):
                            S.inc(t.matmul((ps0, ps1)[mh][:, 0:512], onesf[0:1, :],
                                           xnscr[0:1, mh * 512:(mh + 1) * 512], start=True, stop=True), "te")
                    blk.tensor(body)
                    def body(a):
                        S.wait(a, "te")
                        for mh in range(2):
                            S.inc(a.activation(tokrep[:, mh * 512:(mh + 1) * 512],
                                               (ps0, ps1)[mh][:, 0:512], ACT.Copy), "act")
                    blk.scalar(body)
                    for tt in range(NT):
                        def body(a, tt=tt):
                            S.wait(a, "dve")
                            if tt == 0:
                                S.wait(a, "te")
                            S.inc(a.activation(xnscr[:, :], xres[:, tt * M:(tt + 1) * M], ACT.Square,
                                               accum_out=st2[:, 0:1]), "act")
                        blk.scalar(body)
                        def body(v, tt=tt):
                            S.wait(v, "act")
                            S.inc(v.reduce_sum(st2[:, 1:2], xres[:, tt * M:(tt + 1) * M], axis=AX.X), "dve")
                            v.drain()
                            S.inc(v.tensor_scalar_mul(st2[:, 2:3], st2[:, 1:2], -1.0 / M), "dve")
                            v.drain()
                            S.inc(v.tensor_mul(st2[:, 3:4], st2[:, 2:3], st2[:, 2:3]), "dve")
                            v.drain()
                            S.inc(v.scalar_tensor_tensor(st2[:, 4:5], st2[:, 0:1], 1.0 / M,
                                                         st2[:, 3:4], ALU.mult, ALU.subtract), "dve")
                            v.drain()
                            S.inc(v.tensor_scalar_add(st2[:, 4:5], st2[:, 4:5], LNEPS), "dve")
                            v.drain()
                        blk.vector(body)
                        def body(a, tt=tt):
                            S.wait(a, "dve")
                            S.inc(a.activation(st2[:, 5:6], st2[:, 4:5], ACT.Ln), "act")
                            a.drain()
                            S.inc(a.activation(st2[:, 6:7], st2[:, 5:6], ACT.Exp, scale=-0.5), "act")
                        blk.scalar(body)
                        def body(v, tt=tt):
                            S.wait(v, "act")
                            S.inc(v.tensor_scalar(xnscr[:, :], xres[:, tt * M:(tt + 1) * M],
                                                  st2[:, 2:3], st2[:, 6:7], ALU.add, ALU.mult), "dve")
                            v.drain()
                        blk.vector(body)
                        if tt == 0 and "dbg_tokrep" in dram:
                            def body(v):
                                S.inc(v.tensor_copy(xnkeep[:, 0:M], xnscr[:, :]), "dve")
                                S.inc(v.tensor_copy(xnkeep[:, M:M + 8], st2[:, :]), "dve")
                                v.drain()
                            blk.vector(body)
                            def body(g):
                                S.wait(g, "dve")
                                S.inc(g.dma_start(out=dram["dbg_tokrep"][:, :], in_=tokrep[:, :]), "gdma", 16)
                                S.inc(g.dma_start(out=dram["dbg_xnt"][:, :], in_=xnkeep[:, 0:M]), "gdma", 16)
                                S.inc(g.dma_start(out=dram["dbg_tiny"][:, :], in_=xnkeep[0:1, M:M + 8]), "gdma", 16)
                            blk.gpsimd(body)
                        def body(v, tt=tt):
                            S.inc(v.tensor_mul(xnscr[:, :], xnscr[:, :], tokrep[:, :]), "dve")
                            v.drain()
                            S.inc(v.reduce_sum(s2d[:, tt:tt + 1], xnscr[:, :], axis=AX.X), "dve")
                            v.drain()
                        blk.vector(body)

    routing_s()

    def mamba():
        DIN, DST, NDT = 2048, 16, 16
        TC, NC = 512, 4
        L = ExitStack()
        def lsb(name, shape, dt):
            return L.enter_context(nc.sbuf_tensor(name, shape, dt))
        wq_sb = lsb("mw_q", [128, 2 * 4096], BF16)
        wx_sb = lsb("mw_x", [128, NDT * 33], BF16)
        wdt_n = lsb("m_wdtn", [128, DST], F32)
        bdt_n = lsb("m_bdtn", [128, DST], F32)
        wdt_r = lsb("m_wdtr", [128, DST], F32)
        bdt_r = lsb("m_bdtr", [128, DST], F32)
        cb_sb = lsb("m_cb", [128, DST], F32)
        dp_sb = lsb("m_dp", [128, DST], F32)
        cw_sb = lsb("m_cw", [128, NDT * 4], F32)
        alog = lsb("m_alog", [128, NDT * DST], F32)
        brow = lsb("m_brow", [1, 2 * 512], BF16)
        states = lsb("m_states", [128, NDT * DST], F32)
        xd = lsb("m_xd", [128, NDT * 516], BF16)
        zs = lsb("m_zs", [128, NDT * TC], BF16)
        bca = lsb("m_bca", [128, 32 * TC], BF16)
        ssm_t = lsb("m_ssmt", [33, TC], F32)
        ssm_tb = lsb("m_ssmtb", [33, TC], BF16)
        ssm_e = lsb("m_ssme", [128, 33 * 4], F32)
        lns = lsb("m_lns", [128, TC], F32)
        acc = lns
        zzt = lsb("m_zzt", [128, TC], F32)
        ltt = lsb("m_ltt", [128, TC], F32)
        wdd = lsb("m_w", [128, TC], BF16)
        dA = lsb("m_dA", [128, 2 * TC], F32)
        bP = lsb("m_bP", [128, 2 * TC], BF16)
        hh = lsb("m_h", [128, 4 * TC], BF16)
        gg = lsb("m_g", [128, 2 * TC], BF16)

        PS = ExitStack()
        def psb(name):
            return PS.enter_context(nc.psum_tensor(name, [128, 512], F32))
        ps = [psb(f"mm_p{i}") for i in range(8)]

        wtgt = {}
        hread = {0: 0, 1: 0}
        mkA_mm = {}; mkA_cp = {}; mkF_mm = {}; mk_bca = {}
        mk_zz = {}; mk_abs = {}; mk_ln = {}; mk_sp = {}; mk_wdd = {}
        mk_dA = {}; mk_bP = {}; mk_scan = {}; mk_gg = {}
        mk_mm = {}; mk_mmD = {}; mk_yev = {}
        xd3 = xd.rearrange("p (c w) -> p c w", w=516)

        def uslc(d, a=0, b=TC):
            return xd[:, d * 516 + 3 + a:d * 516 + 3 + b]

        with nc.Block() as blk:
            def body(g):
                S.inc(g.dma_start(out=wx_sb.rearrange("p (c k) -> p c k", k=33),
                                  in_=dram["W_xT"].rearrange("(c p) k -> p c k", p=128)), "gdma", 16)
                S.inc(g.dma_start(out=alog.rearrange("p (c s) -> p c s", s=DST),
                                  in_=dram["A_log"].rearrange("(c p) s -> p c s", p=128)), "gdma", 16)
                S.inc(g.dma_start(out=wdt_r.rearrange("p (c o) -> p c o", o=1),
                                  in_=dram["W_dt"].rearrange("(c p) o -> p c o", p=128),
                                  allow_slow_non_contiguous=True), "gdma", 16)
                S.inc(g.dma_start(out=bdt_r.rearrange("p (c o) -> p c o", o=1),
                                  in_=dram["b_dt"].rearrange("(c p) o -> p c o", p=128),
                                  allow_slow_non_contiguous=True), "gdma", 16)
                S.inc(g.dma_start(out=cb_sb.rearrange("p (c o) -> p c o", o=1),
                                  in_=dram["conv_b"].rearrange("(c p) o -> p c o", p=128),
                                  allow_slow_non_contiguous=True), "gdma", 16)
                S.inc(g.dma_start(out=dp_sb.rearrange("p (c o) -> p c o", o=1),
                                  in_=dram["D_param"].rearrange("(c p) o -> p c o", p=128),
                                  allow_slow_non_contiguous=True), "gdma", 16)
                S.inc(g.dma_start(out=cw_sb.rearrange("p (c j) -> p c j", j=4),
                                  in_=dram["conv_w"].rearrange("(c p) j -> p c j", p=128)), "gdma", 16)
            blk.gpsimd(body)

            def body(v):
                S.wait(v, "gdma")
                S.inc(v.memset(states[:, :], 0.0), "dve")
                S.inc(v.memset(xd3[:, :, 0:3], 0.0), "dve")
                S.inc(v.tensor_scalar_mul(wdt_n[:, :], wdt_r[:, :], -1.0), "dve")
                S.inc(v.tensor_scalar_mul(bdt_n[:, :], bdt_r[:, :], -1.0), "dve")
                v.drain()
            blk.vector(body)

            def body(a):
                S.wait(a, "gdma")
                S.inc(a.activation(alog[:, :], alog[:, :], ACT.Exp), "act")
            blk.scalar(body)
            def body(v):
                S.wait(v, "act")
                S.inc(v.tensor_scalar_mul(alog[:, :], alog[:, :], -1.0), "dve")
                v.drain()
            blk.vector(body)

        for c in range(NC):
            t0 = c * TC
            with nc.Block() as blk:
                for q in range(8):
                    hb = (q % 2) * 4096
                    def body(g, q=q, c=c, hb=hb):
                        if hread[q % 2]:
                            g.wait_ge(S.h["te"], hread[q % 2])
                        S.inc(g.dma_start(out=wq_sb[:, hb:hb + 4096].rearrange("p (m d) -> p m d", d=512),
                                          in_=dram["W_inT"][:, q * 512:(q + 1) * 512]
                                          .rearrange("(m p) d -> p m d", p=128)), "gdma", 16)
                        wtgt[("in", c, q)] = S.c["gdma"]
                    blk.gpsimd(body)

                    def body(t, q=q, c=c, hb=hb):
                        S.wait(t, "gdma", wtgt[("in", c, q)])
                        if q >= 2:
                            S.wait(t, "act", mkA_cp[(c, q - 2)])
                        else:
                            S.wait(t, "act")
                            S.wait(t, "dve")
                        for ddb in range(4):
                            for mc in range(NM):
                                ins = t.matmul(ps[(q % 2) * 4 + ddb][:, 0:512],
                                               wq_sb[:, hb + mc * 512 + ddb * 128:hb + mc * 512 + ddb * 128 + 128],
                                               xn_fm[:, mc * T + t0:mc * T + t0 + TC],
                                               start=(mc == 0), stop=(mc == NM - 1))
                            S.inc(ins, "te")
                        hread[q % 2] = S.c["te"]
                        mkA_mm[(c, q)] = S.c["te"]
                    blk.tensor(body)

                    def body(a, q=q, c=c):
                        S.wait(a, "te", mkA_mm[(c, q)])
                        for ddb in range(4):
                            gi = q * 4 + ddb
                            if gi < 16:
                                S.inc(a.activation(xd[:, gi * 516 + 3:gi * 516 + 515],
                                                   ps[(q % 2) * 4 + ddb][:, 0:512], ACT.Copy), "act")
                            else:
                                zq = gi - 16
                                S.inc(a.activation(zs[:, zq * TC:(zq + 1) * TC],
                                                   ps[(q % 2) * 4 + ddb][:, 0:512], ACT.Silu), "act")
                        mkA_cp[(c, q)] = S.c["act"]
                    blk.scalar(body)

                for quar in (0, 1):
                    def body(g, quar=quar, c=c):
                        g.wait_ge(S.h["te"], hread[quar % 2])
                        S.inc(g.dma_start(out=wq_sb[:, (quar % 2) * 4096:(quar % 2) * 4096 + 4096]
                                          .rearrange("p (m d) -> p m d", d=1024),
                                          in_=dram["W_outT"][quar * 512:(quar + 1) * 512, :]
                                          .rearrange("(m p) d -> p m d", p=128)), "gdma", 16)
                        wtgt[("out", c, quar)] = S.c["gdma"]
                    blk.gpsimd(body)

                for dt_i in range(NDT):
                    def body(v, dt_i=dt_i):
                        S.wait(v, "act")
                        b0 = dt_i * 516
                        S.inc(v.tensor_scalar_mul(acc[:, :], xd[:, b0:b0 + 512],
                                                  cw_sb[:, dt_i * 4:dt_i * 4 + 1]), "dve")
                        v.drain()
                        for j in (1, 2, 3):
                            S.inc(v.scalar_tensor_tensor(acc[:, :], xd[:, b0 + j:b0 + j + 512],
                                                         cw_sb[:, dt_i * 4 + j:dt_i * 4 + j + 1],
                                                         acc[:, :], ALU.mult, ALU.add), "dve")
                            v.drain()
                        S.inc(v.tensor_copy(xd3[:, dt_i:dt_i + 1, 0:3], xd3[:, dt_i:dt_i + 1, 512:515]), "dve")
                    blk.vector(body)
                    def body(a, dt_i=dt_i):
                        S.wait(a, "dve")
                        S.inc(a.activation(uslc(dt_i), acc[:, :], ACT.Silu,
                                           bias=cb_sb[:, dt_i:dt_i + 1]), "act")
                    blk.scalar(body)

                def body(t):
                    S.wait(t, "act")
                    for ts4 in range(4):
                        for dt_i in range(NDT):
                            ins = t.matmul(ps[ts4][:, 0:33],
                                           uslc(dt_i, ts4 * 128, (ts4 + 1) * 128),
                                           wx_sb[:, dt_i * 33:(dt_i + 1) * 33],
                                           start=(dt_i == 0), stop=(dt_i == NDT - 1))
                        S.inc(ins, "te")
                blk.tensor(body)
                def body(a):
                    S.wait(a, "te")
                    for ts4 in range(4):
                        S.inc(a.activation(ssm_e[:, ts4 * 33:(ts4 + 1) * 33], ps[ts4][:, 0:33],
                                           ACT.Copy), "act")
                blk.scalar(body)
                for ts4 in range(4):
                    def body(t, ts4=ts4):
                        S.wait(t, "act")
                        S.wait(t, "dve")
                        S.inc(t.transpose(ps[4 + ts4][0:33, 0:128],
                                          ssm_e[:, ts4 * 33:(ts4 + 1) * 33], identf[:, :]), "te")
                    blk.tensor(body)
                    def body(v, ts4=ts4):
                        S.wait(v, "te")
                        S.inc(v.tensor_copy(ssm_t[:, ts4 * 128:(ts4 + 1) * 128],
                                            ps[4 + ts4][0:33, 0:128]), "dve")
                    blk.vector(body)
                def body(v):
                    v.drain()
                    S.inc(v.tensor_copy(ssm_tb[:, :], ssm_t[:, :]), "dve")
                    v.drain()
                blk.vector(body)

                def body(t):
                    S.wait(t, "dve")
                    S.inc(t.matmul(ps[1][:, 0:512], onesf[0:1, :], ssm_t[0:1, :],
                                   start=True, stop=True), "te")
                blk.tensor(body)
                for s in range(DST):
                    def body(q, s=s):
                        if s == 0:
                            S.wait(q, "dve")
                        else:
                            S.wait(q, "te")
                        S.inc(q.dma_start(out=brow[0:1, 0:512],
                                          in_=ssm_tb[1 + s:2 + s, :]), "dma", 16)
                        S.inc(q.dma_start(out=brow[0:1, 512:1024],
                                          in_=ssm_tb[17 + s:18 + s, :]), "dma", 16)
                    blk.sync(body)
                    def body(t, s=s):
                        S.wait(t, "dma")
                        if s > 0:
                            S.wait(t, "act")
                        S.inc(t.matmul(ps[2][:, 0:512], onesb[0:1, :],
                                       brow[0:1, 0:512], start=True, stop=True), "te")
                        S.inc(t.matmul(ps[3][:, 0:512], onesb[0:1, :],
                                       brow[0:1, 512:1024], start=True, stop=True), "te")
                    blk.tensor(body)
                    def body(a, s=s, c=c):
                        S.wait(a, "te")
                        S.inc(a.activation(bca[:, s * TC:(s + 1) * TC], ps[2][:, 0:512], ACT.Copy), "act")
                        S.inc(a.activation(bca[:, (16 + s) * TC:(17 + s) * TC], ps[3][:, 0:512], ACT.Copy), "act")
                        mk_bca[(c, s)] = S.c["act"]
                    blk.scalar(body)

                if c == 0 and "dbg_m_xd" in dram:
                    def body(g):
                        S.wait(g, "act")
                        S.wait(g, "dve")
                        S.inc(g.dma_start(out=dram["dbg_m_xd"][:, :], in_=xd[:, :]), "gdma", 16)
                        S.inc(g.dma_start(out=dram["dbg_m_u"].rearrange("p (c t) -> p c t", t=512),
                                          in_=xd3[:, :, 3:515]), "gdma", 16)
                        S.inc(g.dma_start(out=dram["dbg_m_ssmt"][:, :], in_=ssm_t[:, :]), "gdma", 16)
                        S.inc(g.dma_start(out=dram["dbg_m_bca"][:, :], in_=bca[:, :]), "gdma", 16)
                        S.inc(g.dma_start(out=dram["dbg_m_zs"][:, :], in_=zs[:, :]), "gdma", 16)
                        S.wait(g, "gdma")
                    blk.gpsimd(body)

                for dt_i in range(NDT):
                    key = (c, dt_i)
                    prevk = (c, dt_i - 1) if dt_i > 0 else ((c - 1, NDT - 1) if c > 0 else None)
                    def body(a, key=key, prevk=prevk, dt_i=dt_i):
                        if prevk is not None:
                            S.wait(a, "dve", mk_sp[prevk])
                        S.inc(a.activation(zzt[:, :], ps[1][:, 0:512], ACT.Identity,
                                           scale=wdt_r[:, dt_i:dt_i + 1], bias=bdt_r[:, dt_i:dt_i + 1]), "act")
                        mk_zz[key] = S.c["act"]
                    blk.scalar(body)
                    def body(v, key=key):
                        S.wait(v, "act", mk_zz[key])
                        S.inc(v.scalar_tensor_tensor(ltt[:, :], zzt[:, :], -1.0, zzt[:, :],
                                                     ALU.mult, ALU.max), "dve")
                        mk_abs[key] = S.c["dve"]
                    blk.vector(body)
                    def body(a, key=key):
                        S.wait(a, "dve", mk_abs[key])
                        S.inc(a.activation(ltt[:, :], ltt[:, :], ACT.Exp, scale=-1.0), "act")
                        a.drain()
                        S.inc(a.activation(ltt[:, :], ltt[:, :], ACT.Ln, bias=consts[:, 2:3]), "act")
                        a.drain()
                        mk_ln[key] = S.c["act"]
                    blk.scalar(body)
                    def body(v, key=key, prevk=prevk, dt_i=dt_i):
                        S.wait(v, "act", mk_ln[key])
                        S.inc(v.scalar_tensor_tensor(lns[:, :], zzt[:, :], 0.0, ltt[:, :],
                                                     ALU.max, ALU.add), "dve")
                        mk_sp[key] = S.c["dve"]
                        v.drain()
                        if prevk is not None:
                            S.wait(v, "gp", mk_bP[prevk + (DST - 1,)])
                        S.inc(v.tensor_mul(wdd[:, :], lns[:, :], uslc(dt_i)), "dve")
                        mk_wdd[key] = S.c["dve"]
                    blk.vector(body)
                    for s in range(DST):
                        pr = s % 2
                        kk = key + (s,)
                        def body(a, kk=kk, key=key, s=s, pr=pr, dt_i=dt_i):
                            if s >= 2:
                                S.wait(a, "dve", mk_scan[key + (s - 2,)])
                            else:
                                S.wait(a, "dve", mk_sp[key])
                            S.inc(a.activation(dA[:, pr * TC:(pr + 1) * TC], lns[:, :], ACT.Exp,
                                               scale=alog[:, dt_i * DST + s:dt_i * DST + s + 1]), "act")
                            mk_dA[kk] = S.c["act"]
                        blk.scalar(body)
                        def body(g, kk=kk, key=key, s=s, pr=pr, c=c, dt_i=dt_i):
                            if s >= 2:
                                g.wait_ge(S.h["dve"], mk_scan[key + (s - 2,)])
                            else:
                                g.wait_ge(S.h["dve"], mk_wdd[key])
                            if key[1] == 0:
                                g.wait_ge(S.h["act"], mk_bca[(c, s)])
                            S.inc(g.tensor_mul(bP[:, pr * TC:(pr + 1) * TC], wdd[:, :],
                                               bca[:, s * TC:(s + 1) * TC]), "gp")
                            mk_bP[kk] = S.c["gp"]
                            if s >= 1:
                                g.wait_ge(S.h["dve"], mk_scan[key + (s - 1,)])
                                sl = (s - 1) % 4
                                S.inc(g.tensor_copy(states[:, dt_i * DST + s - 1:dt_i * DST + s],
                                                    hh[:, sl * TC + TC - 1:sl * TC + TC]), "gp")
                        blk.gpsimd(body)
                        def body(v, kk=kk, key=key, prevk=prevk, s=s, pr=pr, dt_i=dt_i):
                            S.wait(v, "act", mk_dA[kk])
                            S.wait(v, "gp", mk_bP[kk])
                            S.inc(v.tensor_tensor_scan(hh[:, (s % 4) * TC:(s % 4 + 1) * TC],
                                                       dA[:, pr * TC:(pr + 1) * TC],
                                                       bP[:, pr * TC:(pr + 1) * TC],
                                                       states[:, dt_i * DST + s:dt_i * DST + s + 1],
                                                       ALU.mult, ALU.add), "dve")
                            mk_scan[kk] = S.c["dve"]
                            if s >= 1:
                                p1 = (s - 1) % 2
                                if s >= 3:
                                    tgt = mk_mm.get(key + (s - 3,))
                                elif prevk is not None:
                                    tgt = mk_mmD.get(prevk)
                                else:
                                    tgt = None
                                if tgt:
                                    S.wait(v, "te", tgt)
                                S.inc(v.tensor_mul(gg[:, p1 * TC:(p1 + 1) * TC],
                                                   hh[:, ((s - 1) % 4) * TC:((s - 1) % 4 + 1) * TC],
                                                   bca[:, (16 + s - 1) * TC:(16 + s) * TC]), "dve")
                                mk_gg[key + (s - 1,)] = S.c["dve"]
                        blk.vector(body)
                        if s >= 1 and not globals().get("SKIP_SCAN_S", False):
                            def body(t, key=key, s=s):
                                p1 = (s - 1) % 2
                                S.wait(t, "dve", mk_gg[key + (s - 1,)])
                                S.inc(t.matmul(ps[0][:, 0:512], identb[:, :],
                                               gg[:, p1 * TC:(p1 + 1) * TC],
                                               start=(s == 1), stop=False), "te")
                                mk_mm[key + (s - 1,)] = S.c["te"]
                            blk.tensor(body)
                    def body(g, key=key, dt_i=dt_i):
                        g.wait_ge(S.h["dve"], mk_scan[key + (DST - 1,)])
                        sl = (DST - 1) % 4
                        S.inc(g.tensor_copy(states[:, dt_i * DST + DST - 1:dt_i * DST + DST],
                                            hh[:, sl * TC + TC - 1:sl * TC + TC]), "gp")
                    blk.gpsimd(body)
                    def body(v, key=key, dt_i=dt_i):
                        v.drain()
                        tgt = mk_mm.get(key + (DST - 3,))
                        if tgt:
                            S.wait(v, "te", tgt)
                        S.inc(v.tensor_mul(gg[:, TC:2 * TC], hh[:, ((DST - 1) % 4) * TC:((DST - 1) % 4 + 1) * TC],
                                           bca[:, (16 + DST - 1) * TC:(16 + DST) * TC]), "dve")
                        mk_gg[key + (DST - 1,)] = S.c["dve"]
                    blk.vector(body)
                    if not globals().get("SKIP_SCAN_S", False):
                        def body(t, key=key):
                            S.wait(t, "dve", mk_gg[key + (DST - 1,)])
                            S.inc(t.matmul(ps[0][:, 0:512], identb[:, :], gg[:, TC:2 * TC],
                                           start=False, stop=False), "te")
                            mk_mm[key + (DST - 1,)] = S.c["te"]
                        blk.tensor(body)
                    def body(v, key=key, dt_i=dt_i):
                        S.wait(v, "te")
                        S.inc(v.tensor_scalar_mul(gg[:, 0:TC], uslc(dt_i),
                                                  dp_sb[:, dt_i:dt_i + 1]), "dve")
                        mk_gg[key + (DST,)] = S.c["dve"]
                    blk.vector(body)
                    def body(t, key=key):
                        S.wait(t, "dve", mk_gg[key + (DST,)])
                        sk = globals().get("SKIP_SCAN_S", False)
                        S.inc(t.matmul(ps[0][:, 0:512], identb[:, :], gg[:, 0:TC],
                                       start=sk, stop=True), "te")
                        mk_mmD[key] = S.c["te"]
                    blk.tensor(body)
                    def body(v, key=key, dt_i=dt_i):
                        S.wait(v, "te")
                        S.inc(v.tensor_mul(xd[:, dt_i * 516 + 3:dt_i * 516 + 515], ps[0][:, 0:512],
                                           zs[:, dt_i * TC:(dt_i + 1) * TC]), "dve")
                        mk_yev[key] = S.c["dve"]
                    blk.vector(body)

                for quar in range(4):
                    hb = (quar % 2) * 4096
                    def body(t, quar=quar, c=c, hb=hb):
                        S.wait(t, "gdma", wtgt[("out", c, quar)])
                        S.wait(t, "dve")
                        for ts4 in range(4):
                            for mh in range(2):
                                for db in range(4):
                                    dt_i = quar * 4 + db
                                    ins = t.matmul(ps[ts4 * 2 + mh][:, 0:512],
                                                   xd[:, dt_i * 516 + 3 + ts4 * 128:dt_i * 516 + 3 + (ts4 + 1) * 128],
                                                   wq_sb[:, hb + db * 1024 + mh * 512:hb + db * 1024 + (mh + 1) * 512],
                                                   start=(quar == 0 and db == 0),
                                                   stop=(quar == 3 and db == 3))
                                S.inc(ins, "te")
                        hread[quar % 2] = S.c["te"]
                        mkF_mm[(c, quar)] = S.c["te"]
                    blk.tensor(body)
                    if quar < 2:
                        def body(g, quar=quar, c=c):
                            g.wait_ge(S.h["te"], mkF_mm[(c, quar)])
                            nq = quar + 2
                            S.inc(g.dma_start(out=wq_sb[:, (nq % 2) * 4096:(nq % 2) * 4096 + 4096]
                                              .rearrange("p (m d) -> p m d", d=1024),
                                              in_=dram["W_outT"][nq * 512:(nq + 1) * 512, :]
                                              .rearrange("(m p) d -> p m d", p=128)), "gdma", 16)
                            wtgt[("out", c, nq)] = S.c["gdma"]
                        blk.gpsimd(body)
                def body(v, c=c):
                    S.wait(v, "te")
                    for ts4 in range(4):
                        tg = c * 4 + ts4
                        for mh in range(2):
                            S.inc(v.tensor_add(xres[:, tg * M + mh * 512:tg * M + (mh + 1) * 512],
                                               ps[ts4 * 2 + mh][:, 0:512],
                                               xres[:, tg * M + mh * 512:tg * M + (mh + 1) * 512]), "dve")
                    v.drain()
                blk.vector(body)

        L.close()
        PS.close()

    if not globals().get('SKIP_MAMBA', False):
        mamba()

    def l1ffn():
        LH = 8
        LOGK = float(np.log(1152.0))
        L = ExitStack()
        def lsb(name, shape, dt):
            return L.enter_context(nc.sbuf_tensor(name, shape, dt))
        onescol_f = lsb("f_ones", [128, 1], F32)
        onesall = lsb("f_onesall", [128, 128], F32)
        rowbank = lsb("f_rowb", [128, T], F32)
        sqscr = lsb("f_sq", [128, T], BF16)
        rfacc = lsb("f_rfac", [128, NT], F32)
        rfac8 = lsb("f_rfac8", [128, NT], F32)
        gcol = lsb("f_gcol", [128, NM], F32)
        bb = lsb("f_bb", [128, 16], F32)
        XX = lsb("f_XX", [128, 16], F32)
        score = lsb("f_score", [128, 16], F32)
        mask2d = lsb("f_mask", [128, 16], F32)
        eqm = lsb("f_eqm", [128, 16], BF16)
        etile = lsb("f_et", [128, 16], F32)
        zzrow = lsb("f_zzr", [1, 16], F32)
        tiny = lsb("f_tiny", [1, 8], F32)
        st2 = lsb("f_st2", [128, 8], F32)
        tcol = lsb("f_tcol", [128, 8], F32)
        qn2 = lsb("f_qn2", [128, 2 * NT], F32)
        q_tok = lsb("f_qtok", [128, 512], BF16)
        k_tok = lsb("f_ktok", [128, 512], BF16)
        v_tok = lsb("f_vtok", [128, 512], BF16)
        p_sb = lsb("f_psb", [128, 128], F32)
        pT_sb = lsb("f_pTsb", [128, 128], BF16)
        qfw = lsb("f_qfw", [128, 512], BF16)
        kfw = lsb("f_kfw", [128, 512], BF16)
        ogl = lsb("f_ogl", [128, 512], BF16)
        ofw = lsb("f_ofw", [128, 512], BF16)

        PS = ExitStack()
        def psb(name, shape):
            return PS.enter_context(nc.psum_tensor(name, shape, F32))
        ps_a = psb("f_pa", [128, 512])
        ps_b = psb("f_pb", [128, 512])
        ps_c = psb("f_pc", [128, 512])
        ps_d = psb("f_pd", [128, 512])
        ps_e = psb("f_pe", [128, 2048])
        ps_q = (ps_a, ps_b, ps_c, ps_d)
        wtgt = {}

        with nc.Block() as blk:
            def body(q):
                S.inc(q.dma_start(out=onescol_f[:, :], in_=dram["onescol_f"][:, :]), "dma", 16)
            blk.sync(body)
            def body(g):
                S.inc(g.dma_start(out=gcol.rearrange("p (c o) -> p c o", o=1),
                                  in_=dram["q_rt"].rearrange("(c p) o -> p c o", p=128),
                                  allow_slow_non_contiguous=True), "gdma", 16)
                wtgt["qrt"] = S.c["gdma"]
            blk.gpsimd(body)
            def body(v):
                S.inc(v.memset(zzrow[:, :], 0.0), "dve")
                S.inc(v.memset(onesall[:, :], 1.0), "dve")
                v.drain()
            blk.vector(body)
            def body(v):
                S.wait(v, "te")
                S.wait(v, "act")
                S.inc(v.reduce_max(st2[:, 0:1], s2d[:, :], axis=AX.X), "dve")
                v.drain()
            blk.vector(body)
            def body(t):
                S.wait(t, "dve")
                S.inc(t.transpose(ps_a[0:1, 0:128], st2[:, 0:1], identf[:, :]), "te")
            blk.tensor(body)
            def body(v):
                S.wait(v, "te")
                S.inc(v.reduce_max(tiny[0:1, 0:1], ps_a[0:1, 0:128], axis=AX.X, negate=True), "dve")
                v.drain()
            blk.vector(body)
            for mb in range(NM):
                def body(a, mb=mb):
                    S.wait(a, "te")
                    S.inc(a.activation(sqscr[:, :], xn_fm[:, mb * T:(mb + 1) * T], ACT.Square), "act")
                blk.scalar(body)
                def body(t, mb=mb):
                    S.wait(t, "act")
                    for q4 in range(4):
                        S.inc(t.matmul(ps_e[0:1, q4 * 512:(q4 + 1) * 512], onescol_b[:, 0:1],
                                       sqscr[:, q4 * 512:(q4 + 1) * 512],
                                       start=(mb == 0), stop=(mb == NM - 1)), "te")
                blk.tensor(body)
            def body(v):
                S.wait(v, "te")
                S.inc(v.tensor_scalar_max(rowbank[0:1, 0:T], ps_e[0:1, 0:T], 1e-24), "dve")
                v.drain()
            blk.vector(body)
            def body(a):
                S.wait(a, "dve")
                S.inc(a.activation(rowbank[0:1, 0:T], rowbank[0:1, 0:T], ACT.Ln), "act")
                a.drain()
                S.inc(a.activation(rowbank[0:1, 0:T], rowbank[0:1, 0:T], ACT.Exp,
                                   scale=-0.5, bias=consts[0:1, 1:2]), "act")
            blk.scalar(body)
            for grp in range(4):
                def body(t, grp=grp):
                    S.wait(t, "dve")
                    S.wait(t, "act")
                    for q4 in range(4):
                        tt = grp * 4 + q4
                        S.inc(t.transpose(ps_q[q4][:, 0:1],
                                          rowbank[0:1, tt * 128:(tt + 1) * 128],
                                          identf[0:1, 0:1]), "te")
                blk.tensor(body)
                def body(v, grp=grp):
                    S.wait(v, "te")
                    for q4 in range(4):
                        tt = grp * 4 + q4
                        S.inc(v.tensor_copy(rfacc[:, tt:tt + 1], ps_q[q4][:, 0:1]), "dve")
                    v.drain()
                blk.vector(body)
            def body(v):
                S.inc(v.tensor_scalar_mul(rfac8[:, :], rfacc[:, :], 0.125), "dve")
                S.inc(v.tensor_scalar_mul(bb[:, :], s2d[:, :], -1.0), "dve")
                v.drain()
            blk.vector(body)
            def body(t):
                S.wait(t, "dve")
                S.inc(t.matmul(ps_a[:, 0:1], onesf[0:1, :], tiny[0:1, 0:1], start=True, stop=True), "te")
            blk.tensor(body)
            def body(v):
                S.wait(v, "te")
                S.inc(v.tensor_copy(tcol[:, 0:1], ps_a[:, 0:1]), "dve")
                v.drain()
            blk.vector(body)

        with nc.Block() as blk:
            for it in range(50):
                def body(v, it=it):
                    if it > 0:
                        S.wait(v, "te")
                    S.inc(v.tensor_add(XX[:, :], s2d[:, :], bb[:, :]), "dve")
                    v.drain()
                    S.inc(v.reduce_max(st2[:, 2:3], XX[:, :], axis=AX.X), "dve")
                    v.drain()
                blk.vector(body)
                def body(t):
                    S.wait(t, "dve")
                    S.inc(t.transpose(ps_c[0:1, 0:128], st2[:, 2:3], identf[:, :]), "te")
                blk.tensor(body)
                def body(v):
                    S.wait(v, "te")
                    S.inc(v.reduce_max(tiny[0:1, 2:3], ps_c[0:1, 0:128], axis=AX.X, negate=True), "dve")
                    v.drain()
                blk.vector(body)
                def body(t):
                    S.wait(t, "dve")
                    S.inc(t.matmul(ps_d[:, 0:1], onesf[0:1, :], tiny[0:1, 2:3], start=True, stop=True), "te")
                blk.tensor(body)
                def body(v):
                    S.wait(v, "te")
                    S.inc(v.tensor_copy(tcol[:, 0:1], ps_d[:, 0:1]), "dve")
                    v.drain()
                blk.vector(body)
                def body(a):
                    S.wait(a, "dve")
                    S.inc(a.activation(etile[:, :], XX[:, :], ACT.Exp, bias=tcol[:, 0:1],
                                       accum_out=st2[:, 0:1]), "act")
                blk.scalar(body)
                def body(t):
                    S.wait(t, "act")
                    S.inc(t.matmul(ps_b[0:1, 0:1], onescol_f[:, 0:1], st2[:, 0:1],
                                   start=True, stop=True), "te")
                blk.tensor(body)
                def body(a):
                    S.wait(a, "te")
                    S.inc(a.activation(tiny[0:1, 1:2], ps_b[0:1, 0:1], ACT.Ln), "act")
                blk.scalar(body)
                def body(v):
                    S.wait(v, "act")
                    S.inc(v.tensor_sub(tiny[0:1, 3:4], tiny[0:1, 2:3], tiny[0:1, 1:2]), "dve")
                    v.drain()
                    S.inc(v.tensor_scalar_add(tiny[0:1, 3:4], tiny[0:1, 3:4], LOGK), "dve")
                    v.drain()
                blk.vector(body)
                def body(t):
                    S.wait(t, "dve")
                    S.inc(t.matmul(ps_a[:, 0:1], onesf[0:1, :], tiny[0:1, 3:4], start=True, stop=True), "te")
                blk.tensor(body)
                def body(v):
                    S.wait(v, "te")
                    S.inc(v.tensor_copy(tcol[:, 1:2], ps_a[:, 0:1]), "dve")
                    v.drain()
                    S.inc(v.tensor_scalar_add(XX[:, :], s2d[:, :], tcol[:, 1:2]), "dve")
                    v.drain()
                    S.inc(v.tensor_scalar(bb[:, :], XX[:, :], -1.0, 0.0, ALU.mult, ALU.min), "dve")
                    v.drain()
                blk.vector(body)
            def body(v):
                S.inc(v.tensor_scalar_min(score[:, :], XX[:, :], 0.0), "dve")
                v.drain()
            blk.vector(body)
            def body(a):
                S.wait(a, "dve")
                S.inc(a.activation(score[:, :], score[:, :], ACT.Exp), "act")
            blk.scalar(body)

        with nc.Block() as blk:
            def body(v):
                S.wait(v, "act")
                S.inc(v.memset(tiny[0:1, 4:5], 0.0), "dve")
                S.inc(v.memset(tiny[0:1, 5:6], 1.5), "dve")
                v.drain()
            blk.vector(body)
            for it in range(40):
                def body(v):
                    S.wait(v, "te")
                    S.inc(v.tensor_add(tiny[0:1, 6:7], tiny[0:1, 4:5], tiny[0:1, 5:6]), "dve")
                    v.drain()
                    S.inc(v.tensor_scalar_mul(tiny[0:1, 6:7], tiny[0:1, 6:7], 0.5), "dve")
                    v.drain()
                blk.vector(body)
                def body(t):
                    S.wait(t, "dve")
                    S.inc(t.matmul(ps_a[:, 0:1], onesf[0:1, :], tiny[0:1, 6:7], start=True, stop=True), "te")
                blk.tensor(body)
                def body(v):
                    S.wait(v, "te")
                    S.inc(v.tensor_copy(tcol[:, 2:3], ps_a[:, 0:1]), "dve")
                    v.drain()
                    S.inc(v.tensor_scalar(etile[:, :], score[:, :], tcol[:, 2:3], None,
                                          ALU.is_gt), "dve")
                    v.drain()
                    S.inc(v.reduce_sum(st2[:, 1:2], etile[:, :], axis=AX.X), "dve")
                    v.drain()
                blk.vector(body)
                def body(t):
                    S.wait(t, "dve")
                    S.inc(t.matmul(ps_b[0:1, 0:1], onescol_f[:, 0:1], st2[:, 1:2],
                                   start=True, stop=True), "te")
                blk.tensor(body)
                def body(v):
                    S.wait(v, "te")
                    S.inc(v.tensor_scalar(tiny[0:1, 7:8], ps_b[0:1, 0:1], 1024.0, None, ALU.is_ge), "dve")
                    v.drain()
                    S.inc(v.tensor_sub(st2[0:1, 2:3], tiny[0:1, 6:7], tiny[0:1, 4:5]), "dve")
                    S.inc(v.tensor_sub(st2[0:1, 3:4], tiny[0:1, 6:7], tiny[0:1, 5:6]), "dve")
                    v.drain()
                    S.inc(v.scalar_tensor_tensor(tiny[0:1, 4:5], st2[0:1, 2:3], tiny[0:1, 7:8],
                                                 tiny[0:1, 4:5], ALU.mult, ALU.add), "dve")
                    v.drain()
                    S.inc(v.tensor_scalar(st2[0:1, 4:5], tiny[0:1, 7:8], -1.0, 1.0, ALU.mult, ALU.add), "dve")
                    v.drain()
                    S.inc(v.scalar_tensor_tensor(tiny[0:1, 5:6], st2[0:1, 3:4], st2[0:1, 4:5],
                                                 tiny[0:1, 5:6], ALU.mult, ALU.add), "dve")
                    v.drain()
                blk.vector(body)
            def body(t):
                S.wait(t, "dve")
                S.inc(t.matmul(ps_a[:, 0:1], onesf[0:1, :], tiny[0:1, 5:6], start=True, stop=True), "te")
                S.inc(t.matmul(ps_b[:, 0:1], onesf[0:1, :], tiny[0:1, 4:5], start=True, stop=True), "te")
            blk.tensor(body)
            def body(v):
                S.wait(v, "te")
                S.inc(v.tensor_copy(tcol[:, 3:4], ps_a[:, 0:1]), "dve")
                S.inc(v.tensor_copy(tcol[:, 4:5], ps_b[:, 0:1]), "dve")
                v.drain()
                S.inc(v.tensor_scalar(mask2d[:, :], score[:, :], tcol[:, 3:4], None, ALU.is_gt), "dve")
                v.drain()
                S.inc(v.reduce_sum(st2[:, 5:6], mask2d[:, :], axis=AX.X), "dve")
                v.drain()
                S.inc(v.tensor_scalar(etile[:, :], score[:, :], tcol[:, 4:5], None, ALU.is_gt), "dve")
                S.inc(v.tensor_scalar(XX[:, :], score[:, :], tcol[:, 3:4], None, ALU.is_le), "dve")
                v.drain()
                S.inc(v.tensor_mul(eqm[:, :], etile[:, :], XX[:, :]), "dve")
                v.drain()
            blk.vector(body)
            def body(t):
                S.wait(t, "dve")
                S.inc(t.matmul(ps_c[0:1, 0:1], onescol_f[:, 0:1], st2[:, 5:6], start=True, stop=True), "te")
                S.inc(t.matmul(ps_d[:, 0:16], trilT_sb[:, 0:128], eqm[:, :], start=True, stop=False), "te")
                S.inc(t.matmul(ps_e[0:1, 0:16], onescol_b[:, 0:1], eqm[:, :], start=True, stop=True), "te")
            blk.tensor(body)
            def body(v):
                S.wait(v, "te")
                S.inc(v.tensor_scalar(tiny[0:1, 1:2], ps_c[0:1, 0:1], -1.0, 1024.0, ALU.mult, ALU.add), "dve")
                v.drain()
                S.inc(v.tensor_tensor_scan(etile[0:1, 0:16], ps_e[0:1, 0:16], zzrow[0:1, 0:16],
                                           0.0, ALU.add, ALU.add), "dve")
                v.drain()
                S.inc(v.tensor_sub(etile[0:1, 0:16], etile[0:1, 0:16], ps_e[0:1, 0:16]), "dve")
                v.drain()
            blk.vector(body)
            def body(t):
                S.wait(t, "dve")
                S.inc(t.matmul(ps_a[:, 0:1], onesf[0:1, :], tiny[0:1, 1:2], start=True, stop=True), "te")
                S.inc(t.matmul(ps_d[:, 0:16], onesf[0:1, :], etile[0:1, 0:16], start=False, stop=True), "te")
            blk.tensor(body)
            def body(v):
                S.wait(v, "te")
                S.inc(v.tensor_copy(tcol[:, 4:5], ps_a[:, 0:1]), "dve")
                v.drain()
                S.inc(v.tensor_scalar(XX[:, :], ps_d[:, 0:16], tcol[:, 4:5], None, ALU.is_le), "dve")
                v.drain()
                S.inc(v.tensor_mul(XX[:, :], XX[:, :], eqm[:, :]), "dve")
                v.drain()
                S.inc(v.tensor_add(mask2d[:, :], mask2d[:, :], XX[:, :]), "dve")
                v.drain()
            blk.vector(body)
            if "dbg_mask" in dram:
                def body(g):
                    S.wait(g, "dve")
                    S.inc(g.dma_start(out=dram["dbg_mask"][:, :], in_=mask2d[:, :]), "gdma", 16)
                    S.inc(g.dma_start(out=dram["dbg_score"][:, :], in_=score[:, :]), "gdma", 16)
                    S.inc(g.dma_start(out=dram["dbg_s2d"][:, :], in_=s2d[:, :]), "gdma", 16)
                    S.inc(g.dma_start(out=dram["dbg_tiny"][:, :], in_=tiny[:, :]), "gdma", 16)
                    S.wait(g, "gdma")
                blk.gpsimd(body)

        with ExitStack() as WL:
            wql = WL.enter_context(nc.sbuf_tensor("f_wql", [128, NM * 512], BF16))
            wkvl = WL.enter_context(nc.sbuf_tensor("f_wkvl", [128, NM * 1024], BF16))
            wol = WL.enter_context(nc.sbuf_tensor("f_wol", [128, 4 * M], BF16))
            with nc.Block() as blk:
                def body(g):
                    S.inc(g.dma_start(out=wql.rearrange("p (c h) -> p c h", h=512),
                                      in_=dram["wq_l"].rearrange("(c p) h -> p c h", p=128)), "gdma", 16)
                    S.inc(g.dma_start(out=wkvl.rearrange("p (c h) -> p c h", h=1024),
                                      in_=dram["wkv_l"].rearrange("(c p) h -> p c h", p=128)), "gdma", 16)
                    S.inc(g.dma_start(out=wol.rearrange("p (c h) -> p c h", h=1024),
                                      in_=dram["wo_l"].rearrange("(c p) h -> p c h", p=128)), "gdma", 16)
                    S.inc(g.dma_start(out=gcol.rearrange("p (c o) -> p c o", o=1),
                                      in_=dram["gam_l"].rearrange("(c p) o -> p c o", p=128),
                                      allow_slow_non_contiguous=True), "gdma", 16)
                    wtgt["wl"] = S.c["gdma"]
                blk.gpsimd(body)
                def body(v):
                    S.wait(v, "gdma", wtgt["wl"])
                    for mc in range(NM):
                        S.inc(v.tensor_scalar_mul(wql[:, mc * 512:(mc + 1) * 512],
                                                  wql[:, mc * 512:(mc + 1) * 512], gcol[:, mc:mc + 1]), "dve")
                        S.inc(v.tensor_scalar_mul(wkvl[:, mc * M:(mc + 1) * M],
                                                  wkvl[:, mc * M:(mc + 1) * M], gcol[:, mc:mc + 1]), "dve")
                    v.drain()
                blk.vector(body)
                for tt in range(NT):
                    def body(t, tt=tt):
                        S.wait(t, "dve")
                        S.wait(t, "act")
                        for mc in range(NM):
                            ins = t.matmul(ps_a[:, 0:512],
                                           xn_fm[:, mc * T + tt * 128:mc * T + (tt + 1) * 128],
                                           wql[:, mc * 512:(mc + 1) * 512],
                                           start=(mc == 0), stop=(mc == NM - 1))
                        S.inc(ins, "te")
                        for half in range(2):
                            for mc in range(NM):
                                ins = t.matmul((ps_b, ps_c)[half][:, 0:512],
                                               xn_fm[:, mc * T + tt * 128:mc * T + (tt + 1) * 128],
                                               wkvl[:, mc * M + half * 512:mc * M + (half + 1) * 512],
                                               start=(mc == 0), stop=(mc == NM - 1))
                            S.inc(ins, "te")
                    blk.tensor(body)
                    def body(a, tt=tt):
                        S.wait(a, "te")
                        S.wait(a, "dma")
                        S.inc(a.activation(q_tok[:, :], ps_a[:, 0:512], ACT.Copy,
                                           scale=rfac8[:, tt:tt + 1]), "act")
                        S.inc(a.activation(k_tok[:, :], ps_b[:, 0:512], ACT.Copy,
                                           scale=rfacc[:, tt:tt + 1]), "act")
                        S.inc(a.activation(v_tok[:, :], ps_c[:, 0:512], ACT.Copy,
                                           scale=rfacc[:, tt:tt + 1]), "act")
                    blk.scalar(body)
                    def body(q, tt=tt):
                        S.wait(q, "act")
                        S.wait(q, "te")
                        for db in range(4):
                            S.inc(q.dma_start_transpose(qfw[:, db * 128:(db + 1) * 128],
                                                        q_tok[:, db * 128:(db + 1) * 128]), "dma", 16)
                            S.inc(q.dma_start_transpose(kfw[:, db * 128:(db + 1) * 128],
                                                        k_tok[:, db * 128:(db + 1) * 128]), "dma", 16)
                    blk.sync(body)
                    for h in range(LH):
                        po = (h % 2) * 64
                        sg = h // 2
                        def body(t, h=h, po=po, sg=sg):
                            if h == 0:
                                S.wait(t, "dma")
                            S.wait(t, "dve")
                            S.inc(t.matmul(ps_d[:, 0:128],
                                           qfw[po:po + 64, sg * 128:(sg + 1) * 128],
                                           kfw[po:po + 64, sg * 128:(sg + 1) * 128],
                                           start=True, stop=True), "te")
                        blk.tensor(body)
                        def body(v, h=h):
                            S.wait(v, "te")
                            S.inc(v.reduce_max(st2[:, 0:1], ps_d[:, 0:128], axis=AX.X, negate=True), "dve")
                            v.drain()
                        blk.vector(body)
                        def body(a, h=h):
                            S.wait(a, "dve")
                            S.inc(a.activation(p_sb[:, :], ps_d[:, 0:128], ACT.Exp,
                                               bias=st2[:, 0:1], accum_out=st2[:, 1:2]), "act")
                        blk.scalar(body)
                        def body(v, h=h):
                            S.wait(v, "act")
                            S.inc(v.reciprocal(st2[:, 2:3], st2[:, 1:2]), "dve")
                            v.drain()
                        blk.vector(body)
                        def body(t, h=h):
                            S.wait(t, "dve")
                            S.inc(t.transpose(ps_a[:, 0:128], p_sb[:, :], identf[:, :]), "te")
                        blk.tensor(body)
                        def body(a, h=h):
                            S.wait(a, "te")
                            S.inc(a.activation(pT_sb[:, :], ps_a[:, 0:128], ACT.Copy), "act")
                        blk.scalar(body)
                        def body(t, h=h):
                            S.wait(t, "act")
                            S.inc(t.matmul(ps_b[:, 0:64], pT_sb[:, :],
                                           v_tok[:, h * 64:(h + 1) * 64], start=True, stop=True), "te")
                        blk.tensor(body)
                        def body(v, h=h):
                            S.wait(v, "te")
                            S.inc(v.tensor_scalar_mul(ogl[:, h * 64:(h + 1) * 64],
                                                      ps_b[:, 0:64], st2[:, 2:3]), "dve")
                            v.drain()
                        blk.vector(body)
                    def body(q, tt=tt):
                        S.wait(q, "dve")
                        for db in range(4):
                            S.inc(q.dma_start_transpose(ofw[:, db * 128:(db + 1) * 128],
                                                        ogl[:, db * 128:(db + 1) * 128]), "dma", 16)
                    blk.sync(body)
                    def body(t, tt=tt):
                        S.wait(t, "dma")
                        for mh in range(2):
                            for sg4 in range(4):
                                ins = t.matmul((ps_c, ps_d)[mh][:, 0:512],
                                               ofw[:, sg4 * 128:(sg4 + 1) * 128],
                                               wol[:, sg4 * M + mh * 512:sg4 * M + (mh + 1) * 512],
                                               start=(sg4 == 0), stop=(sg4 == 3))
                            S.inc(ins, "te")
                    blk.tensor(body)
                    def body(v, tt=tt):
                        S.wait(v, "te")
                        for mh in range(2):
                            S.inc(v.tensor_add(xres[:, tt * M + mh * 512:tt * M + (mh + 1) * 512],
                                               (ps_c, ps_d)[mh][:, 0:512],
                                               xres[:, tt * M + mh * 512:tt * M + (mh + 1) * 512]), "dve")
                        v.drain()
                    blk.vector(body)

        qfh = lsb("f_qfh", [128, 4 * T], BF16)
        kfh = lsb("f_kfh", [128, 4 * T], BF16)
        v_h = lsb("f_vh", [128, NT * 512], BF16)

        with ExitStack() as WH:
            wqh = WH.enter_context(nc.sbuf_tensor("f_wqh", [128, NM * 512], BF16))
            wkvh = WH.enter_context(nc.sbuf_tensor("f_wkvh", [128, NM * 1024], BF16))
            with nc.Block() as blk:
                def body(g):
                    S.inc(g.dma_start(out=wqh.rearrange("p (c h) -> p c h", h=512),
                                      in_=dram["wq_h"].rearrange("(c p) h -> p c h", p=128)), "gdma", 16)
                    S.inc(g.dma_start(out=wkvh.rearrange("p (c h) -> p c h", h=1024),
                                      in_=dram["wkv_h"].rearrange("(c p) h -> p c h", p=128)), "gdma", 16)
                    S.inc(g.dma_start(out=gcol.rearrange("p (c o) -> p c o", o=1),
                                      in_=dram["gam_h"].rearrange("(c p) o -> p c o", p=128),
                                      allow_slow_non_contiguous=True), "gdma", 16)
                    wtgt["wh"] = S.c["gdma"]
                blk.gpsimd(body)
                def body(v):
                    S.wait(v, "gdma", wtgt["wh"])
                    S.wait(v, "act")
                    S.wait(v, "te")
                    for mc in range(NM):
                        S.inc(v.tensor_scalar_mul(wqh[:, mc * 512:(mc + 1) * 512],
                                                  wqh[:, mc * 512:(mc + 1) * 512], gcol[:, mc:mc + 1]), "dve")
                        S.inc(v.tensor_scalar_mul(wkvh[:, mc * M:(mc + 1) * M],
                                                  wkvh[:, mc * M:(mc + 1) * M], gcol[:, mc:mc + 1]), "dve")
                    v.drain()
                blk.vector(body)
                for tt in range(NT):
                    def body(t, tt=tt):
                        S.wait(t, "dve")
                        S.wait(t, "act")
                        if tt > 0:
                            S.wait(t, "dma")
                        for mc in range(NM):
                            ins = t.matmul(ps_a[:, 0:512],
                                           xn_fm[:, mc * T + tt * 128:mc * T + (tt + 1) * 128],
                                           wqh[:, mc * 512:(mc + 1) * 512],
                                           start=(mc == 0), stop=(mc == NM - 1))
                        S.inc(ins, "te")
                        for half in range(2):
                            for mc in range(NM):
                                ins = t.matmul((ps_b, ps_c)[half][:, 0:512],
                                               xn_fm[:, mc * T + tt * 128:mc * T + (tt + 1) * 128],
                                               wkvh[:, mc * M + half * 512:mc * M + (half + 1) * 512],
                                               start=(mc == 0), stop=(mc == NM - 1))
                            S.inc(ins, "te")
                    blk.tensor(body)
                    def body(a, tt=tt):
                        S.wait(a, "te")
                        S.inc(a.activation(q_tok[:, :], ps_a[:, 0:512], ACT.Copy,
                                           scale=rfac8[:, tt:tt + 1]), "act")
                        S.inc(a.activation(k_tok[:, :], ps_b[:, 0:512], ACT.Copy,
                                           scale=rfacc[:, tt:tt + 1]), "act")
                        S.inc(a.activation(v_h[:, tt * 512:(tt + 1) * 512], ps_c[:, 0:512], ACT.Copy,
                                           scale=rfacc[:, tt:tt + 1]), "act")
                        a.drain()
                        S.inc(a.activation(sqscr[:, 0:512], q_tok[:, :], ACT.Square,
                                           accum_out=qn2[:, tt:tt + 1]), "act")
                        S.inc(a.activation(sqscr[:, 512:1024], k_tok[:, :], ACT.Square,
                                           accum_out=qn2[:, NT + tt:NT + tt + 1]), "act")
                    blk.scalar(body)
                    def body(q, tt=tt):
                        S.wait(q, "act")
                        for db in range(4):
                            S.inc(q.dma_start_transpose(
                                qfh[:, db * T + tt * 128:db * T + (tt + 1) * 128],
                                q_tok[:, db * 128:(db + 1) * 128]), "dma", 16)
                            S.inc(q.dma_start_transpose(
                                kfh[:, db * T + tt * 128:db * T + (tt + 1) * 128],
                                k_tok[:, db * 128:(db + 1) * 128]), "dma", 16)
                    blk.sync(body)

        pT_h = lsb("f_pTh", [128, T], BF16)
        o_sbh = lsb("f_osbh", [128, T], BF16)
        o_fmh = lsb("f_ofmh", [128, 4 * T], BF16)
        if True:
            with nc.Block() as blk:
                def body(v):
                    S.wait(v, "act")
                    S.inc(v.reduce_max(st2[:, 0:1], qn2[:, 0:NT], axis=AX.X), "dve")
                    S.inc(v.reduce_max(st2[:, 1:2], qn2[:, NT:2 * NT], axis=AX.X), "dve")
                    v.drain()
                blk.vector(body)
                def body(t):
                    S.wait(t, "dve")
                    S.inc(t.transpose(ps_a[0:1, 0:128], st2[:, 0:1], identf[:, :]), "te")
                    S.inc(t.transpose(ps_b[0:1, 0:128], st2[:, 1:2], identf[:, :]), "te")
                blk.tensor(body)
                def body(v):
                    S.wait(v, "te")
                    S.inc(v.reduce_max(tiny[0:1, 2:3], ps_a[0:1, 0:128], axis=AX.X), "dve")
                    S.inc(v.reduce_max(tiny[0:1, 3:4], ps_b[0:1, 0:128], axis=AX.X), "dve")
                    v.drain()
                blk.vector(body)
                def body(a):
                    S.wait(a, "dve")
                    S.inc(a.activation(tiny[0:1, 2:3], tiny[0:1, 2:3], ACT.Ln), "act")
                    a.drain()
                    S.inc(a.activation(tiny[0:1, 3:4], tiny[0:1, 3:4], ACT.Ln), "act")
                    a.drain()
                blk.scalar(body)
                def body(v):
                    S.wait(v, "act")
                    S.inc(v.tensor_add(tiny[0:1, 2:3], tiny[0:1, 2:3], tiny[0:1, 3:4]), "dve")
                    v.drain()
                blk.vector(body)
                def body(a):
                    S.wait(a, "dve")
                    S.inc(a.activation(tiny[0:1, 2:3], tiny[0:1, 2:3], ACT.Exp, scale=0.5,
                                       bias=consts[0:1, 3:4]), "act")
                blk.scalar(body)
                def body(v):
                    S.wait(v, "act")
                    S.inc(v.tensor_scalar_mul(tiny[0:1, 2:3], tiny[0:1, 2:3], -1.0), "dve")
                    v.drain()
                blk.vector(body)
                def body(t):
                    S.wait(t, "dve")
                    S.inc(t.matmul(ps_a[:, 0:1], onesf[0:1, :], tiny[0:1, 2:3], start=True, stop=True), "te")
                blk.tensor(body)
                def body(v):
                    S.wait(v, "te")
                    S.inc(v.tensor_copy(tcol[:, 3:4], ps_a[:, 0:1]), "dve")
                    v.drain()
                blk.vector(body)

            with nc.Block() as blk:
                for h in range(LH):
                    po = (h % 2) * 64
                    sg = h // 2
                    srow = 64 - po
                    for jt in range(NT):
                        def body(t, h=h, po=po, sg=sg, jt=jt):
                            S.wait(t, "act")
                            S.wait(t, "dve")
                            if h == 0 and jt == 0:
                                S.wait(t, "dma")
                            for ib in range(4):
                                S.inc(t.matmul(ps_e[:, ib * 512:(ib + 1) * 512],
                                               kfh[po:po + 64, sg * T + jt * 128:sg * T + (jt + 1) * 128],
                                               qfh[po:po + 64, sg * T + ib * 512:sg * T + (ib + 1) * 512],
                                               start=True, stop=True), "te")
                        blk.tensor(body)
                        def body(a, h=h, jt=jt):
                            S.wait(a, "te")
                            S.inc(a.activation(pT_h[:, :], ps_e[:, 0:2048], ACT.Exp,
                                               bias=tcol[:, 3:4]), "act")
                        blk.scalar(body)
                        def body(t, h=h, po=po, jt=jt, srow=srow):
                            S.wait(t, "act")
                            for ib in range(4):
                                S.inc(t.matmul(ps_q[ib][po:po + 64, 0:512],
                                               v_h[:, jt * 512 + h * 64:jt * 512 + (h + 1) * 64],
                                               pT_h[:, ib * 512:(ib + 1) * 512],
                                               start=(jt == 0), stop=(jt == NT - 1)), "te")
                                S.inc(t.matmul(ps_q[ib][srow:srow + 1, 0:512],
                                               onescol_b[:, 0:1],
                                               pT_h[:, ib * 512:(ib + 1) * 512],
                                               start=(jt == 0), stop=(jt == NT - 1)), "te")
                        blk.tensor(body)
                    def body(v, h=h, srow=srow):
                        S.wait(v, "te")
                        for ib in range(4):
                            S.inc(v.reciprocal(rowbank[srow:srow + 1, ib * 512:(ib + 1) * 512],
                                               ps_q[ib][srow:srow + 1, 0:512]), "dve")
                        v.drain()
                    blk.vector(body)
                    def body(t, h=h, po=po, srow=srow):
                        S.wait(t, "dve")
                        for ib in range(4):
                            S.inc(t.matmul(ps_e[po:po + 64, ib * 512:(ib + 1) * 512],
                                           onesall[srow:srow + 1, 0:64],
                                           rowbank[srow:srow + 1, ib * 512:(ib + 1) * 512],
                                           start=True, stop=True), "te")
                    blk.tensor(body)
                    def body(a, h=h, po=po):
                        S.wait(a, "te")
                        for ib in range(4):
                            S.inc(a.activation(o_sbh[po:po + 64, ib * 512:(ib + 1) * 512],
                                               ps_q[ib][po:po + 64, 0:512], ACT.Copy), "act")
                    blk.scalar(body)
                    def body(v, h=h, po=po, sg=sg):
                        S.wait(v, "act")
                        S.inc(v.tensor_mul(o_fmh[po:po + 64, sg * T:(sg + 1) * T],
                                           o_sbh[po:po + 64, :],
                                           ps_e[po:po + 64, 0:2048]), "dve")
                        v.drain()
                    blk.vector(body)

            with ExitStack() as WO:
                woh = WO.enter_context(nc.sbuf_tensor("f_woh", [128, 4 * M], BF16))
                with nc.Block() as blk:
                    def body(g):
                        S.inc(g.dma_start(out=woh.rearrange("p (c h) -> p c h", h=1024),
                                          in_=dram["wo_h"].rearrange("(c p) h -> p c h", p=128)), "gdma", 16)
                        wtgt["woh"] = S.c["gdma"]
                    blk.gpsimd(body)
                    for tt in range(NT):
                        def body(t, tt=tt):
                            S.wait(t, "gdma", wtgt["woh"])
                            if tt == 0:
                                S.wait(t, "dve")
                            else:
                                S.wait(t, "dve")
                            for mh in range(2):
                                for sg4 in range(4):
                                    ins = t.matmul((ps_a, ps_b)[mh][:, 0:512],
                                                   o_fmh[:, sg4 * T + tt * 128:sg4 * T + (tt + 1) * 128],
                                                   woh[:, sg4 * M + mh * 512:sg4 * M + (mh + 1) * 512],
                                                   start=(sg4 == 0), stop=(sg4 == 3))
                                S.inc(ins, "te")
                        blk.tensor(body)
                        def body(v, tt=tt):
                            S.wait(v, "te")
                            for mh in range(2):
                                S.inc(v.scalar_tensor_tensor(
                                    xres[:, tt * M + mh * 512:tt * M + (mh + 1) * 512],
                                    (ps_a, ps_b)[mh][:, 0:512],
                                    mask2d[:, tt:tt + 1],
                                    xres[:, tt * M + mh * 512:tt * M + (mh + 1) * 512],
                                    ALU.mult, ALU.add), "dve")
                            v.drain()
                        blk.vector(body)

        L.close()
        PS.close()

    if not globals().get('SKIP_L1FFN', False):
        l1ffn()

    with nc.Block() as blk:
        def body(q):
            S.wait(q, "dve")
            S.inc(q.dma_start(out=dram["out"].rearrange("(t p) m -> p t m", p=128), in_=xres.rearrange("p (t m) -> p t m", t=NT)), "dma", 16)
            S.wait(q, "dma")
        blk.sync(body)

    stack.close()
    return nc, dram


def _reorder_kvh(wkv):
    wt = wkv.T
    idx_k = np.concatenate([np.arange(h * 128, h * 128 + 64) for h in range(8)])
    idx_v = np.concatenate([np.arange(h * 128 + 64, h * 128 + 128) for h in range(8)])
    return np.ascontiguousarray(wt[:, np.concatenate([idx_k, idx_v])])


def host_inputs(inputs, core):
    x = np.ascontiguousarray(inputs["L0_attn_x"][core])
    d = {
        "x": x,
        "Wq": inputs["L0_attn_Wq"], "Wk": inputs["L0_attn_Wk"], "Wv": inputs["L0_attn_Wv"],
        "Wg": inputs["L0_attn_Wg"], "Wo": inputs["L0_attn_Wo"], "Wgt": inputs["L0_attn_Wgt"],
        "gate_w": inputs["L0_ffn_gate_w"], "w1": inputs["L0_ffn_w1"],
        "w2": inputs["L0_ffn_w2"], "w3": inputs["L0_ffn_w3"],
        "W_inT": np.ascontiguousarray(np.asarray(inputs["L1_attn_W_in"]).T),
        "conv_w": np.asarray(inputs["L1_attn_conv_w"]).reshape(2048, 4),
        "conv_b": np.asarray(inputs["L1_attn_conv_b"]).reshape(2048, 1),
        "W_xT": np.ascontiguousarray(np.asarray(inputs["L1_attn_W_x"]).T),
        "W_dt": np.asarray(inputs["L1_attn_W_dt"]).reshape(2048, 1),
        "b_dt": np.asarray(inputs["L1_attn_b_dt"]).reshape(2048, 1),
        "A_log": inputs["L1_attn_A_log"],
        "D_param": np.asarray(inputs["L1_attn_D_param"]).reshape(2048, 1),
        "W_outT": np.ascontiguousarray(np.asarray(inputs["L1_attn_W_out"]).T),
        "gam_l": np.asarray(inputs["L1_ffn_gamma_light"]).reshape(M, 1),
        "wq_l": inputs["L1_ffn_wq_light"],
        "wkv_l": inputs["L1_ffn_wkv_light"],
        "wo_l": inputs["L1_ffn_wo_light"],
        "q_rt": np.asarray(inputs["L1_ffn_q_routing_token"]).reshape(M, 1),
        "q_rt_row": np.asarray(inputs["L1_ffn_q_routing_token"]).reshape(1, M),
        "gam_h": np.asarray(inputs["L1_ffn_gamma_heavy"]).reshape(M, 1),
        "wq_h": inputs["L1_ffn_wq_heavy"],
        "wkv_h": _reorder_kvh(np.asarray(inputs["L1_ffn_wkv_heavy"])),
        "wo_h": inputs["L1_ffn_wo_heavy"],
        "onescol_f": np.ones((128, 1), dtype=np.float32),
        "ident": np.eye(128, dtype=np.float32),
        "selh": np.kron(np.eye(16, dtype=np.float32), np.ones((1, 128), np.float32)).reshape(16, 16 * 128),
        "ones": np.ones((1, 128), dtype=np.float32),
        "onescol": np.ones((128, 1), dtype=np.float32),
        "trilT": np.triu(np.ones((CH, CH), dtype=np.float32)),
    }
    return {k: np.ascontiguousarray(np.asarray(v, dtype=np.float32)) for k, v in d.items()}


def kernel(**inputs):
    nc, _ = build(debug=False)
    in_maps = [host_inputs(inputs, core) for core in range(8)]
    res = run_bass_kernel_spmd(nc, in_maps, core_ids=list(range(8)))
    out = np.stack([r["out"] for r in res.results], axis=0)
    return out.astype(np.float32)


# version 03
# speedup vs baseline: 1.2357x; 1.0090x over previous
import numpy as np
import concourse.bass as bass
import concourse.mybir as mybir
from concourse.bass_utils import run_bass_kernel_spmd
from contextlib import ExitStack

F32 = mybir.dt.float32
BF16 = mybir.dt.bfloat16
ALU = mybir.AluOpType
ACT = mybir.ActivationFunctionType
AX = mybir.AxisListType

T = 2048
M = 1024
NT = T // 128
NM = M // 128
NH = 16
DH = 64
CH = 256
NCH = T // CH
GEPS = 1e-6
LNEPS = 1e-5
LOG_SCALE = float(np.log(DH ** -0.5))


class Sems:
    def __init__(self, nc, stack, names):
        self.h = {n: stack.enter_context(nc.semaphore(n)) for n in names}
        self.c = {n: 0 for n in names}

    def inc(self, instr, name, k=1):
        self.c[name] += k
        instr.then_inc(self.h[name], k)
        return instr

    def wait(self, eng, name, val=None):
        eng.wait_ge(self.h[name], self.c[name] if val is None else val)


SKIP_GLA = False


def build(debug=False):
    nc = bass.Bass("TRN2", target_bir_lowering=False)
    stack = ExitStack()
    dram = {}

    def din(name, shape, dt=F32):
        dram[name] = nc.dram_tensor(name, shape, dt, kind="ExternalInput").ap()

    def dout(name, shape, dt=F32):
        dram[name] = nc.dram_tensor(name, shape, dt, kind="ExternalOutput").ap()

    din("x", [T, M])
    for w in ("Wq", "Wk", "Wv", "Wg", "Wo"):
        din(w, [M, M])
    din("Wgt", [M, NH])
    din("gate_w", [M, 8])
    din("w1", [8, M, 512])
    din("w2", [8, 512, M])
    din("w3", [8, M, 512])
    din("W_inT", [M, 4096])
    din("conv_w", [2048, 4])
    din("conv_b", [2048, 1])
    din("W_xT", [2048, 33])
    din("W_dt", [2048, 1])
    din("b_dt", [2048, 1])
    din("A_log", [2048, 16])
    din("D_param", [2048, 1])
    din("W_outT", [2048, M])
    din("gam_l", [M, 1])
    din("wq_l", [M, 512])
    din("wkv_l", [M, 1024])
    din("wo_l", [512, M])
    din("q_rt", [M, 1])
    din("q_rt_row", [1, M])
    din("gam_h", [M, 1])
    din("wq_h", [M, 512])
    din("wkv_h", [M, 1024])
    din("wo_h", [512, M])
    din("onescol_f", [128, 1])
    din("ident", [128, 128])
    din("ones", [1, 128])
    din("trilT", [CH, CH])
    din("selh", [NH, NH * 128])
    din("onescol", [128, 1])
    dout("out", [T, M])
    if debug:
        dout("dbg_xn", [T, M])
        for nm, shp in (("dbg_q", [128, 2048]), ("dbg_k", [128, 2048]), ("dbg_v", [128, 2048]),
                        ("dbg_bc", [NH, T]), ("dbg_bcq", [NH, CH]), ("dbg_bkneg", [NH, CH]),
                        ("dbg_mtab", [128, 160]), ("dbg_silug", [128, 2048]),
                        ("dbg_dmask", [128, 512]), ("dbg_msT", [128, 512]),
                        ("dbg_otok", [128, 2048]), ("dbg_ktm", [128, 2048]), ("dbg_vtm", [128, 2048]), ("dbg_xnfm", [128, NM * T]), ("dbg_stt", [128, 8 * NT]),
                        ("dbg_og", [128, 2048]), ("dbg_ofm", [128, 2048]), ("dbg_xres0", [128, 2048]),
                        ("dbg_m_xd", [128, 16 * 516]), ("dbg_m_u", [128, 16 * 512]),
                        ("dbg_m_ssmt", [33, 512]), ("dbg_m_bca", [128, 32 * 512]),
                        ("dbg_m_lns", [128, 512]), ("dbg_m_h", [128, 512]),
                        ("dbg_m_dA", [128, 512]), ("dbg_m_bP", [128, 512]), ("dbg_m_wdd", [128, 512]),
                        ("dbg_m_zs", [128, 16 * 512]), ("dbg_mask", [128, 16]), ("dbg_score", [128, 16]),
                        ("dbg_s2d", [128, 16]), ("dbg_tiny", [1, 8]), ("dbg_tokrep", [128, M]), ("dbg_xnt", [128, M])):
            dout(nm, shp)

    S = Sems(nc, stack, ["te", "act", "dve", "dma", "gdma", "gp"])

    def sb(name, shape, dt):
        return stack.enter_context(nc.sbuf_tensor(name, shape, dt))

    xres = sb("xres", [128, NT * M], F32)
    xn_fm = sb("xn_fm", [128, NM * T], BF16)
    identf = sb("identf", [128, 128], F32)
    identb = sb("identb", [128, 128], BF16)
    onesf = sb("onesf", [1, 128], F32)
    onesb = sb("onesb", [1, 128], BF16)
    onescol_b = sb("onescol_b", [128, 1], BF16)
    consts = sb("consts", [128, 4], F32)
    trilT_sb = sb("trilT_sb", [128, 2 * CH], BF16)
    s2d = sb("s2d_rt", [128, 16], F32)

    with nc.Block() as blk:
        def body(q):
            S.inc(q.dma_start(out=xres.rearrange("p (t m) -> p t m", t=NT), in_=dram["x"].rearrange("(t p) m -> p t m", p=128)), "dma", 16)
            S.inc(q.dma_start(out=identf[:, :], in_=dram["ident"][:, :]), "dma", 16)
            S.inc(q.dma_start(out=onesf[:, :], in_=dram["ones"][:, :]), "dma", 16)
        blk.sync(body)

        def body(v):
            S.inc(v.memset(consts[:, 0:1], GEPS), "dve")
            S.inc(v.memset(consts[:, 1:2], float(np.log(32.0))), "dve")
            S.inc(v.memset(consts[:, 2:3], 1.0), "dve")
            S.inc(v.memset(consts[:, 3:4], float(np.log(0.125))), "dve")
            v.drain()
        blk.vector(body)

        def body(g):
            S.inc(g.dma_start(out=identb[:, :], in_=dram["ident"][:, :]), "gdma", 16)
            S.inc(g.dma_start(out=onesb[:, :], in_=dram["ones"][:, :]), "gdma", 16)
            S.inc(g.dma_start(out=onescol_b[:, :], in_=dram["onescol"][:, :]), "gdma", 16)
            S.inc(g.dma_start(out=trilT_sb[:, 0:CH], in_=dram["trilT"][0:128, :]), "gdma", 16)
            S.inc(g.dma_start(out=trilT_sb[:, CH:2 * CH], in_=dram["trilT"][128:256, :]), "gdma", 16)
        blk.gpsimd(body)

    def layernorm(tag, dbg=None):
        with ExitStack() as L:
            xn_tm = L.enter_context(nc.sbuf_tensor(f"xntm_{tag}", [128, NT * M], BF16))
            scr = L.enter_context(nc.sbuf_tensor(f"lnscr_{tag}", [128, M], F32))
            stt = L.enter_context(nc.sbuf_tensor(f"lnst_{tag}", [128, 8 * NT], F32))
            with nc.Block() as blk:
                def body(a):
                    S.wait(a, "dma")
                    for tt in range(NT):
                        c = 8 * tt
                        S.inc(a.activation(scr[:, :], xres[:, tt * M:(tt + 1) * M], ACT.Square,
                                           accum_out=stt[:, c + 0:c + 1]), "act")
                blk.scalar(body)

                def body(v):
                    S.wait(v, "act")
                    for tt in range(NT):
                        c = 8 * tt
                        S.inc(v.reduce_sum(stt[:, c + 6:c + 7], xres[:, tt * M:(tt + 1) * M], axis=AX.X), "dve")
                    v.drain()
                    for tt in range(NT):
                        c = 8 * tt
                        S.inc(v.tensor_scalar_mul(stt[:, c + 1:c + 2], stt[:, c + 6:c + 7], -1.0 / M), "dve")
                    v.drain()
                    for tt in range(NT):
                        c = 8 * tt
                        S.inc(v.tensor_mul(stt[:, c + 2:c + 3], stt[:, c + 1:c + 2], stt[:, c + 1:c + 2]), "dve")
                    v.drain()
                    for tt in range(NT):
                        c = 8 * tt
                        S.inc(v.scalar_tensor_tensor(stt[:, c + 3:c + 4], stt[:, c + 0:c + 1], 1.0 / M,
                                                     stt[:, c + 2:c + 3], ALU.mult, ALU.subtract), "dve")
                    v.drain()
                    for tt in range(NT):
                        c = 8 * tt
                        S.inc(v.tensor_scalar_add(stt[:, c + 3:c + 4], stt[:, c + 3:c + 4], LNEPS), "dve")
                    v.drain()
                blk.vector(body)

                def body(a):
                    S.wait(a, "dve")
                    for tt in range(NT):
                        c = 8 * tt
                        S.inc(a.activation(stt[:, c + 4:c + 5], stt[:, c + 3:c + 4], ACT.Ln), "act")
                    a.drain()
                    for tt in range(NT):
                        c = 8 * tt
                        S.inc(a.activation(stt[:, c + 5:c + 6], stt[:, c + 4:c + 5], ACT.Exp, scale=-0.5), "act")
                blk.scalar(body)

                def body(v):
                    S.wait(v, "act")
                    for tt in range(NT):
                        c = 8 * tt
                        S.inc(v.tensor_scalar(xn_tm[:, tt * M:(tt + 1) * M], xres[:, tt * M:(tt + 1) * M],
                                              stt[:, c + 1:c + 2], stt[:, c + 5:c + 6],
                                              ALU.add, ALU.mult), "dve")
                blk.vector(body)

                def body(q):
                    S.wait(q, "dve")
                    for tt in range(NT):
                        for mb in range(NM):
                            S.inc(q.dma_start_transpose(
                                xn_fm[:, mb * T + tt * 128:mb * T + (tt + 1) * 128],
                                xn_tm[:, tt * M + mb * 128:tt * M + (mb + 1) * 128]), "dma", 16)
                blk.sync(body)

                if dbg is not None:
                    def body(g):
                        S.wait(g, "dve")
                        S.inc(g.dma_start(out=dbg.rearrange("(t p) m -> p t m", p=128),
                                          in_=xn_tm.rearrange("p (t m) -> p t m", t=NT)), "gdma", 16)
                        if "dbg_stt" in dram:
                            S.inc(g.dma_start(out=dram["dbg_stt"][:, :], in_=stt[:, :]), "gdma", 16)
                        S.wait(g, "gdma")
                    blk.gpsimd(body)

    layernorm("l0", dram.get("dbg_xn"))
    if debug:
        with nc.Block() as blk:
            def body(g):
                S.wait(g, "dma")
                S.inc(g.dma_start(out=dram["dbg_xnfm"][:, :], in_=xn_fm[:, :]), "gdma", 16)
                S.wait(g, "gdma")
            blk.gpsimd(body)

    def gla():
        L = ExitStack()
        def lsb(name, shape, dt):
            return L.enter_context(nc.sbuf_tensor(name, shape, dt))

        wbuf = [lsb(f"glaw{i}", [128, NM * M], BF16) for i in range(2)]
        wgt_sb = lsb("wgt_sb", [128, NM * NH], BF16)
        bc = lsb("bc", [NH, T], F32)
        zz16 = lsb("zz16", [NH, CH], F32)
        ebrow = lsb("ebrow", [1, 128], F32)
        eb64 = lsb("eb64", [128, 128], F32)
        selh = lsb("selh_sb", [NH, NH * 128], F32)
        q_sb = lsb("q_sb", [128, NM * CH], BF16)
        k_sb = lsb("k_sb", [128, NM * CH], BF16)
        v_scr = lsb("v_scr", [128, NM * CH], BF16)
        sq_scr = lsb("sq_scr", [128, NM * CH], BF16)
        k_tm = lsb("k_tm", [128, 2 * M], BF16)
        v_tm = lsb("v_tm", [128, 2 * M], BF16)
        silu_g = lsb("silu_g", [128, 2 * M], BF16)
        rows = lsb("rows", [1, 3 * CH], F32)
        bcq = lsb("bcq", [NH, CH], F32)
        bkneg = lsb("bkneg", [NH, CH], F32)
        mk = lsb("mk", [NH, CH], F32)
        mtab_tm = lsb("mtab_tm", [128, 2 * 80], F32)
        S_f = lsb("S_f", [128, (NH // 2) * 64], F32)
        S_b = lsb("S_b", [128, (NH // 2) * 64], BF16)
        dmask = lsb("dmask", [128, 2 * CH], BF16)
        msT = lsb("msT", [128, 2 * CH], BF16)
        o_tok = lsb("o_tok", [128, 2 * M], F32)
        osq = lsb("osq", [128, M], F32)
        rmsst = lsb("rmsst", [128, 2 * 64], F32)

        PS = ExitStack()
        def psb(name):
            return PS.enter_context(nc.psum_tensor(name, [128, 512], F32))
        ps = [psb(f"g_p{i}") for i in range(8)]
        ps_a0, ps_a1, ps_a2, ps_a3, ps_b, ps_c, ps_d, ps_e = ps

        wname = ("Wq", "Wk", "Wv", "Wg", "Wo")
        wtarget = {}
        wfree = {}

        def load_w(g, idx):
            assert idx < 2 or idx - 2 in wfree, f"load_w({idx}) before consumer of {idx-2} emitted"
            if idx - 2 in wfree:
                S.wait(g, "te", wfree[idx - 2])
            S.inc(g.dma_start(out=wbuf[idx % 2].rearrange("p (c d) -> p c d", d=M),
                              in_=dram[wname[idx % 5]].rearrange("(c p) d -> p c d", p=128)), "gdma", 16)
            wtarget[idx] = S.c["gdma"]

        with ExitStack() as G:
            gtt = G.enter_context(nc.sbuf_tensor("gtt", [NH, 512], F32))
            with nc.Block() as blk:
                def body(g):
                    S.inc(g.dma_start(out=wgt_sb.rearrange("p (c h) -> p c h", h=NH),
                                      in_=dram["Wgt"].rearrange("(c p) h -> p c h", p=128)), "gdma", 16)
                    wtarget["gt"] = S.c["gdma"]
                    load_w(g, 0)
                    load_w(g, 1)
                blk.gpsimd(body)

                def body(q):
                    S.inc(q.dma_start(out=selh[:, :], in_=dram["selh"][:, :]), "dma", 16)
                blk.sync(body)

                def body(v):
                    S.inc(v.memset(zz16[:, :], 0.0), "dve")
                    S.inc(v.memset(S_f[:, :], 0.0), "dve")
                    S.inc(v.memset(S_b[:, :], 0.0), "dve")
                    v.drain()
                blk.vector(body)

                def body(t):
                    S.wait(t, "gdma", wtarget["gt"])
                    S.wait(t, "dma")
                    for ts4 in range(4):
                        for mc in range(NM):
                            ins = t.matmul(ps[ts4][0:NH, 0:512],
                                           wgt_sb[:, mc * NH:(mc + 1) * NH],
                                           xn_fm[:, mc * T + ts4 * 512:mc * T + (ts4 + 1) * 512],
                                           start=(mc == 0), stop=(mc == NM - 1))
                        S.inc(ins, "te")
                blk.tensor(body)

                for ts4 in range(4):
                    def body(a, ts4=ts4):
                        if ts4 == 0:
                            S.wait(a, "te")
                        S.wait(a, "dve")
                        S.inc(a.activation(gtt[:, :], ps[ts4][0:NH, 0:512], ACT.Sigmoid), "act")
                        a.drain()
                        S.inc(a.activation(gtt[:, :], gtt[:, :], ACT.Ln), "act")
                    blk.scalar(body)
                    def body(v, ts4=ts4):
                        S.wait(v, "act")
                        S.inc(v.tensor_scalar_mul(gtt[:, :], gtt[:, :], 1.0 / 16.0), "dve")
                        v.drain()
                        for cc in range(2):
                            c = ts4 * 2 + cc
                            S.inc(v.tensor_tensor_scan(bc[:, c * CH:(c + 1) * CH],
                                                       gtt[:, cc * CH:(cc + 1) * CH],
                                                       zz16[:, :], 0.0, ALU.add, ALU.add), "dve")
                        v.drain()
                    blk.vector(body)

                def body(q):
                    S.wait(q, "dve")
                    S.inc(q.dma_start(out=ebrow[0:1, :].rearrange("o (h c) -> o h c", h=NH),
                                      in_=bc.rearrange("h (c q) -> h c q", q=CH)[:, :, CH - 1:CH],
                                      allow_slow_non_contiguous=True), "dma", 16)
                blk.sync(body)

                def body(a):
                    S.wait(a, "dma")
                    S.inc(a.activation(ebrow[:, :], ebrow[:, :], ACT.Exp), "act")
                blk.scalar(body)

                def body(t):
                    S.wait(t, "act")
                    S.inc(t.matmul(ps_b[:, 0:128], onesf[0:1, 0:128], ebrow[0:1, :],
                                   start=True, stop=True), "te")
                blk.tensor(body)

                def body(v):
                    S.wait(v, "te")
                    S.inc(v.tensor_copy(eb64[:, :], ps_b[:, 0:128]), "dve")
                blk.vector(body)

        for c in range(NCH):
            t0 = c * CH
            with nc.Block() as blk:
                for wi, dst in ((0, q_sb), (1, k_sb), (2, v_scr)):
                    widx = c * 5 + wi
                    wb = wbuf[widx % 2]
                    def body(t, wb=wb, widx=widx):
                        S.wait(t, "gdma", wtarget[widx])
                        S.wait(t, "act")
                        S.wait(t, "dve")
                        for ddb in range(NM):
                            for mc in range(NM):
                                ins = t.matmul(ps[ddb][:, 0:CH],
                                               wb[:, mc * M + ddb * 128:mc * M + ddb * 128 + 128],
                                               xn_fm[:, mc * T + t0:mc * T + t0 + CH],
                                               start=(mc == 0), stop=(mc == NM - 1))
                            S.inc(ins, "te")
                        wfree[widx] = S.c["te"]
                    blk.tensor(body)

                    def body(a, dst=dst):
                        S.wait(a, "te")
                        S.wait(a, "dma")
                        for ddb in range(NM):
                            S.inc(a.activation(dst[:, ddb * CH:(ddb + 1) * CH],
                                               ps[ddb][:, 0:CH], ACT.Copy), "act")
                            S.inc(a.activation(sq_scr[:, ddb * CH:(ddb + 1) * CH],
                                               ps[ddb][:, 0:CH], ACT.Square), "act")
                    blk.scalar(body)

                    def body(g, widx=widx):
                        nxt = widx + 2
                        if nxt < 40 and nxt not in wtarget:
                            load_w(g, nxt)
                    blk.gpsimd(body)

                    def body(t):
                        S.wait(t, "act")
                        S.wait(t, "dve")
                        for seg in range(NM):
                            ins = t.matmul(ps_b[0:1, 0:CH], onescol_b[:, 0:1],
                                           sq_scr[:, seg * CH:(seg + 1) * CH],
                                           start=(seg == 0), stop=(seg == NM - 1))
                        S.inc(ins, "te")
                    blk.tensor(body)

                    if wi == 0:
                        def body(a):
                            S.wait(a, "te")
                            S.inc(a.activation(rows[0:1, 0:CH], ps_b[0:1, 0:CH],
                                               ACT.Ln, bias=consts[0:1, 0:1], scale=1.0 / M), "act")
                        blk.scalar(body)
                        def body(v):
                            S.wait(v, "act")
                            S.inc(v.tensor_scalar(rows[0:1, 0:CH], rows[0:1, 0:CH],
                                                  -0.5, LOG_SCALE, ALU.mult, ALU.add), "dve")
                            v.drain()
                        blk.vector(body)
                    elif wi == 1:
                        def body(a):
                            S.wait(a, "te")
                            S.inc(a.activation(rows[0:1, CH:2 * CH], ps_b[0:1, 0:CH],
                                               ACT.Ln, bias=consts[0:1, 0:1], scale=1.0 / M), "act")
                        blk.scalar(body)
                        def body(v):
                            S.wait(v, "act")
                            S.inc(v.tensor_scalar_mul(rows[0:1, CH:2 * CH], rows[0:1, CH:2 * CH], -0.5), "dve")
                            v.drain()
                        blk.vector(body)
                    else:
                        def body(v):
                            S.wait(v, "te")
                            S.inc(v.tensor_scalar(rows[0:1, 2 * CH:3 * CH], ps_b[0:1, 0:CH],
                                                  1.0 / M, GEPS, ALU.mult, ALU.add), "dve")
                            v.drain()
                        blk.vector(body)

                gidx = c * 5 + 3
                wbg = wbuf[gidx % 2]
                for tl in range(2):
                    for half in range(2):
                        pst = ps[tl * 2 + half]
                        def body(t, pst=pst, tl=tl, half=half):
                            S.wait(t, "gdma", wtarget[gidx])
                            S.wait(t, "act")
                            S.wait(t, "dve")
                            for mc in range(NM):
                                ins = t.matmul(pst[:, 0:512],
                                               xn_fm[:, mc * T + t0 + tl * 128:mc * T + t0 + (tl + 1) * 128],
                                               wbg[:, mc * M + half * 512:mc * M + (half + 1) * 512],
                                               start=(mc == 0), stop=(mc == NM - 1))
                            S.inc(ins, "te")
                            if half == 1 and tl == 1:
                                wfree[gidx] = S.c["te"]
                        blk.tensor(body)
                        def body(a, pst=pst, tl=tl, half=half):
                            S.wait(a, "te")
                            S.inc(a.activation(silu_g[:, tl * M + half * 512:tl * M + (half + 1) * 512],
                                               pst[:, 0:512], ACT.Silu), "act")
                        blk.scalar(body)

                def body(g, gidx=gidx):
                    nxt = gidx + 2
                    if nxt < 40 and nxt not in wtarget:
                        load_w(g, nxt)
                blk.gpsimd(body)

                def body(t):
                    S.wait(t, "dve")
                    S.wait(t, "act")
                    S.inc(t.matmul(ps_b[0:NH, 0:CH], onesf[0:1, 0:NH], rows[0:1, 0:CH],
                                   start=True, stop=True), "te")
                    S.inc(t.matmul(ps_c[0:NH, 0:CH], onesf[0:1, 0:NH], rows[0:1, CH:2 * CH],
                                   start=True, stop=True), "te")
                blk.tensor(body)

                def body(v):
                    S.wait(v, "te")
                    S.inc(v.tensor_add(bcq[:, :], bc[:, t0:t0 + CH], ps_b[0:NH, 0:CH]), "dve")
                    S.inc(v.scalar_tensor_tensor(bkneg[:, :], bc[:, t0:t0 + CH], -1.0,
                                                 ps_c[0:NH, 0:CH], ALU.mult, ALU.add), "dve")
                    v.drain()
                blk.vector(body)

                def body(a):
                    S.wait(a, "dve")
                    S.inc(a.activation(mk[:, :], bkneg[:, :], ACT.Exp,
                                       bias=bc[:, t0 + CH - 1:t0 + CH]), "act")
                blk.scalar(body)

                for tl in range(2):
                    def body(t, tl=tl):
                        S.wait(t, "act")
                        S.wait(t, "dve")
                        sl = slice(tl * 128, (tl + 1) * 128)
                        S.inc(t.transpose(ps_b[:, 0:16], bcq[0:16, sl], identf[0:16, 0:16]), "te")
                        S.inc(t.transpose(ps_c[:, 0:16], mk[0:16, sl], identf[0:16, 0:16]), "te")
                        S.inc(t.transpose(ps_d[:, 0:1], rows[0:1, 2 * CH + tl * 128:2 * CH + (tl + 1) * 128],
                                          identf[0:1, 0:1]), "te")
                        S.inc(t.transpose(ps_e[:, 0:16], bkneg[0:16, sl], identf[0:16, 0:16]), "te")
                    blk.tensor(body)
                    def body(v, tl=tl):
                        S.wait(v, "te")
                        b0 = tl * 80
                        S.inc(v.tensor_copy(mtab_tm[:, b0 + 0:b0 + 16], ps_b[:, 0:16]), "dve")
                        S.inc(v.tensor_copy(mtab_tm[:, b0 + 16:b0 + 32], ps_c[:, 0:16]), "dve")
                        S.inc(v.tensor_copy(mtab_tm[:, b0 + 32:b0 + 33], ps_d[:, 0:1]), "dve")
                        S.inc(v.tensor_copy(mtab_tm[:, b0 + 33:b0 + 49], ps_e[:, 0:16]), "dve")
                        v.drain()
                    blk.vector(body)
                    def body(a, tl=tl):
                        S.wait(a, "dve")
                        b0 = tl * 80
                        S.inc(a.activation(mtab_tm[:, b0 + 0:b0 + 16], mtab_tm[:, b0 + 0:b0 + 16], ACT.Exp), "act")
                        S.inc(a.activation(mtab_tm[:, b0 + 32:b0 + 33], mtab_tm[:, b0 + 32:b0 + 33], ACT.Ln), "act")
                        a.drain()
                        S.inc(a.activation(mtab_tm[:, b0 + 32:b0 + 33], mtab_tm[:, b0 + 32:b0 + 33], ACT.Exp, scale=-0.5), "act")
                    blk.scalar(body)

                def body(q):
                    S.wait(q, "act")
                    S.wait(q, "te")
                    for tl in range(2):
                        for seg in range(NM):
                            S.inc(q.dma_start_transpose(
                                k_tm[:, tl * M + seg * 128:tl * M + (seg + 1) * 128],
                                k_sb[:, seg * CH + tl * 128:seg * CH + (tl + 1) * 128]), "dma", 16)
                            S.inc(q.dma_start_transpose(
                                v_tm[:, tl * M + seg * 128:tl * M + (seg + 1) * 128],
                                v_scr[:, seg * CH + tl * 128:seg * CH + (tl + 1) * 128]), "dma", 16)
                blk.sync(body)

                def body(v):
                    S.wait(v, "dma")
                    S.wait(v, "act")
                    v.drain()
                    for tl in range(2):
                        S.inc(v.tensor_scalar_mul(v_tm[:, tl * M:(tl + 1) * M],
                                                  v_tm[:, tl * M:(tl + 1) * M],
                                                  mtab_tm[:, tl * 80 + 32:tl * 80 + 33]), "dve")
                        for h in range(NH):
                            S.inc(v.tensor_scalar_mul(k_tm[:, tl * M + h * 64:tl * M + (h + 1) * 64],
                                                      k_tm[:, tl * M + h * 64:tl * M + (h + 1) * 64],
                                                      mtab_tm[:, tl * 80 + 16 + h:tl * 80 + 17 + h]), "dve")
                blk.vector(body)

                if c == 0 and "dbg_q" in dram:
                    def body(g):
                        S.wait(g, "dve")
                        S.wait(g, "act")
                        S.inc(g.dma_start(out=dram["dbg_q"][:, :], in_=q_sb[:, :]), "gdma", 16)
                        S.inc(g.dma_start(out=dram["dbg_k"][:, :], in_=k_sb[:, :]), "gdma", 16)
                        S.inc(g.dma_start(out=dram["dbg_v"][:, :], in_=v_scr[:, :]), "gdma", 16)
                        S.inc(g.dma_start(out=dram["dbg_bc"][:, :], in_=bc[:, :]), "gdma", 16)
                        S.inc(g.dma_start(out=dram["dbg_bcq"][:, :], in_=bcq[:, :]), "gdma", 16)
                        S.inc(g.dma_start(out=dram["dbg_bkneg"][:, :], in_=bkneg[:, :]), "gdma", 16)
                        S.inc(g.dma_start(out=dram["dbg_mtab"][:, :], in_=mtab_tm[:, :]), "gdma", 16)
                        S.inc(g.dma_start(out=dram["dbg_silug"][:, :], in_=silu_g[:, :]), "gdma", 16)
                        S.inc(g.dma_start(out=dram["dbg_ktm"][:, :], in_=k_tm[:, :]), "gdma", 16)
                        S.inc(g.dma_start(out=dram["dbg_vtm"][:, :], in_=v_tm[:, :]), "gdma", 16)
                        S.wait(g, "gdma")
                    blk.gpsimd(body)

                for h in range(NH):
                    po = (h % 2) * 64
                    sg = h // 2
                    def body(t, h=h):
                        S.wait(t, "act")
                        S.wait(t, "dve")
                        S.inc(t.matmul(ps_c[:, 0:CH], selh[:, h * 128:(h + 1) * 128],
                                       bcq[:, :], start=True, stop=True), "te")
                    blk.tensor(body)

                    def body(a, h=h):
                        S.wait(a, "te")
                        for jh in range(2):
                            S.inc(a.activation(dmask[:, jh * CH:(jh + 1) * CH], ps_c[:, 0:CH],
                                               ACT.Exp,
                                               bias=mtab_tm[:, jh * 80 + 33 + h:jh * 80 + 34 + h]), "act")
                    blk.scalar(body)

                    def body(t, h=h, po=po, sg=sg):
                        for jh in range(2):
                            S.inc(t.matmul((ps_d, ps_e)[jh][:, 0:CH],
                                           k_sb[po:po + 64, sg * CH + jh * 128:sg * CH + (jh + 1) * 128],
                                           q_sb[po:po + 64, sg * CH:(sg + 1) * CH],
                                           start=True, stop=True), "te")
                    blk.tensor(body)

                    def body(v, h=h):
                        S.wait(v, "act")
                        S.wait(v, "te")
                        for jh in range(2):
                            S.inc(v.tensor_mul(dmask[:, jh * CH:(jh + 1) * CH],
                                               dmask[:, jh * CH:(jh + 1) * CH],
                                               trilT_sb[:, jh * CH:(jh + 1) * CH]), "dve")
                        v.drain()
                        for jh in range(2):
                            S.inc(v.tensor_mul(msT[:, jh * CH:(jh + 1) * CH],
                                               (ps_d, ps_e)[jh][:, 0:CH],
                                               dmask[:, jh * CH:(jh + 1) * CH]), "dve")
                        v.drain()
                    blk.vector(body)

                    if c == 0 and h == 0 and "dbg_dmask" in dram:
                        def body(g):
                            S.wait(g, "dve")
                            S.inc(g.dma_start(out=dram["dbg_dmask"][:, :], in_=dmask[:, :]), "gdma", 16)
                            S.inc(g.dma_start(out=dram["dbg_msT"][:, :], in_=msT[:, :]), "gdma", 16)
                            S.wait(g, "gdma")
                        blk.gpsimd(body)

                    def body(t, h=h, po=po, sg=sg):
                        S.wait(t, "dve")
                        S.wait(t, "act")
                        for tl in range(2):
                            S.inc(t.matmul(ps_a2[po:po + 64, 0:64],
                                           k_tm[:, tl * M + h * 64:tl * M + (h + 1) * 64],
                                           v_tm[:, tl * M + h * 64:tl * M + (h + 1) * 64],
                                           start=(tl == 0), stop=(tl == 1)), "te")
                        for ih in range(2):
                            for jh in range(2):
                                S.inc(t.matmul((ps_a0, ps_a1)[ih][:, 0:64],
                                               msT[:, jh * CH + ih * 128:jh * CH + (ih + 1) * 128],
                                               v_tm[:, jh * M + h * 64:jh * M + (h + 1) * 64],
                                               start=(jh == 0), stop=(jh == 1)), "te")
                            S.inc(t.matmul((ps_d, ps_e)[ih][:, 0:64],
                                           q_sb[po:po + 64, sg * CH + ih * 128:sg * CH + (ih + 1) * 128],
                                           S_b[po:po + 64, sg * 64:(sg + 1) * 64],
                                           start=True, stop=True), "te")
                    blk.tensor(body)

                    def body(a, h=h):
                        S.wait(a, "te")
                        for ih in range(2):
                            S.inc(a.activation(o_tok[:, ih * M + h * 64:ih * M + (h + 1) * 64],
                                               (ps_a0, ps_a1)[ih][:, 0:64], ACT.Copy), "act")
                    blk.scalar(body)

                    def body(v, h=h, c=c, po=po, sg=sg):
                        S.wait(v, "act")
                        for ih in range(2):
                            S.inc(v.scalar_tensor_tensor(
                                o_tok[:, ih * M + h * 64:ih * M + (h + 1) * 64],
                                (ps_d, ps_e)[ih][:, 0:64],
                                mtab_tm[:, ih * 80 + h:ih * 80 + h + 1],
                                o_tok[:, ih * M + h * 64:ih * M + (h + 1) * 64],
                                ALU.mult, ALU.add), "dve")
                        S.inc(v.scalar_tensor_tensor(S_f[po:po + 64, sg * 64:(sg + 1) * 64],
                                                     S_f[po:po + 64, sg * 64:(sg + 1) * 64],
                                                     eb64[po:po + 64, h * NCH + c:h * NCH + c + 1],
                                                     ps_a2[po:po + 64, 0:64], ALU.mult, ALU.add), "dve")
                        v.drain()
                        S.inc(v.tensor_copy(S_b[po:po + 64, sg * 64:(sg + 1) * 64],
                                            S_f[po:po + 64, sg * 64:(sg + 1) * 64]), "dve")
                        v.drain()
                    blk.vector(body)

                if c == 0 and "dbg_otok" in dram:
                    def body(g):
                        S.wait(g, "dve")
                        S.inc(g.dma_start(out=dram["dbg_otok"][:, :], in_=o_tok[:, :]), "gdma", 16)
                        S.wait(g, "gdma")
                    blk.gpsimd(body)

                oidx = c * 5 + 4
                wbo = wbuf[oidx % 2]
                og = v_scr
                o_fm = q_sb
                for ih in range(2):
                    tg = 2 * c + ih
                    r = ih * 64
                    def body(a, ih=ih):
                        S.wait(a, "dve")
                        S.inc(a.activation(osq[:, :], o_tok[:, ih * M:(ih + 1) * M], ACT.Square), "act")
                    blk.scalar(body)
                    def body(v, r=r):
                        S.wait(v, "act")
                        S.inc(v.reduce_sum(rmsst[:, r:r + 16],
                                           osq.rearrange("p (h d) -> p h d", d=64), axis=AX.X), "dve")
                        v.drain()
                        S.inc(v.tensor_scalar(rmsst[:, r + 16:r + 32], rmsst[:, r:r + 16],
                                              1.0 / 64, GEPS, ALU.mult, ALU.add), "dve")
                        v.drain()
                    blk.vector(body)
                    def body(a, r=r):
                        S.wait(a, "dve")
                        S.inc(a.activation(rmsst[:, r + 32:r + 48], rmsst[:, r + 16:r + 32], ACT.Ln), "act")
                        a.drain()
                        S.inc(a.activation(rmsst[:, r + 48:r + 64], rmsst[:, r + 32:r + 48], ACT.Exp, scale=-0.5), "act")
                    blk.scalar(body)
                    def body(v, ih=ih, r=r):
                        S.wait(v, "act")
                        for h in range(NH):
                            S.inc(v.scalar_tensor_tensor(
                                og[:, ih * M + h * 64:ih * M + (h + 1) * 64],
                                o_tok[:, ih * M + h * 64:ih * M + (h + 1) * 64],
                                rmsst[:, r + 48 + h:r + 49 + h],
                                silu_g[:, ih * M + h * 64:ih * M + (h + 1) * 64],
                                ALU.mult, ALU.mult), "dve")
                        v.drain()
                    blk.vector(body)
                    def body(q, ih=ih):
                        S.wait(q, "dve")
                        S.wait(q, "te")
                        for seg in range(NM):
                            S.inc(q.dma_start_transpose(
                                o_fm[:, seg * CH + ih * 128:seg * CH + (ih + 1) * 128],
                                og[:, ih * M + seg * 128:ih * M + (seg + 1) * 128]), "dma", 16)
                    blk.sync(body)
                    def body(t, ih=ih, wbo=wbo, oidx=oidx):
                        S.wait(t, "gdma", wtarget[oidx])
                        S.wait(t, "dma")
                        S.wait(t, "act")
                        S.wait(t, "dve")
                        for mh in range(2):
                            for seg in range(NM):
                                ins = t.matmul((ps_b, ps_c)[mh][:, 0:512],
                                               o_fm[:, seg * CH + ih * 128:seg * CH + (ih + 1) * 128],
                                               wbo[:, seg * M + mh * 512:seg * M + (mh + 1) * 512],
                                               start=(seg == 0), stop=(seg == NM - 1))
                            S.inc(ins, "te")
                        if ih == 1:
                            wfree[oidx] = S.c["te"]
                    blk.tensor(body)
                    def body(v, tg=tg):
                        S.wait(v, "te")
                        for mh in range(2):
                            S.inc(v.tensor_add(xres[:, tg * M + mh * 512:tg * M + (mh + 1) * 512],
                                               (ps_b, ps_c)[mh][:, 0:512],
                                               xres[:, tg * M + mh * 512:tg * M + (mh + 1) * 512]), "dve")
                        v.drain()
                    blk.vector(body)

                if c == 0 and "dbg_og" in dram:
                    def body(g):
                        S.wait(g, "dve")
                        S.wait(g, "dma")
                        S.inc(g.dma_start(out=dram["dbg_og"][:, :], in_=og[:, :]), "gdma", 16)
                        S.inc(g.dma_start(out=dram["dbg_ofm"][:, :], in_=o_fm[:, :]), "gdma", 16)
                        S.inc(g.dma_start(out=dram["dbg_xres0"][:, :], in_=xres[:, 0:2 * M]), "gdma", 16)
                        S.wait(g, "gdma")
                    blk.gpsimd(body)

                def body(g, oidx=oidx):
                    for nxt in (oidx + 1, oidx + 2):
                        if nxt < 40 and nxt not in wtarget:
                            load_w(g, nxt)
                blk.gpsimd(body)

        L.close()
        PS.close()

    if not SKIP_GLA:
        gla()

    def moe():
        NE, HF = 8, 512
        L = ExitStack()
        def lsb(name, shape, dt):
            return L.enter_context(nc.sbuf_tensor(name, shape, dt))
        gw_sb = lsb("gw_sb", [128, NM * NE], BF16)
        glog = lsb("glog", [NE, T], F32)
        gsc = lsb("gsc", [128, NT * NE], F32)
        scl = lsb("scl", [128, NT * NE], F32)
        mx8 = lsb("mx8", [128, 8], F32)
        mi8 = lsb("mi8", [128, 8], mybir.dt.uint32)
        gst = lsb("gst", [128, NT * 4], F32)
        w1_sb = lsb("w1_sb", [128, NM * HF], BF16)
        w3_sb = lsb("w3_sb", [128, NM * HF], BF16)
        w2_sb = lsb("w2_sb", [128, 4 * M], BF16)
        h1_sb = lsb("h1_sb", [128, HF], BF16)
        h_sb = lsb("h_sb", [128, HF], BF16)
        h_fm = lsb("h_fm", [128, 4 * 128], BF16)
        y_sb = lsb("y_sb", [128, NT * M], F32)

        PS = ExitStack()
        def psb(name, shape):
            return PS.enter_context(nc.psum_tensor(name, shape, F32))
        ps_l = psb("m_pl", [NE, T])
        ps0 = psb("m_p0", [128, 512])
        ps1 = psb("m_p1", [128, 512])
        ps2 = psb("m_p2", [128, 512])
        ps3 = psb("m_p3", [128, 512])

        wtgt = {}
        def load_we(g, e):
            S.inc(g.dma_start(out=w1_sb.rearrange("p (c h) -> p c h", h=HF),
                              in_=dram["w1"][e].rearrange("(c p) h -> p c h", p=128)), "gdma", 16)
            S.inc(g.dma_start(out=w3_sb.rearrange("p (c h) -> p c h", h=HF),
                              in_=dram["w3"][e].rearrange("(c p) h -> p c h", p=128)), "gdma", 16)
            S.inc(g.dma_start(out=w2_sb.rearrange("p (c m) -> p c m", m=M),
                              in_=dram["w2"][e].rearrange("(c p) m -> p c m", p=128)), "gdma", 16)
            wtgt[e] = S.c["gdma"]

        with nc.Block() as blk:
            def body(g):
                S.inc(g.dma_start(out=gw_sb.rearrange("p (c e) -> p c e", e=NE),
                                  in_=dram["gate_w"].rearrange("(c p) e -> p c e", p=128)), "gdma", 16)
                wtgt["g"] = S.c["gdma"]
            blk.gpsimd(body)

            def body(t):
                S.wait(t, "gdma", wtgt["g"])
                S.wait(t, "dve")
                for ts4 in range(4):
                    for mc in range(NM):
                        ins = t.matmul(ps_l[0:NE, ts4 * 512:(ts4 + 1) * 512],
                                       gw_sb[:, mc * NE:(mc + 1) * NE],
                                       xn_fm[:, mc * T + ts4 * 512:mc * T + (ts4 + 1) * 512],
                                       start=(mc == 0), stop=(mc == NM - 1))
                    S.inc(ins, "te")
            blk.tensor(body)

            def body(a):
                S.wait(a, "te")
                S.inc(a.activation(glog[:, :], ps_l[0:NE, 0:T], ACT.Copy), "act")
            blk.scalar(body)

            for grp in range(4):
                def body(t, grp=grp):
                    S.wait(t, "act")
                    S.wait(t, "dve")
                    for q4 in range(4):
                        tt = grp * 4 + q4
                        S.inc(t.transpose((ps0, ps1, ps2, ps3)[q4][:, 0:NE],
                                          glog[0:NE, tt * 128:(tt + 1) * 128],
                                          identf[0:NE, 0:NE]), "te")
                blk.tensor(body)
                def body(v, grp=grp):
                    S.wait(v, "te")
                    for q4 in range(4):
                        tt = grp * 4 + q4
                        S.inc(v.tensor_copy(gsc[:, tt * NE:(tt + 1) * NE],
                                            (ps0, ps1, ps2, ps3)[q4][:, 0:NE]), "dve")
                blk.vector(body)

            def body(v):
                for tt in range(NT):
                    S.inc(v.reduce_max(gst[:, tt * 4:tt * 4 + 1], gsc[:, tt * NE:(tt + 1) * NE],
                                       axis=AX.X, negate=True), "dve")
                v.drain()
            blk.vector(body)
            def body(a):
                S.wait(a, "dve")
                for tt in range(NT):
                    S.inc(a.activation(scl[:, tt * NE:(tt + 1) * NE], gsc[:, tt * NE:(tt + 1) * NE],
                                       ACT.Exp, bias=gst[:, tt * 4:tt * 4 + 1],
                                       accum_out=gst[:, tt * 4 + 1:tt * 4 + 2]), "act")
            blk.scalar(body)
            def body(v):
                S.wait(v, "act")
                for tt in range(NT):
                    S.inc(v.reciprocal(gst[:, tt * 4 + 2:tt * 4 + 3], gst[:, tt * 4 + 1:tt * 4 + 2]), "dve")
                v.drain()
                for tt in range(NT):
                    S.inc(v.tensor_scalar_mul(scl[:, tt * NE:(tt + 1) * NE], scl[:, tt * NE:(tt + 1) * NE],
                                              gst[:, tt * 4 + 2:tt * 4 + 3]), "dve")
                v.drain()
                for tt in range(NT):
                    v.max_with_indices(mx8[:, :], mi8[:, :], scl[:, tt * NE:(tt + 1) * NE])
                    v.drain()
                    S.inc(v.scalar_tensor_tensor(scl[:, tt * NE:(tt + 1) * NE],
                                                 scl[:, tt * NE:(tt + 1) * NE],
                                                 mx8[:, 1:2],
                                                 scl[:, tt * NE:(tt + 1) * NE],
                                                 ALU.is_ge, ALU.mult), "dve")
                    v.drain()
            blk.vector(body)

        for e in range(NE):
            with nc.Block() as blk:
                def body(g, e=e):
                    S.wait(g, "te")
                    load_we(g, e)
                blk.gpsimd(body)

                for tt in range(NT):
                    def body(t, e=e, tt=tt):
                        S.wait(t, "gdma", wtgt[e])
                        S.wait(t, "act")
                        S.wait(t, "dve")
                        for mc in range(NM):
                            ins = t.matmul(ps0[:, 0:HF],
                                           xn_fm[:, mc * T + tt * 128:mc * T + (tt + 1) * 128],
                                           w1_sb[:, mc * HF:(mc + 1) * HF],
                                           start=(mc == 0), stop=(mc == NM - 1))
                        S.inc(ins, "te")
                        for mc in range(NM):
                            ins = t.matmul(ps1[:, 0:HF],
                                           xn_fm[:, mc * T + tt * 128:mc * T + (tt + 1) * 128],
                                           w3_sb[:, mc * HF:(mc + 1) * HF],
                                           start=(mc == 0), stop=(mc == NM - 1))
                        S.inc(ins, "te")
                    blk.tensor(body)

                    def body(a, e=e, tt=tt):
                        S.wait(a, "te")
                        S.inc(a.activation(h1_sb[:, :], ps0[:, 0:HF], ACT.Silu,
                                           scale=scl[:, tt * NE + e:tt * NE + e + 1]), "act")
                    blk.scalar(body)

                    def body(v, e=e, tt=tt):
                        S.wait(v, "act")
                        S.inc(v.scalar_tensor_tensor(h_sb[:, :], ps1[:, 0:HF],
                                                     scl[:, tt * NE + e:tt * NE + e + 1],
                                                     h1_sb[:, :], ALU.mult, ALU.mult), "dve")
                        v.drain()
                    blk.vector(body)

                    def body(q):
                        S.wait(q, "dve")
                        S.wait(q, "te")
                        for hb in range(4):
                            S.inc(q.dma_start_transpose(h_fm[:, hb * 128:(hb + 1) * 128],
                                                        h_sb[:, hb * 128:(hb + 1) * 128]), "dma", 16)
                    blk.sync(body)

                    def body(t, tt=tt):
                        S.wait(t, "dma")
                        for mh in range(2):
                            for hb in range(4):
                                ins = t.matmul((ps2, ps3)[mh][:, 0:512],
                                               h_fm[:, hb * 128:(hb + 1) * 128],
                                               w2_sb[:, hb * M + mh * 512:hb * M + (mh + 1) * 512],
                                               start=(hb == 0), stop=(hb == 3))
                            S.inc(ins, "te")
                    blk.tensor(body)

                    if e == 0:
                        def body(a, tt=tt):
                            S.wait(a, "te")
                            for mh in range(2):
                                S.inc(a.activation(y_sb[:, tt * M + mh * 512:tt * M + (mh + 1) * 512],
                                                   (ps2, ps3)[mh][:, 0:512], ACT.Copy), "act")
                        blk.scalar(body)
                    else:
                        def body(v, tt=tt):
                            S.wait(v, "te")
                            for mh in range(2):
                                S.inc(v.tensor_add(y_sb[:, tt * M + mh * 512:tt * M + (mh + 1) * 512],
                                                   (ps2, ps3)[mh][:, 0:512],
                                                   y_sb[:, tt * M + mh * 512:tt * M + (mh + 1) * 512]), "dve")
                            v.drain()
                        blk.vector(body)

        with nc.Block() as blk:
            def body(v):
                S.wait(v, "act")
                for tt in range(NT):
                    for mh in range(2):
                        S.inc(v.tensor_add(xres[:, tt * M + mh * 512:tt * M + (mh + 1) * 512],
                                           y_sb[:, tt * M + mh * 512:tt * M + (mh + 1) * 512],
                                           xres[:, tt * M + mh * 512:tt * M + (mh + 1) * 512]), "dve")
                v.drain()
            blk.vector(body)

        L.close()
        PS.close()

    if not globals().get('SKIP_MOE', False):
        moe()

    layernorm("l1")

    def routing_s():
        with ExitStack() as RS:
            tokrep = RS.enter_context(nc.sbuf_tensor("rt_tokrep", [128, M], F32))
            xnscr = RS.enter_context(nc.sbuf_tensor("rt_xnscr", [128, M], F32))
            st2 = RS.enter_context(nc.sbuf_tensor("rt_st2", [128, 8], F32))
            xnkeep = RS.enter_context(nc.sbuf_tensor("rt_keep", [128, M + 8], F32))
            with nc.Block() as blk:
                with ExitStack() as RPS:
                    ps0 = RPS.enter_context(nc.psum_tensor("rt_ps0", [128, 512], F32))
                    ps1 = RPS.enter_context(nc.psum_tensor("rt_ps1", [128, 512], F32))
                    def body(q):
                        S.inc(q.dma_start(out=xnscr[0:1, 0:M], in_=dram["q_rt_row"][:, :]), "dma", 16)
                    blk.sync(body)
                    def body(t):
                        S.wait(t, "dma")
                        S.wait(t, "dve")
                        S.wait(t, "act")
                        for mh in range(2):
                            S.inc(t.matmul((ps0, ps1)[mh][:, 0:512], onesf[0:1, :],
                                           xnscr[0:1, mh * 512:(mh + 1) * 512], start=True, stop=True), "te")
                    blk.tensor(body)
                    def body(a):
                        S.wait(a, "te")
                        for mh in range(2):
                            S.inc(a.activation(tokrep[:, mh * 512:(mh + 1) * 512],
                                               (ps0, ps1)[mh][:, 0:512], ACT.Copy), "act")
                    blk.scalar(body)
                    for tt in range(NT):
                        def body(a, tt=tt):
                            S.wait(a, "dve")
                            if tt == 0:
                                S.wait(a, "te")
                            S.inc(a.activation(xnscr[:, :], xres[:, tt * M:(tt + 1) * M], ACT.Square,
                                               accum_out=st2[:, 0:1]), "act")
                        blk.scalar(body)
                        def body(v, tt=tt):
                            S.wait(v, "act")
                            S.inc(v.reduce_sum(st2[:, 1:2], xres[:, tt * M:(tt + 1) * M], axis=AX.X), "dve")
                            v.drain()
                            S.inc(v.tensor_scalar_mul(st2[:, 2:3], st2[:, 1:2], -1.0 / M), "dve")
                            v.drain()
                            S.inc(v.tensor_mul(st2[:, 3:4], st2[:, 2:3], st2[:, 2:3]), "dve")
                            v.drain()
                            S.inc(v.scalar_tensor_tensor(st2[:, 4:5], st2[:, 0:1], 1.0 / M,
                                                         st2[:, 3:4], ALU.mult, ALU.subtract), "dve")
                            v.drain()
                            S.inc(v.tensor_scalar_add(st2[:, 4:5], st2[:, 4:5], LNEPS), "dve")
                            v.drain()
                        blk.vector(body)
                        def body(a, tt=tt):
                            S.wait(a, "dve")
                            S.inc(a.activation(st2[:, 5:6], st2[:, 4:5], ACT.Ln), "act")
                            a.drain()
                            S.inc(a.activation(st2[:, 6:7], st2[:, 5:6], ACT.Exp, scale=-0.5), "act")
                        blk.scalar(body)
                        def body(v, tt=tt):
                            S.wait(v, "act")
                            S.inc(v.tensor_scalar(xnscr[:, :], xres[:, tt * M:(tt + 1) * M],
                                                  st2[:, 2:3], st2[:, 6:7], ALU.add, ALU.mult), "dve")
                            v.drain()
                        blk.vector(body)
                        if tt == 0 and "dbg_tokrep" in dram:
                            def body(v):
                                S.inc(v.tensor_copy(xnkeep[:, 0:M], xnscr[:, :]), "dve")
                                S.inc(v.tensor_copy(xnkeep[:, M:M + 8], st2[:, :]), "dve")
                                v.drain()
                            blk.vector(body)
                            def body(g):
                                S.wait(g, "dve")
                                S.inc(g.dma_start(out=dram["dbg_tokrep"][:, :], in_=tokrep[:, :]), "gdma", 16)
                                S.inc(g.dma_start(out=dram["dbg_xnt"][:, :], in_=xnkeep[:, 0:M]), "gdma", 16)
                                S.inc(g.dma_start(out=dram["dbg_tiny"][:, :], in_=xnkeep[0:1, M:M + 8]), "gdma", 16)
                            blk.gpsimd(body)
                        def body(v, tt=tt):
                            S.inc(v.tensor_mul(xnscr[:, :], xnscr[:, :], tokrep[:, :]), "dve")
                            v.drain()
                            S.inc(v.reduce_sum(s2d[:, tt:tt + 1], xnscr[:, :], axis=AX.X), "dve")
                            v.drain()
                        blk.vector(body)

    routing_s()

    def mamba():
        DIN, DST, NDT = 2048, 16, 16
        TC, NC = 512, 4
        L = ExitStack()
        def lsb(name, shape, dt):
            return L.enter_context(nc.sbuf_tensor(name, shape, dt))
        wq_sb = lsb("mw_q", [128, 2 * 4096], BF16)
        wx_sb = lsb("mw_x", [128, NDT * 33], BF16)
        wdt_n = lsb("m_wdtn", [128, DST], F32)
        bdt_n = lsb("m_bdtn", [128, DST], F32)
        wdt_r = lsb("m_wdtr", [128, DST], F32)
        bdt_r = lsb("m_bdtr", [128, DST], F32)
        cb_sb = lsb("m_cb", [128, DST], F32)
        dp_sb = lsb("m_dp", [128, DST], F32)
        cw_sb = lsb("m_cw", [128, NDT * 4], F32)
        alog = lsb("m_alog", [128, NDT * DST], F32)
        brow = lsb("m_brow", [1, 2 * 512], BF16)
        states = lsb("m_states", [128, NDT * DST], F32)
        xd = lsb("m_xd", [128, NDT * 516], BF16)
        zs = lsb("m_zs", [128, NDT * TC], BF16)
        bca = lsb("m_bca", [128, 32 * TC], BF16)
        ssm_t = lsb("m_ssmt", [33, TC], F32)
        ssm_tb = lsb("m_ssmtb", [33, TC], BF16)
        ssm_e = lsb("m_ssme", [128, 33 * 4], F32)
        lns = lsb("m_lns", [128, TC], F32)
        acc = lns
        zzt = lsb("m_zzt", [128, TC], F32)
        ltt = lsb("m_ltt", [128, TC], F32)
        wdd = lsb("m_w", [128, TC], BF16)
        dA = lsb("m_dA", [128, 2 * TC], F32)
        bP = lsb("m_bP", [128, 2 * TC], BF16)
        hh = lsb("m_h", [128, 4 * TC], BF16)
        gg = lsb("m_g", [128, 2 * TC], BF16)

        PS = ExitStack()
        def psb(name):
            return PS.enter_context(nc.psum_tensor(name, [128, 512], F32))
        ps = [psb(f"mm_p{i}") for i in range(8)]

        wtgt = {}
        hread = {0: 0, 1: 0}
        mkA_mm = {}; mkA_cp = {}; mkF_mm = {}; mk_bca = {}
        mk_zz = {}; mk_abs = {}; mk_ln = {}; mk_sp = {}; mk_wdd = {}
        mk_dA = {}; mk_bP = {}; mk_scan = {}; mk_gg = {}
        mk_mm = {}; mk_mmD = {}; mk_yev = {}; mk_gp = {}
        xd3 = xd.rearrange("p (c w) -> p c w", w=516)

        def uslc(d, a=0, b=TC):
            return xd[:, d * 516 + 3 + a:d * 516 + 3 + b]

        with nc.Block() as blk:
            def body(g):
                S.inc(g.dma_start(out=wx_sb.rearrange("p (c k) -> p c k", k=33),
                                  in_=dram["W_xT"].rearrange("(c p) k -> p c k", p=128)), "gdma", 16)
                S.inc(g.dma_start(out=alog.rearrange("p (c s) -> p c s", s=DST),
                                  in_=dram["A_log"].rearrange("(c p) s -> p c s", p=128)), "gdma", 16)
                S.inc(g.dma_start(out=wdt_r.rearrange("p (c o) -> p c o", o=1),
                                  in_=dram["W_dt"].rearrange("(c p) o -> p c o", p=128),
                                  allow_slow_non_contiguous=True), "gdma", 16)
                S.inc(g.dma_start(out=bdt_r.rearrange("p (c o) -> p c o", o=1),
                                  in_=dram["b_dt"].rearrange("(c p) o -> p c o", p=128),
                                  allow_slow_non_contiguous=True), "gdma", 16)
                S.inc(g.dma_start(out=cb_sb.rearrange("p (c o) -> p c o", o=1),
                                  in_=dram["conv_b"].rearrange("(c p) o -> p c o", p=128),
                                  allow_slow_non_contiguous=True), "gdma", 16)
                S.inc(g.dma_start(out=dp_sb.rearrange("p (c o) -> p c o", o=1),
                                  in_=dram["D_param"].rearrange("(c p) o -> p c o", p=128),
                                  allow_slow_non_contiguous=True), "gdma", 16)
                S.inc(g.dma_start(out=cw_sb.rearrange("p (c j) -> p c j", j=4),
                                  in_=dram["conv_w"].rearrange("(c p) j -> p c j", p=128)), "gdma", 16)
            blk.gpsimd(body)

            def body(v):
                S.wait(v, "gdma")
                S.inc(v.memset(states[:, :], 0.0), "dve")
                S.inc(v.memset(xd3[:, :, 0:3], 0.0), "dve")
                S.inc(v.tensor_scalar_mul(wdt_n[:, :], wdt_r[:, :], -1.0), "dve")
                S.inc(v.tensor_scalar_mul(bdt_n[:, :], bdt_r[:, :], -1.0), "dve")
                v.drain()
            blk.vector(body)

            def body(a):
                S.wait(a, "gdma")
                S.inc(a.activation(alog[:, :], alog[:, :], ACT.Exp), "act")
            blk.scalar(body)
            def body(v):
                S.wait(v, "act")
                S.inc(v.tensor_scalar_mul(alog[:, :], alog[:, :], -1.0), "dve")
                v.drain()
            blk.vector(body)

        for c in range(NC):
            t0 = c * TC
            with nc.Block() as blk:
                for q in range(8):
                    hb = (q % 2) * 4096
                    def body(g, q=q, c=c, hb=hb):
                        if hread[q % 2]:
                            g.wait_ge(S.h["te"], hread[q % 2])
                        S.inc(g.dma_start(out=wq_sb[:, hb:hb + 4096].rearrange("p (m d) -> p m d", d=512),
                                          in_=dram["W_inT"][:, q * 512:(q + 1) * 512]
                                          .rearrange("(m p) d -> p m d", p=128)), "gdma", 16)
                        wtgt[("in", c, q)] = S.c["gdma"]
                    blk.gpsimd(body)

                    def body(t, q=q, c=c, hb=hb):
                        S.wait(t, "gdma", wtgt[("in", c, q)])
                        if q >= 2:
                            S.wait(t, "act", mkA_cp[(c, q - 2)])
                        else:
                            S.wait(t, "act")
                            S.wait(t, "dve")
                        for ddb in range(4):
                            for mc in range(NM):
                                ins = t.matmul(ps[(q % 2) * 4 + ddb][:, 0:512],
                                               wq_sb[:, hb + mc * 512 + ddb * 128:hb + mc * 512 + ddb * 128 + 128],
                                               xn_fm[:, mc * T + t0:mc * T + t0 + TC],
                                               start=(mc == 0), stop=(mc == NM - 1))
                            S.inc(ins, "te")
                        hread[q % 2] = S.c["te"]
                        mkA_mm[(c, q)] = S.c["te"]
                    blk.tensor(body)

                    def body(a, q=q, c=c):
                        S.wait(a, "te", mkA_mm[(c, q)])
                        for ddb in range(4):
                            gi = q * 4 + ddb
                            if gi < 16:
                                S.inc(a.activation(xd[:, gi * 516 + 3:gi * 516 + 515],
                                                   ps[(q % 2) * 4 + ddb][:, 0:512], ACT.Copy), "act")
                            else:
                                zq = gi - 16
                                S.inc(a.activation(zs[:, zq * TC:(zq + 1) * TC],
                                                   ps[(q % 2) * 4 + ddb][:, 0:512], ACT.Silu), "act")
                        mkA_cp[(c, q)] = S.c["act"]
                    blk.scalar(body)

                for quar in (0, 1):
                    def body(g, quar=quar, c=c):
                        g.wait_ge(S.h["te"], hread[quar % 2])
                        S.inc(g.dma_start(out=wq_sb[:, (quar % 2) * 4096:(quar % 2) * 4096 + 4096]
                                          .rearrange("p (m d) -> p m d", d=1024),
                                          in_=dram["W_outT"][quar * 512:(quar + 1) * 512, :]
                                          .rearrange("(m p) d -> p m d", p=128)), "gdma", 16)
                        wtgt[("out", c, quar)] = S.c["gdma"]
                    blk.gpsimd(body)

                for dt_i in range(NDT):
                    def body(v, dt_i=dt_i):
                        S.wait(v, "act")
                        b0 = dt_i * 516
                        S.inc(v.tensor_scalar_mul(acc[:, :], xd[:, b0:b0 + 512],
                                                  cw_sb[:, dt_i * 4:dt_i * 4 + 1]), "dve")
                        v.drain()
                        for j in (1, 2, 3):
                            S.inc(v.scalar_tensor_tensor(acc[:, :], xd[:, b0 + j:b0 + j + 512],
                                                         cw_sb[:, dt_i * 4 + j:dt_i * 4 + j + 1],
                                                         acc[:, :], ALU.mult, ALU.add), "dve")
                            v.drain()
                        S.inc(v.tensor_copy(xd3[:, dt_i:dt_i + 1, 0:3], xd3[:, dt_i:dt_i + 1, 512:515]), "dve")
                    blk.vector(body)
                    def body(a, dt_i=dt_i):
                        S.wait(a, "dve")
                        S.inc(a.activation(uslc(dt_i), acc[:, :], ACT.Silu,
                                           bias=cb_sb[:, dt_i:dt_i + 1]), "act")
                    blk.scalar(body)

                def body(t):
                    S.wait(t, "act")
                    for ts4 in range(4):
                        for dt_i in range(NDT):
                            ins = t.matmul(ps[ts4][:, 0:33],
                                           uslc(dt_i, ts4 * 128, (ts4 + 1) * 128),
                                           wx_sb[:, dt_i * 33:(dt_i + 1) * 33],
                                           start=(dt_i == 0), stop=(dt_i == NDT - 1))
                        S.inc(ins, "te")
                blk.tensor(body)
                def body(a):
                    S.wait(a, "te")
                    for ts4 in range(4):
                        S.inc(a.activation(ssm_e[:, ts4 * 33:(ts4 + 1) * 33], ps[ts4][:, 0:33],
                                           ACT.Copy), "act")
                blk.scalar(body)
                for ts4 in range(4):
                    def body(t, ts4=ts4):
                        S.wait(t, "act")
                        S.wait(t, "dve")
                        S.inc(t.transpose(ps[4 + ts4][0:33, 0:128],
                                          ssm_e[:, ts4 * 33:(ts4 + 1) * 33], identf[:, :]), "te")
                    blk.tensor(body)
                    def body(v, ts4=ts4):
                        S.wait(v, "te")
                        S.inc(v.tensor_copy(ssm_t[:, ts4 * 128:(ts4 + 1) * 128],
                                            ps[4 + ts4][0:33, 0:128]), "dve")
                    blk.vector(body)
                def body(v):
                    v.drain()
                    S.inc(v.tensor_copy(ssm_tb[:, :], ssm_t[:, :]), "dve")
                    v.drain()
                blk.vector(body)

                def body(t):
                    S.wait(t, "dve")
                    S.inc(t.matmul(ps[1][:, 0:512], onesf[0:1, :], ssm_t[0:1, :],
                                   start=True, stop=True), "te")
                blk.tensor(body)
                for s in range(DST):
                    def body(q, s=s):
                        if s == 0:
                            S.wait(q, "dve")
                        else:
                            S.wait(q, "te")
                        S.inc(q.dma_start(out=brow[0:1, 0:512],
                                          in_=ssm_tb[1 + s:2 + s, :]), "dma", 16)
                        S.inc(q.dma_start(out=brow[0:1, 512:1024],
                                          in_=ssm_tb[17 + s:18 + s, :]), "dma", 16)
                    blk.sync(body)
                    def body(t, s=s):
                        S.wait(t, "dma")
                        if s > 0:
                            S.wait(t, "act")
                        S.inc(t.matmul(ps[2][:, 0:512], onesb[0:1, :],
                                       brow[0:1, 0:512], start=True, stop=True), "te")
                        S.inc(t.matmul(ps[3][:, 0:512], onesb[0:1, :],
                                       brow[0:1, 512:1024], start=True, stop=True), "te")
                    blk.tensor(body)
                    def body(a, s=s, c=c):
                        S.wait(a, "te")
                        S.inc(a.activation(bca[:, s * TC:(s + 1) * TC], ps[2][:, 0:512], ACT.Copy), "act")
                        S.inc(a.activation(bca[:, (16 + s) * TC:(17 + s) * TC], ps[3][:, 0:512], ACT.Copy), "act")
                        mk_bca[(c, s)] = S.c["act"]
                    blk.scalar(body)

                if c == 0 and "dbg_m_xd" in dram:
                    def body(g):
                        S.wait(g, "act")
                        S.wait(g, "dve")
                        S.inc(g.dma_start(out=dram["dbg_m_xd"][:, :], in_=xd[:, :]), "gdma", 16)
                        S.inc(g.dma_start(out=dram["dbg_m_u"].rearrange("p (c t) -> p c t", t=512),
                                          in_=xd3[:, :, 3:515]), "gdma", 16)
                        S.inc(g.dma_start(out=dram["dbg_m_ssmt"][:, :], in_=ssm_t[:, :]), "gdma", 16)
                        S.inc(g.dma_start(out=dram["dbg_m_bca"][:, :], in_=bca[:, :]), "gdma", 16)
                        S.inc(g.dma_start(out=dram["dbg_m_zs"][:, :], in_=zs[:, :]), "gdma", 16)
                        S.wait(g, "gdma")
                    blk.gpsimd(body)

                for dt_i in range(NDT):
                    key = (c, dt_i)
                    prevk = (c, dt_i - 1) if dt_i > 0 else ((c - 1, NDT - 1) if c > 0 else None)
                    base_dve = S.c["dve"]
                    pred_abs = base_dve + 1
                    pred_sp = base_dve + 2
                    pred_scan = {s: base_dve + (35 if s == 15 else 6 + 2 * s) for s in range(DST)}

                    def body(a, key=key, prevk=prevk, dt_i=dt_i,
                             pred_abs=pred_abs, pred_sp=pred_sp, pred_scan=pred_scan):
                        if prevk is not None:
                            S.wait(a, "dve", mk_sp[prevk])
                        S.inc(a.activation(zzt[:, :], ps[1][:, 0:512], ACT.Identity,
                                           scale=wdt_r[:, dt_i:dt_i + 1], bias=bdt_r[:, dt_i:dt_i + 1]), "act")
                        mk_zz[key] = S.c["act"]
                        a.wait_ge(S.h["dve"], pred_abs)
                        S.inc(a.activation(ltt[:, :], ltt[:, :], ACT.Exp, scale=-1.0), "act")
                        a.drain()
                        S.inc(a.activation(ltt[:, :], ltt[:, :], ACT.Ln, bias=consts[:, 2:3]), "act")
                        mk_ln[key] = S.c["act"]
                        for s in range(DST):
                            pr = s % 2
                            a.wait_ge(S.h["dve"], pred_scan[s - 2] if s >= 2 else pred_sp)
                            S.inc(a.activation(dA[:, pr * TC:(pr + 1) * TC], lns[:, :], ACT.Exp,
                                               scale=alog[:, dt_i * DST + s:dt_i * DST + s + 1]), "act")
                            mk_dA[key + (s,)] = S.c["act"]
                            if s >= 2:
                                sl = (s - 2) % 4
                                S.inc(a.activation(states[:, dt_i * DST + s - 2:dt_i * DST + s - 1],
                                                   hh[:, sl * TC + TC - 1:sl * TC + TC],
                                                   ACT.Identity), "act")
                    blk.scalar(body)

                    def body(v, key=key, prevk=prevk, dt_i=dt_i,
                             pred_abs=pred_abs, pred_sp=pred_sp):
                        S.wait(v, "act", mk_zz[key])
                        S.inc(v.scalar_tensor_tensor(ltt[:, :], zzt[:, :], -1.0, zzt[:, :],
                                                     ALU.mult, ALU.max), "dve")
                        assert S.c["dve"] == pred_abs
                        S.wait(v, "act", mk_ln[key])
                        S.inc(v.scalar_tensor_tensor(lns[:, :], zzt[:, :], 0.0, ltt[:, :],
                                                     ALU.max, ALU.add), "dve")
                        assert S.c["dve"] == pred_sp
                        mk_sp[key] = S.c["dve"]
                        v.drain()
                        S.inc(v.tensor_mul(wdd[:, :], lns[:, :], uslc(dt_i)), "dve")
                        v.drain()
                        S.inc(v.tensor_mul(bP[:, 0:TC], wdd[:, :], bca[:, 0:TC]), "dve")
                    blk.vector(body)

                    for s in range(DST):
                        pr = s % 2
                        kk = key + (s,)
                        def body(v, kk=kk, key=key, prevk=prevk, s=s, pr=pr, dt_i=dt_i,
                                 pred_scan=pred_scan):
                            if s < DST - 1:
                                S.inc(v.tensor_mul(bP[:, ((s + 1) % 2) * TC:((s + 1) % 2 + 1) * TC],
                                                   wdd[:, :], bca[:, (s + 1) * TC:(s + 2) * TC]), "dve")
                            S.wait(v, "act", mk_dA[kk])
                            if s >= 4:
                                S.wait(v, "gp", mk_gg[key + (s - 4,)])
                            elif prevk is not None:
                                S.wait(v, "gp", mk_gp.get(prevk, 0))
                            S.inc(v.tensor_tensor_scan(hh[:, (s % 4) * TC:(s % 4 + 1) * TC],
                                                       dA[:, pr * TC:(pr + 1) * TC],
                                                       bP[:, pr * TC:(pr + 1) * TC],
                                                       states[:, dt_i * DST + s:dt_i * DST + s + 1],
                                                       ALU.mult, ALU.add), "dve")
                            assert S.c["dve"] == pred_scan[s], (S.c["dve"], pred_scan[s], s)
                            mk_scan[kk] = S.c["dve"]
                        blk.vector(body)
                        def body(g, kk=kk, key=key, prevk=prevk, s=s, dt_i=dt_i):
                            g.wait_ge(S.h["dve"], mk_scan[kk])
                            tgt = mk_mm.get(key + (s - 2,)) if s >= 2 else \
                                (mk_mmD.get(prevk) if prevk is not None else None)
                            if tgt:
                                g.wait_ge(S.h["te"], tgt)
                            S.inc(g.tensor_mul(gg[:, (s % 2) * TC:(s % 2 + 1) * TC],
                                               hh[:, (s % 4) * TC:(s % 4 + 1) * TC],
                                               bca[:, (16 + s) * TC:(17 + s) * TC]), "gp")
                            mk_gg[kk] = S.c["gp"]
                            if s == DST - 1:
                                for s2 in (DST - 2, DST - 1):
                                    sl = s2 % 4
                                    S.inc(g.tensor_copy(states[:, dt_i * DST + s2:dt_i * DST + s2 + 1],
                                                        hh[:, sl * TC + TC - 1:sl * TC + TC]), "gp")
                                mk_gp[key] = S.c["gp"]
                        blk.gpsimd(body)
                        if not globals().get("SKIP_SCAN_S", False):
                            def body(t, kk=kk, s=s):
                                S.wait(t, "gp", mk_gg[kk])
                                S.inc(t.matmul(ps[0][:, 0:512], identb[:, :],
                                               gg[:, (s % 2) * TC:(s % 2 + 1) * TC],
                                               start=(s == 0), stop=False), "te")
                                mk_mm[kk] = S.c["te"]
                            blk.tensor(body)

                    def body(v, key=key, dt_i=dt_i):
                        S.wait(v, "te")
                        S.inc(v.tensor_scalar_mul(gg[:, 0:TC], uslc(dt_i),
                                                  dp_sb[:, dt_i:dt_i + 1]), "dve")
                        mk_gg[key + (DST,)] = S.c["dve"]
                    blk.vector(body)
                    def body(t, key=key):
                        S.wait(t, "dve", mk_gg[key + (DST,)])
                        sk = globals().get("SKIP_SCAN_S", False)
                        S.inc(t.matmul(ps[0][:, 0:512], identb[:, :], gg[:, 0:TC],
                                       start=sk, stop=True), "te")
                        mk_mmD[key] = S.c["te"]
                    blk.tensor(body)
                    def body(v, key=key, dt_i=dt_i):
                        S.wait(v, "te")
                        S.inc(v.tensor_mul(xd[:, dt_i * 516 + 3:dt_i * 516 + 515], ps[0][:, 0:512],
                                           zs[:, dt_i * TC:(dt_i + 1) * TC]), "dve")
                        mk_yev[key] = S.c["dve"]
                    blk.vector(body)

                for quar in range(4):
                    hb = (quar % 2) * 4096
                    def body(t, quar=quar, c=c, hb=hb):
                        S.wait(t, "gdma", wtgt[("out", c, quar)])
                        S.wait(t, "dve")
                        for ts4 in range(4):
                            for mh in range(2):
                                for db in range(4):
                                    dt_i = quar * 4 + db
                                    ins = t.matmul(ps[ts4 * 2 + mh][:, 0:512],
                                                   xd[:, dt_i * 516 + 3 + ts4 * 128:dt_i * 516 + 3 + (ts4 + 1) * 128],
                                                   wq_sb[:, hb + db * 1024 + mh * 512:hb + db * 1024 + (mh + 1) * 512],
                                                   start=(quar == 0 and db == 0),
                                                   stop=(quar == 3 and db == 3))
                                S.inc(ins, "te")
                        hread[quar % 2] = S.c["te"]
                        mkF_mm[(c, quar)] = S.c["te"]
                    blk.tensor(body)
                    if quar < 2:
                        def body(g, quar=quar, c=c):
                            g.wait_ge(S.h["te"], mkF_mm[(c, quar)])
                            nq = quar + 2
                            S.inc(g.dma_start(out=wq_sb[:, (nq % 2) * 4096:(nq % 2) * 4096 + 4096]
                                              .rearrange("p (m d) -> p m d", d=1024),
                                              in_=dram["W_outT"][nq * 512:(nq + 1) * 512, :]
                                              .rearrange("(m p) d -> p m d", p=128)), "gdma", 16)
                            wtgt[("out", c, nq)] = S.c["gdma"]
                        blk.gpsimd(body)
                def body(v, c=c):
                    S.wait(v, "te")
                    for ts4 in range(4):
                        tg = c * 4 + ts4
                        for mh in range(2):
                            S.inc(v.tensor_add(xres[:, tg * M + mh * 512:tg * M + (mh + 1) * 512],
                                               ps[ts4 * 2 + mh][:, 0:512],
                                               xres[:, tg * M + mh * 512:tg * M + (mh + 1) * 512]), "dve")
                    v.drain()
                blk.vector(body)

        L.close()
        PS.close()

    if not globals().get('SKIP_MAMBA', False):
        mamba()

    def l1ffn():
        LH = 8
        LOGK = float(np.log(1152.0))
        L = ExitStack()
        def lsb(name, shape, dt):
            return L.enter_context(nc.sbuf_tensor(name, shape, dt))
        onescol_f = lsb("f_ones", [128, 1], F32)
        onesall = lsb("f_onesall", [128, 128], F32)
        rowbank = lsb("f_rowb", [128, T], F32)
        sqscr = lsb("f_sq", [128, T], BF16)
        rfacc = lsb("f_rfac", [128, NT], F32)
        rfac8 = lsb("f_rfac8", [128, NT], F32)
        gcol = lsb("f_gcol", [128, NM], F32)
        bb = lsb("f_bb", [128, 16], F32)
        XX = lsb("f_XX", [128, 16], F32)
        score = lsb("f_score", [128, 16], F32)
        mask2d = lsb("f_mask", [128, 16], F32)
        eqm = lsb("f_eqm", [128, 16], BF16)
        etile = lsb("f_et", [128, 16], F32)
        zzrow = lsb("f_zzr", [1, 16], F32)
        tiny = lsb("f_tiny", [1, 8], F32)
        st2 = lsb("f_st2", [128, 8], F32)
        tcol = lsb("f_tcol", [128, 8], F32)
        qn2 = lsb("f_qn2", [128, 2 * NT], F32)
        q_tok = lsb("f_qtok", [128, 512], BF16)
        k_tok = lsb("f_ktok", [128, 512], BF16)
        v_tok = lsb("f_vtok", [128, 512], BF16)
        p_sb = lsb("f_psb", [128, 128], F32)
        pT_sb = lsb("f_pTsb", [128, 128], BF16)
        qfw = lsb("f_qfw", [128, 512], BF16)
        kfw = lsb("f_kfw", [128, 512], BF16)
        ogl = lsb("f_ogl", [128, 512], BF16)
        ofw = lsb("f_ofw", [128, 512], BF16)

        PS = ExitStack()
        def psb(name, shape):
            return PS.enter_context(nc.psum_tensor(name, shape, F32))
        ps_a = psb("f_pa", [128, 512])
        ps_b = psb("f_pb", [128, 512])
        ps_c = psb("f_pc", [128, 512])
        ps_d = psb("f_pd", [128, 512])
        ps_e = psb("f_pe", [128, 2048])
        ps_q = (ps_a, ps_b, ps_c, ps_d)
        wtgt = {}

        with nc.Block() as blk:
            def body(q):
                S.inc(q.dma_start(out=onescol_f[:, :], in_=dram["onescol_f"][:, :]), "dma", 16)
            blk.sync(body)
            def body(g):
                S.inc(g.dma_start(out=gcol.rearrange("p (c o) -> p c o", o=1),
                                  in_=dram["q_rt"].rearrange("(c p) o -> p c o", p=128),
                                  allow_slow_non_contiguous=True), "gdma", 16)
                wtgt["qrt"] = S.c["gdma"]
            blk.gpsimd(body)
            def body(v):
                S.inc(v.memset(zzrow[:, :], 0.0), "dve")
                S.inc(v.memset(onesall[:, :], 1.0), "dve")
                v.drain()
            blk.vector(body)
            def body(v):
                S.wait(v, "te")
                S.wait(v, "act")
                S.inc(v.reduce_max(st2[:, 0:1], s2d[:, :], axis=AX.X), "dve")
                v.drain()
            blk.vector(body)
            def body(t):
                S.wait(t, "dve")
                S.inc(t.transpose(ps_a[0:1, 0:128], st2[:, 0:1], identf[:, :]), "te")
            blk.tensor(body)
            def body(v):
                S.wait(v, "te")
                S.inc(v.reduce_max(tiny[0:1, 0:1], ps_a[0:1, 0:128], axis=AX.X, negate=True), "dve")
                v.drain()
            blk.vector(body)
            for mb in range(NM):
                def body(a, mb=mb):
                    S.wait(a, "te")
                    S.inc(a.activation(sqscr[:, :], xn_fm[:, mb * T:(mb + 1) * T], ACT.Square), "act")
                blk.scalar(body)
                def body(t, mb=mb):
                    S.wait(t, "act")
                    for q4 in range(4):
                        S.inc(t.matmul(ps_e[0:1, q4 * 512:(q4 + 1) * 512], onescol_b[:, 0:1],
                                       sqscr[:, q4 * 512:(q4 + 1) * 512],
                                       start=(mb == 0), stop=(mb == NM - 1)), "te")
                blk.tensor(body)
            def body(v):
                S.wait(v, "te")
                S.inc(v.tensor_scalar_max(rowbank[0:1, 0:T], ps_e[0:1, 0:T], 1e-24), "dve")
                v.drain()
            blk.vector(body)
            def body(a):
                S.wait(a, "dve")
                S.inc(a.activation(rowbank[0:1, 0:T], rowbank[0:1, 0:T], ACT.Ln), "act")
                a.drain()
                S.inc(a.activation(rowbank[0:1, 0:T], rowbank[0:1, 0:T], ACT.Exp,
                                   scale=-0.5, bias=consts[0:1, 1:2]), "act")
            blk.scalar(body)
            for grp in range(4):
                def body(t, grp=grp):
                    S.wait(t, "dve")
                    S.wait(t, "act")
                    for q4 in range(4):
                        tt = grp * 4 + q4
                        S.inc(t.transpose(ps_q[q4][:, 0:1],
                                          rowbank[0:1, tt * 128:(tt + 1) * 128],
                                          identf[0:1, 0:1]), "te")
                blk.tensor(body)
                def body(v, grp=grp):
                    S.wait(v, "te")
                    for q4 in range(4):
                        tt = grp * 4 + q4
                        S.inc(v.tensor_copy(rfacc[:, tt:tt + 1], ps_q[q4][:, 0:1]), "dve")
                    v.drain()
                blk.vector(body)
            def body(v):
                S.inc(v.tensor_scalar_mul(rfac8[:, :], rfacc[:, :], 0.125), "dve")
                S.inc(v.tensor_scalar_mul(bb[:, :], s2d[:, :], -1.0), "dve")
                v.drain()
            blk.vector(body)
            def body(t):
                S.wait(t, "dve")
                S.inc(t.matmul(ps_a[:, 0:1], onesf[0:1, :], tiny[0:1, 0:1], start=True, stop=True), "te")
            blk.tensor(body)
            def body(v):
                S.wait(v, "te")
                S.inc(v.tensor_copy(tcol[:, 0:1], ps_a[:, 0:1]), "dve")
                v.drain()
            blk.vector(body)

        with nc.Block() as blk:
            for it in range(50):
                def body(v, it=it):
                    if it > 0:
                        S.wait(v, "te")
                    S.inc(v.tensor_add(XX[:, :], s2d[:, :], bb[:, :]), "dve")
                    v.drain()
                    S.inc(v.reduce_max(st2[:, 2:3], XX[:, :], axis=AX.X), "dve")
                    v.drain()
                blk.vector(body)
                def body(t):
                    S.wait(t, "dve")
                    S.inc(t.transpose(ps_c[0:1, 0:128], st2[:, 2:3], identf[:, :]), "te")
                blk.tensor(body)
                def body(v):
                    S.wait(v, "te")
                    S.inc(v.reduce_max(tiny[0:1, 2:3], ps_c[0:1, 0:128], axis=AX.X, negate=True), "dve")
                    v.drain()
                blk.vector(body)
                def body(t):
                    S.wait(t, "dve")
                    S.inc(t.matmul(ps_d[:, 0:1], onesf[0:1, :], tiny[0:1, 2:3], start=True, stop=True), "te")
                blk.tensor(body)
                def body(v):
                    S.wait(v, "te")
                    S.inc(v.tensor_copy(tcol[:, 0:1], ps_d[:, 0:1]), "dve")
                    v.drain()
                blk.vector(body)
                def body(a):
                    S.wait(a, "dve")
                    S.inc(a.activation(etile[:, :], XX[:, :], ACT.Exp, bias=tcol[:, 0:1],
                                       accum_out=st2[:, 0:1]), "act")
                blk.scalar(body)
                def body(t):
                    S.wait(t, "act")
                    S.inc(t.matmul(ps_b[0:1, 0:1], onescol_f[:, 0:1], st2[:, 0:1],
                                   start=True, stop=True), "te")
                blk.tensor(body)
                def body(a):
                    S.wait(a, "te")
                    S.inc(a.activation(tiny[0:1, 1:2], ps_b[0:1, 0:1], ACT.Ln), "act")
                blk.scalar(body)
                def body(v):
                    S.wait(v, "act")
                    S.inc(v.tensor_sub(tiny[0:1, 3:4], tiny[0:1, 2:3], tiny[0:1, 1:2]), "dve")
                    v.drain()
                    S.inc(v.tensor_scalar_add(tiny[0:1, 3:4], tiny[0:1, 3:4], LOGK), "dve")
                    v.drain()
                blk.vector(body)
                def body(t):
                    S.wait(t, "dve")
                    S.inc(t.matmul(ps_a[:, 0:1], onesf[0:1, :], tiny[0:1, 3:4], start=True, stop=True), "te")
                blk.tensor(body)
                def body(v):
                    S.wait(v, "te")
                    S.inc(v.tensor_copy(tcol[:, 1:2], ps_a[:, 0:1]), "dve")
                    v.drain()
                    S.inc(v.tensor_scalar_add(XX[:, :], s2d[:, :], tcol[:, 1:2]), "dve")
                    v.drain()
                    S.inc(v.tensor_scalar(bb[:, :], XX[:, :], -1.0, 0.0, ALU.mult, ALU.min), "dve")
                    v.drain()
                blk.vector(body)
            def body(v):
                S.inc(v.tensor_scalar_min(score[:, :], XX[:, :], 0.0), "dve")
                v.drain()
            blk.vector(body)
            def body(a):
                S.wait(a, "dve")
                S.inc(a.activation(score[:, :], score[:, :], ACT.Exp), "act")
            blk.scalar(body)

        with nc.Block() as blk:
            def body(v):
                S.wait(v, "act")
                S.inc(v.memset(tiny[0:1, 4:5], 0.0), "dve")
                S.inc(v.memset(tiny[0:1, 5:6], 1.5), "dve")
                v.drain()
            blk.vector(body)
            for it in range(40):
                def body(v):
                    S.wait(v, "te")
                    S.inc(v.tensor_add(tiny[0:1, 6:7], tiny[0:1, 4:5], tiny[0:1, 5:6]), "dve")
                    v.drain()
                    S.inc(v.tensor_scalar_mul(tiny[0:1, 6:7], tiny[0:1, 6:7], 0.5), "dve")
                    v.drain()
                blk.vector(body)
                def body(t):
                    S.wait(t, "dve")
                    S.inc(t.matmul(ps_a[:, 0:1], onesf[0:1, :], tiny[0:1, 6:7], start=True, stop=True), "te")
                blk.tensor(body)
                def body(v):
                    S.wait(v, "te")
                    S.inc(v.tensor_copy(tcol[:, 2:3], ps_a[:, 0:1]), "dve")
                    v.drain()
                    S.inc(v.tensor_scalar(etile[:, :], score[:, :], tcol[:, 2:3], None,
                                          ALU.is_gt), "dve")
                    v.drain()
                    S.inc(v.reduce_sum(st2[:, 1:2], etile[:, :], axis=AX.X), "dve")
                    v.drain()
                blk.vector(body)
                def body(t):
                    S.wait(t, "dve")
                    S.inc(t.matmul(ps_b[0:1, 0:1], onescol_f[:, 0:1], st2[:, 1:2],
                                   start=True, stop=True), "te")
                blk.tensor(body)
                def body(v):
                    S.wait(v, "te")
                    S.inc(v.tensor_scalar(tiny[0:1, 7:8], ps_b[0:1, 0:1], 1024.0, None, ALU.is_ge), "dve")
                    v.drain()
                    S.inc(v.tensor_sub(st2[0:1, 2:3], tiny[0:1, 6:7], tiny[0:1, 4:5]), "dve")
                    S.inc(v.tensor_sub(st2[0:1, 3:4], tiny[0:1, 6:7], tiny[0:1, 5:6]), "dve")
                    v.drain()
                    S.inc(v.scalar_tensor_tensor(tiny[0:1, 4:5], st2[0:1, 2:3], tiny[0:1, 7:8],
                                                 tiny[0:1, 4:5], ALU.mult, ALU.add), "dve")
                    v.drain()
                    S.inc(v.tensor_scalar(st2[0:1, 4:5], tiny[0:1, 7:8], -1.0, 1.0, ALU.mult, ALU.add), "dve")
                    v.drain()
                    S.inc(v.scalar_tensor_tensor(tiny[0:1, 5:6], st2[0:1, 3:4], st2[0:1, 4:5],
                                                 tiny[0:1, 5:6], ALU.mult, ALU.add), "dve")
                    v.drain()
                blk.vector(body)
            def body(t):
                S.wait(t, "dve")
                S.inc(t.matmul(ps_a[:, 0:1], onesf[0:1, :], tiny[0:1, 5:6], start=True, stop=True), "te")
                S.inc(t.matmul(ps_b[:, 0:1], onesf[0:1, :], tiny[0:1, 4:5], start=True, stop=True), "te")
            blk.tensor(body)
            def body(v):
                S.wait(v, "te")
                S.inc(v.tensor_copy(tcol[:, 3:4], ps_a[:, 0:1]), "dve")
                S.inc(v.tensor_copy(tcol[:, 4:5], ps_b[:, 0:1]), "dve")
                v.drain()
                S.inc(v.tensor_scalar(mask2d[:, :], score[:, :], tcol[:, 3:4], None, ALU.is_gt), "dve")
                v.drain()
                S.inc(v.reduce_sum(st2[:, 5:6], mask2d[:, :], axis=AX.X), "dve")
                v.drain()
                S.inc(v.tensor_scalar(etile[:, :], score[:, :], tcol[:, 4:5], None, ALU.is_gt), "dve")
                S.inc(v.tensor_scalar(XX[:, :], score[:, :], tcol[:, 3:4], None, ALU.is_le), "dve")
                v.drain()
                S.inc(v.tensor_mul(eqm[:, :], etile[:, :], XX[:, :]), "dve")
                v.drain()
            blk.vector(body)
            def body(t):
                S.wait(t, "dve")
                S.inc(t.matmul(ps_c[0:1, 0:1], onescol_f[:, 0:1], st2[:, 5:6], start=True, stop=True), "te")
                S.inc(t.matmul(ps_d[:, 0:16], trilT_sb[:, 0:128], eqm[:, :], start=True, stop=False), "te")
                S.inc(t.matmul(ps_e[0:1, 0:16], onescol_b[:, 0:1], eqm[:, :], start=True, stop=True), "te")
            blk.tensor(body)
            def body(v):
                S.wait(v, "te")
                S.inc(v.tensor_scalar(tiny[0:1, 1:2], ps_c[0:1, 0:1], -1.0, 1024.0, ALU.mult, ALU.add), "dve")
                v.drain()
                S.inc(v.tensor_tensor_scan(etile[0:1, 0:16], ps_e[0:1, 0:16], zzrow[0:1, 0:16],
                                           0.0, ALU.add, ALU.add), "dve")
                v.drain()
                S.inc(v.tensor_sub(etile[0:1, 0:16], etile[0:1, 0:16], ps_e[0:1, 0:16]), "dve")
                v.drain()
            blk.vector(body)
            def body(t):
                S.wait(t, "dve")
                S.inc(t.matmul(ps_a[:, 0:1], onesf[0:1, :], tiny[0:1, 1:2], start=True, stop=True), "te")
                S.inc(t.matmul(ps_d[:, 0:16], onesf[0:1, :], etile[0:1, 0:16], start=False, stop=True), "te")
            blk.tensor(body)
            def body(v):
                S.wait(v, "te")
                S.inc(v.tensor_copy(tcol[:, 4:5], ps_a[:, 0:1]), "dve")
                v.drain()
                S.inc(v.tensor_scalar(XX[:, :], ps_d[:, 0:16], tcol[:, 4:5], None, ALU.is_le), "dve")
                v.drain()
                S.inc(v.tensor_mul(XX[:, :], XX[:, :], eqm[:, :]), "dve")
                v.drain()
                S.inc(v.tensor_add(mask2d[:, :], mask2d[:, :], XX[:, :]), "dve")
                v.drain()
            blk.vector(body)
            if "dbg_mask" in dram:
                def body(g):
                    S.wait(g, "dve")
                    S.inc(g.dma_start(out=dram["dbg_mask"][:, :], in_=mask2d[:, :]), "gdma", 16)
                    S.inc(g.dma_start(out=dram["dbg_score"][:, :], in_=score[:, :]), "gdma", 16)
                    S.inc(g.dma_start(out=dram["dbg_s2d"][:, :], in_=s2d[:, :]), "gdma", 16)
                    S.inc(g.dma_start(out=dram["dbg_tiny"][:, :], in_=tiny[:, :]), "gdma", 16)
                    S.wait(g, "gdma")
                blk.gpsimd(body)

        with ExitStack() as WL:
            wql = WL.enter_context(nc.sbuf_tensor("f_wql", [128, NM * 512], BF16))
            wkvl = WL.enter_context(nc.sbuf_tensor("f_wkvl", [128, NM * 1024], BF16))
            wol = WL.enter_context(nc.sbuf_tensor("f_wol", [128, 4 * M], BF16))
            with nc.Block() as blk:
                def body(g):
                    S.inc(g.dma_start(out=wql.rearrange("p (c h) -> p c h", h=512),
                                      in_=dram["wq_l"].rearrange("(c p) h -> p c h", p=128)), "gdma", 16)
                    S.inc(g.dma_start(out=wkvl.rearrange("p (c h) -> p c h", h=1024),
                                      in_=dram["wkv_l"].rearrange("(c p) h -> p c h", p=128)), "gdma", 16)
                    S.inc(g.dma_start(out=wol.rearrange("p (c h) -> p c h", h=1024),
                                      in_=dram["wo_l"].rearrange("(c p) h -> p c h", p=128)), "gdma", 16)
                    S.inc(g.dma_start(out=gcol.rearrange("p (c o) -> p c o", o=1),
                                      in_=dram["gam_l"].rearrange("(c p) o -> p c o", p=128),
                                      allow_slow_non_contiguous=True), "gdma", 16)
                    wtgt["wl"] = S.c["gdma"]
                blk.gpsimd(body)
                def body(v):
                    S.wait(v, "gdma", wtgt["wl"])
                    for mc in range(NM):
                        S.inc(v.tensor_scalar_mul(wql[:, mc * 512:(mc + 1) * 512],
                                                  wql[:, mc * 512:(mc + 1) * 512], gcol[:, mc:mc + 1]), "dve")
                        S.inc(v.tensor_scalar_mul(wkvl[:, mc * M:(mc + 1) * M],
                                                  wkvl[:, mc * M:(mc + 1) * M], gcol[:, mc:mc + 1]), "dve")
                    v.drain()
                blk.vector(body)
                for tt in range(NT):
                    def body(t, tt=tt):
                        S.wait(t, "dve")
                        S.wait(t, "act")
                        for mc in range(NM):
                            ins = t.matmul(ps_a[:, 0:512],
                                           xn_fm[:, mc * T + tt * 128:mc * T + (tt + 1) * 128],
                                           wql[:, mc * 512:(mc + 1) * 512],
                                           start=(mc == 0), stop=(mc == NM - 1))
                        S.inc(ins, "te")
                        for half in range(2):
                            for mc in range(NM):
                                ins = t.matmul((ps_b, ps_c)[half][:, 0:512],
                                               xn_fm[:, mc * T + tt * 128:mc * T + (tt + 1) * 128],
                                               wkvl[:, mc * M + half * 512:mc * M + (half + 1) * 512],
                                               start=(mc == 0), stop=(mc == NM - 1))
                            S.inc(ins, "te")
                    blk.tensor(body)
                    def body(a, tt=tt):
                        S.wait(a, "te")
                        S.wait(a, "dma")
                        S.inc(a.activation(q_tok[:, :], ps_a[:, 0:512], ACT.Copy,
                                           scale=rfac8[:, tt:tt + 1]), "act")
                        S.inc(a.activation(k_tok[:, :], ps_b[:, 0:512], ACT.Copy,
                                           scale=rfacc[:, tt:tt + 1]), "act")
                        S.inc(a.activation(v_tok[:, :], ps_c[:, 0:512], ACT.Copy,
                                           scale=rfacc[:, tt:tt + 1]), "act")
                    blk.scalar(body)
                    def body(q, tt=tt):
                        S.wait(q, "act")
                        S.wait(q, "te")
                        for db in range(4):
                            S.inc(q.dma_start_transpose(qfw[:, db * 128:(db + 1) * 128],
                                                        q_tok[:, db * 128:(db + 1) * 128]), "dma", 16)
                            S.inc(q.dma_start_transpose(kfw[:, db * 128:(db + 1) * 128],
                                                        k_tok[:, db * 128:(db + 1) * 128]), "dma", 16)
                    blk.sync(body)
                    for h in range(LH):
                        po = (h % 2) * 64
                        sg = h // 2
                        def body(t, h=h, po=po, sg=sg):
                            if h == 0:
                                S.wait(t, "dma")
                            S.wait(t, "dve")
                            S.inc(t.matmul(ps_d[:, 0:128],
                                           qfw[po:po + 64, sg * 128:(sg + 1) * 128],
                                           kfw[po:po + 64, sg * 128:(sg + 1) * 128],
                                           start=True, stop=True), "te")
                        blk.tensor(body)
                        def body(v, h=h):
                            S.wait(v, "te")
                            S.inc(v.reduce_max(st2[:, 0:1], ps_d[:, 0:128], axis=AX.X, negate=True), "dve")
                            v.drain()
                        blk.vector(body)
                        def body(a, h=h):
                            S.wait(a, "dve")
                            S.inc(a.activation(p_sb[:, :], ps_d[:, 0:128], ACT.Exp,
                                               bias=st2[:, 0:1], accum_out=st2[:, 1:2]), "act")
                        blk.scalar(body)
                        def body(v, h=h):
                            S.wait(v, "act")
                            S.inc(v.reciprocal(st2[:, 2:3], st2[:, 1:2]), "dve")
                            v.drain()
                        blk.vector(body)
                        def body(t, h=h):
                            S.wait(t, "dve")
                            S.inc(t.transpose(ps_a[:, 0:128], p_sb[:, :], identf[:, :]), "te")
                        blk.tensor(body)
                        def body(a, h=h):
                            S.wait(a, "te")
                            S.inc(a.activation(pT_sb[:, :], ps_a[:, 0:128], ACT.Copy), "act")
                        blk.scalar(body)
                        def body(t, h=h):
                            S.wait(t, "act")
                            S.inc(t.matmul(ps_b[:, 0:64], pT_sb[:, :],
                                           v_tok[:, h * 64:(h + 1) * 64], start=True, stop=True), "te")
                        blk.tensor(body)
                        def body(v, h=h):
                            S.wait(v, "te")
                            S.inc(v.tensor_scalar_mul(ogl[:, h * 64:(h + 1) * 64],
                                                      ps_b[:, 0:64], st2[:, 2:3]), "dve")
                            v.drain()
                        blk.vector(body)
                    def body(q, tt=tt):
                        S.wait(q, "dve")
                        for db in range(4):
                            S.inc(q.dma_start_transpose(ofw[:, db * 128:(db + 1) * 128],
                                                        ogl[:, db * 128:(db + 1) * 128]), "dma", 16)
                    blk.sync(body)
                    def body(t, tt=tt):
                        S.wait(t, "dma")
                        for mh in range(2):
                            for sg4 in range(4):
                                ins = t.matmul((ps_c, ps_d)[mh][:, 0:512],
                                               ofw[:, sg4 * 128:(sg4 + 1) * 128],
                                               wol[:, sg4 * M + mh * 512:sg4 * M + (mh + 1) * 512],
                                               start=(sg4 == 0), stop=(sg4 == 3))
                            S.inc(ins, "te")
                    blk.tensor(body)
                    def body(v, tt=tt):
                        S.wait(v, "te")
                        for mh in range(2):
                            S.inc(v.tensor_add(xres[:, tt * M + mh * 512:tt * M + (mh + 1) * 512],
                                               (ps_c, ps_d)[mh][:, 0:512],
                                               xres[:, tt * M + mh * 512:tt * M + (mh + 1) * 512]), "dve")
                        v.drain()
                    blk.vector(body)

        qfh = lsb("f_qfh", [128, 4 * T], BF16)
        kfh = lsb("f_kfh", [128, 4 * T], BF16)
        v_h = lsb("f_vh", [128, NT * 512], BF16)

        with ExitStack() as WH:
            wqh = WH.enter_context(nc.sbuf_tensor("f_wqh", [128, NM * 512], BF16))
            wkvh = WH.enter_context(nc.sbuf_tensor("f_wkvh", [128, NM * 1024], BF16))
            with nc.Block() as blk:
                def body(g):
                    S.inc(g.dma_start(out=wqh.rearrange("p (c h) -> p c h", h=512),
                                      in_=dram["wq_h"].rearrange("(c p) h -> p c h", p=128)), "gdma", 16)
                    S.inc(g.dma_start(out=wkvh.rearrange("p (c h) -> p c h", h=1024),
                                      in_=dram["wkv_h"].rearrange("(c p) h -> p c h", p=128)), "gdma", 16)
                    S.inc(g.dma_start(out=gcol.rearrange("p (c o) -> p c o", o=1),
                                      in_=dram["gam_h"].rearrange("(c p) o -> p c o", p=128),
                                      allow_slow_non_contiguous=True), "gdma", 16)
                    wtgt["wh"] = S.c["gdma"]
                blk.gpsimd(body)
                def body(v):
                    S.wait(v, "gdma", wtgt["wh"])
                    S.wait(v, "act")
                    S.wait(v, "te")
                    for mc in range(NM):
                        S.inc(v.tensor_scalar_mul(wqh[:, mc * 512:(mc + 1) * 512],
                                                  wqh[:, mc * 512:(mc + 1) * 512], gcol[:, mc:mc + 1]), "dve")
                        S.inc(v.tensor_scalar_mul(wkvh[:, mc * M:(mc + 1) * M],
                                                  wkvh[:, mc * M:(mc + 1) * M], gcol[:, mc:mc + 1]), "dve")
                    v.drain()
                blk.vector(body)
                for tt in range(NT):
                    def body(t, tt=tt):
                        S.wait(t, "dve")
                        S.wait(t, "act")
                        if tt > 0:
                            S.wait(t, "dma")
                        for mc in range(NM):
                            ins = t.matmul(ps_a[:, 0:512],
                                           xn_fm[:, mc * T + tt * 128:mc * T + (tt + 1) * 128],
                                           wqh[:, mc * 512:(mc + 1) * 512],
                                           start=(mc == 0), stop=(mc == NM - 1))
                        S.inc(ins, "te")
                        for half in range(2):
                            for mc in range(NM):
                                ins = t.matmul((ps_b, ps_c)[half][:, 0:512],
                                               xn_fm[:, mc * T + tt * 128:mc * T + (tt + 1) * 128],
                                               wkvh[:, mc * M + half * 512:mc * M + (half + 1) * 512],
                                               start=(mc == 0), stop=(mc == NM - 1))
                            S.inc(ins, "te")
                    blk.tensor(body)
                    def body(a, tt=tt):
                        S.wait(a, "te")
                        S.inc(a.activation(q_tok[:, :], ps_a[:, 0:512], ACT.Copy,
                                           scale=rfac8[:, tt:tt + 1]), "act")
                        S.inc(a.activation(k_tok[:, :], ps_b[:, 0:512], ACT.Copy,
                                           scale=rfacc[:, tt:tt + 1]), "act")
                        S.inc(a.activation(v_h[:, tt * 512:(tt + 1) * 512], ps_c[:, 0:512], ACT.Copy,
                                           scale=rfacc[:, tt:tt + 1]), "act")
                        a.drain()
                        S.inc(a.activation(sqscr[:, 0:512], q_tok[:, :], ACT.Square,
                                           accum_out=qn2[:, tt:tt + 1]), "act")
                        S.inc(a.activation(sqscr[:, 512:1024], k_tok[:, :], ACT.Square,
                                           accum_out=qn2[:, NT + tt:NT + tt + 1]), "act")
                    blk.scalar(body)
                    def body(q, tt=tt):
                        S.wait(q, "act")
                        for db in range(4):
                            S.inc(q.dma_start_transpose(
                                qfh[:, db * T + tt * 128:db * T + (tt + 1) * 128],
                                q_tok[:, db * 128:(db + 1) * 128]), "dma", 16)
                            S.inc(q.dma_start_transpose(
                                kfh[:, db * T + tt * 128:db * T + (tt + 1) * 128],
                                k_tok[:, db * 128:(db + 1) * 128]), "dma", 16)
                    blk.sync(body)

        pT_h = lsb("f_pTh", [128, T], BF16)
        o_sbh = lsb("f_osbh", [128, T], BF16)
        o_fmh = lsb("f_ofmh", [128, 4 * T], BF16)
        if True:
            with nc.Block() as blk:
                def body(v):
                    S.wait(v, "act")
                    S.inc(v.reduce_max(st2[:, 0:1], qn2[:, 0:NT], axis=AX.X), "dve")
                    S.inc(v.reduce_max(st2[:, 1:2], qn2[:, NT:2 * NT], axis=AX.X), "dve")
                    v.drain()
                blk.vector(body)
                def body(t):
                    S.wait(t, "dve")
                    S.inc(t.transpose(ps_a[0:1, 0:128], st2[:, 0:1], identf[:, :]), "te")
                    S.inc(t.transpose(ps_b[0:1, 0:128], st2[:, 1:2], identf[:, :]), "te")
                blk.tensor(body)
                def body(v):
                    S.wait(v, "te")
                    S.inc(v.reduce_max(tiny[0:1, 2:3], ps_a[0:1, 0:128], axis=AX.X), "dve")
                    S.inc(v.reduce_max(tiny[0:1, 3:4], ps_b[0:1, 0:128], axis=AX.X), "dve")
                    v.drain()
                blk.vector(body)
                def body(a):
                    S.wait(a, "dve")
                    S.inc(a.activation(tiny[0:1, 2:3], tiny[0:1, 2:3], ACT.Ln), "act")
                    a.drain()
                    S.inc(a.activation(tiny[0:1, 3:4], tiny[0:1, 3:4], ACT.Ln), "act")
                    a.drain()
                blk.scalar(body)
                def body(v):
                    S.wait(v, "act")
                    S.inc(v.tensor_add(tiny[0:1, 2:3], tiny[0:1, 2:3], tiny[0:1, 3:4]), "dve")
                    v.drain()
                blk.vector(body)
                def body(a):
                    S.wait(a, "dve")
                    S.inc(a.activation(tiny[0:1, 2:3], tiny[0:1, 2:3], ACT.Exp, scale=0.5,
                                       bias=consts[0:1, 3:4]), "act")
                blk.scalar(body)
                def body(v):
                    S.wait(v, "act")
                    S.inc(v.tensor_scalar_mul(tiny[0:1, 2:3], tiny[0:1, 2:3], -1.0), "dve")
                    v.drain()
                blk.vector(body)
                def body(t):
                    S.wait(t, "dve")
                    S.inc(t.matmul(ps_a[:, 0:1], onesf[0:1, :], tiny[0:1, 2:3], start=True, stop=True), "te")
                blk.tensor(body)
                def body(v):
                    S.wait(v, "te")
                    S.inc(v.tensor_copy(tcol[:, 3:4], ps_a[:, 0:1]), "dve")
                    v.drain()
                blk.vector(body)

            with nc.Block() as blk:
                for h in range(LH):
                    po = (h % 2) * 64
                    sg = h // 2
                    srow = 64 - po
                    for jt in range(NT):
                        def body(t, h=h, po=po, sg=sg, jt=jt):
                            S.wait(t, "act")
                            S.wait(t, "dve")
                            if h == 0 and jt == 0:
                                S.wait(t, "dma")
                            for ib in range(4):
                                S.inc(t.matmul(ps_e[:, ib * 512:(ib + 1) * 512],
                                               kfh[po:po + 64, sg * T + jt * 128:sg * T + (jt + 1) * 128],
                                               qfh[po:po + 64, sg * T + ib * 512:sg * T + (ib + 1) * 512],
                                               start=True, stop=True), "te")
                        blk.tensor(body)
                        def body(a, h=h, jt=jt):
                            S.wait(a, "te")
                            S.inc(a.activation(pT_h[:, :], ps_e[:, 0:2048], ACT.Exp,
                                               bias=tcol[:, 3:4]), "act")
                        blk.scalar(body)
                        def body(t, h=h, po=po, jt=jt, srow=srow):
                            S.wait(t, "act")
                            for ib in range(4):
                                S.inc(t.matmul(ps_q[ib][po:po + 64, 0:512],
                                               v_h[:, jt * 512 + h * 64:jt * 512 + (h + 1) * 64],
                                               pT_h[:, ib * 512:(ib + 1) * 512],
                                               start=(jt == 0), stop=(jt == NT - 1)), "te")
                                S.inc(t.matmul(ps_q[ib][srow:srow + 1, 0:512],
                                               onescol_b[:, 0:1],
                                               pT_h[:, ib * 512:(ib + 1) * 512],
                                               start=(jt == 0), stop=(jt == NT - 1)), "te")
                        blk.tensor(body)
                    def body(v, h=h, srow=srow):
                        S.wait(v, "te")
                        for ib in range(4):
                            S.inc(v.reciprocal(rowbank[srow:srow + 1, ib * 512:(ib + 1) * 512],
                                               ps_q[ib][srow:srow + 1, 0:512]), "dve")
                        v.drain()
                    blk.vector(body)
                    def body(t, h=h, po=po, srow=srow):
                        S.wait(t, "dve")
                        for ib in range(4):
                            S.inc(t.matmul(ps_e[po:po + 64, ib * 512:(ib + 1) * 512],
                                           onesall[srow:srow + 1, 0:64],
                                           rowbank[srow:srow + 1, ib * 512:(ib + 1) * 512],
                                           start=True, stop=True), "te")
                    blk.tensor(body)
                    def body(a, h=h, po=po):
                        S.wait(a, "te")
                        for ib in range(4):
                            S.inc(a.activation(o_sbh[po:po + 64, ib * 512:(ib + 1) * 512],
                                               ps_q[ib][po:po + 64, 0:512], ACT.Copy), "act")
                    blk.scalar(body)
                    def body(v, h=h, po=po, sg=sg):
                        S.wait(v, "act")
                        S.inc(v.tensor_mul(o_fmh[po:po + 64, sg * T:(sg + 1) * T],
                                           o_sbh[po:po + 64, :],
                                           ps_e[po:po + 64, 0:2048]), "dve")
                        v.drain()
                    blk.vector(body)

            with ExitStack() as WO:
                woh = WO.enter_context(nc.sbuf_tensor("f_woh", [128, 4 * M], BF16))
                with nc.Block() as blk:
                    def body(g):
                        S.inc(g.dma_start(out=woh.rearrange("p (c h) -> p c h", h=1024),
                                          in_=dram["wo_h"].rearrange("(c p) h -> p c h", p=128)), "gdma", 16)
                        wtgt["woh"] = S.c["gdma"]
                    blk.gpsimd(body)
                    for tt in range(NT):
                        def body(t, tt=tt):
                            S.wait(t, "gdma", wtgt["woh"])
                            if tt == 0:
                                S.wait(t, "dve")
                            else:
                                S.wait(t, "dve")
                            for mh in range(2):
                                for sg4 in range(4):
                                    ins = t.matmul((ps_a, ps_b)[mh][:, 0:512],
                                                   o_fmh[:, sg4 * T + tt * 128:sg4 * T + (tt + 1) * 128],
                                                   woh[:, sg4 * M + mh * 512:sg4 * M + (mh + 1) * 512],
                                                   start=(sg4 == 0), stop=(sg4 == 3))
                                S.inc(ins, "te")
                        blk.tensor(body)
                        def body(v, tt=tt):
                            S.wait(v, "te")
                            for mh in range(2):
                                S.inc(v.scalar_tensor_tensor(
                                    xres[:, tt * M + mh * 512:tt * M + (mh + 1) * 512],
                                    (ps_a, ps_b)[mh][:, 0:512],
                                    mask2d[:, tt:tt + 1],
                                    xres[:, tt * M + mh * 512:tt * M + (mh + 1) * 512],
                                    ALU.mult, ALU.add), "dve")
                            v.drain()
                        blk.vector(body)

        L.close()
        PS.close()

    if not globals().get('SKIP_L1FFN', False):
        l1ffn()

    with nc.Block() as blk:
        def body(q):
            S.wait(q, "dve")
            S.inc(q.dma_start(out=dram["out"].rearrange("(t p) m -> p t m", p=128), in_=xres.rearrange("p (t m) -> p t m", t=NT)), "dma", 16)
            S.wait(q, "dma")
        blk.sync(body)

    stack.close()
    return nc, dram


def _reorder_kvh(wkv):
    wt = wkv.T
    idx_k = np.concatenate([np.arange(h * 128, h * 128 + 64) for h in range(8)])
    idx_v = np.concatenate([np.arange(h * 128 + 64, h * 128 + 128) for h in range(8)])
    return np.ascontiguousarray(wt[:, np.concatenate([idx_k, idx_v])])


def host_inputs(inputs, core):
    x = np.ascontiguousarray(inputs["L0_attn_x"][core])
    d = {
        "x": x,
        "Wq": inputs["L0_attn_Wq"], "Wk": inputs["L0_attn_Wk"], "Wv": inputs["L0_attn_Wv"],
        "Wg": inputs["L0_attn_Wg"], "Wo": inputs["L0_attn_Wo"], "Wgt": inputs["L0_attn_Wgt"],
        "gate_w": inputs["L0_ffn_gate_w"], "w1": inputs["L0_ffn_w1"],
        "w2": inputs["L0_ffn_w2"], "w3": inputs["L0_ffn_w3"],
        "W_inT": np.ascontiguousarray(np.asarray(inputs["L1_attn_W_in"]).T),
        "conv_w": np.asarray(inputs["L1_attn_conv_w"]).reshape(2048, 4),
        "conv_b": np.asarray(inputs["L1_attn_conv_b"]).reshape(2048, 1),
        "W_xT": np.ascontiguousarray(np.asarray(inputs["L1_attn_W_x"]).T),
        "W_dt": np.asarray(inputs["L1_attn_W_dt"]).reshape(2048, 1),
        "b_dt": np.asarray(inputs["L1_attn_b_dt"]).reshape(2048, 1),
        "A_log": inputs["L1_attn_A_log"],
        "D_param": np.asarray(inputs["L1_attn_D_param"]).reshape(2048, 1),
        "W_outT": np.ascontiguousarray(np.asarray(inputs["L1_attn_W_out"]).T),
        "gam_l": np.asarray(inputs["L1_ffn_gamma_light"]).reshape(M, 1),
        "wq_l": inputs["L1_ffn_wq_light"],
        "wkv_l": inputs["L1_ffn_wkv_light"],
        "wo_l": inputs["L1_ffn_wo_light"],
        "q_rt": np.asarray(inputs["L1_ffn_q_routing_token"]).reshape(M, 1),
        "q_rt_row": np.asarray(inputs["L1_ffn_q_routing_token"]).reshape(1, M),
        "gam_h": np.asarray(inputs["L1_ffn_gamma_heavy"]).reshape(M, 1),
        "wq_h": inputs["L1_ffn_wq_heavy"],
        "wkv_h": _reorder_kvh(np.asarray(inputs["L1_ffn_wkv_heavy"])),
        "wo_h": inputs["L1_ffn_wo_heavy"],
        "onescol_f": np.ones((128, 1), dtype=np.float32),
        "ident": np.eye(128, dtype=np.float32),
        "selh": np.kron(np.eye(16, dtype=np.float32), np.ones((1, 128), np.float32)).reshape(16, 16 * 128),
        "ones": np.ones((1, 128), dtype=np.float32),
        "onescol": np.ones((128, 1), dtype=np.float32),
        "trilT": np.triu(np.ones((CH, CH), dtype=np.float32)),
    }
    return {k: np.ascontiguousarray(np.asarray(v, dtype=np.float32)) for k, v in d.items()}


def kernel(**inputs):
    nc, _ = build(debug=False)
    in_maps = [host_inputs(inputs, core) for core in range(8)]
    res = run_bass_kernel_spmd(nc, in_maps, core_ids=list(range(8)))
    out = np.stack([r["out"] for r in res.results], axis=0)
    return out.astype(np.float32)
